# version 02
# baseline (speedup 1.0000x reference)
import numpy as np
import ml_dtypes
from contextlib import ExitStack

import concourse.bass as bass
import concourse.mybir as mybir
from concourse.bass_utils import run_bass_kernel_spmd

F32 = mybir.dt.float32
BF16 = mybir.dt.bfloat16
AF = mybir.ActivationFunctionType
ALU = mybir.AluOpType

H = 8
HD = 128
D = 1024
FFND = 4096
S = 2048
BLOC = 2
T = BLOC * S
TG = 512
NG = T // TG
EPS = 1e-5
NCORES = 8

BF16NP = ml_dtypes.bfloat16



def _perm_deint():
    return np.concatenate([np.arange(0, HD, 2), np.arange(1, HD, 2)])


def _xpos_tables():
    SCALE_BASE = 512
    scale_vec = (np.arange(0, HD, 2) + 0.4 * HD) / (1.4 * HD)
    power = (np.arange(S, dtype=np.float64) / SCALE_BASE)[:, None]
    scale = scale_vec[None, :] ** power
    half = HD // 2
    inv_freq = 1.0 / 10000.0 ** (np.arange(half, dtype=np.float64) / half)
    ang = np.arange(S, dtype=np.float64)[:, None] * inv_freq[None, :]
    sin, cos = np.sin(ang), np.cos(ang)

    def dupT(m):
        return np.concatenate([m.T, m.T], axis=0)

    cosq = dupT(cos * scale)
    sinq = dupT(sin * scale)
    cosk = dupT(cos / scale)
    sink = dupT(sin / scale)
    sinq[0:64] *= -1.0
    sink[0:64] *= -1.0
    return (cosq.astype(np.float32), sinq.astype(np.float32),
            cosk.astype(np.float32), sink.astype(np.float32))


def _decay_consts():
    gam = 1.0 - np.exp(np.linspace(np.log(1.0 / 32), np.log(1.0 / 512), H))
    j = np.arange(TG, dtype=np.float64)
    gdec = np.broadcast_to(gam[:, None, None] ** j[None, None, :],
                           (H, 128, TG)).astype(np.float32)
    p = np.arange(128, dtype=np.float64)
    mi = np.arange(16, dtype=np.float64)
    expo = 128.0 * (mi[None, None, :] - 3.0) - p[:, None, None]
    dvec = (gam[None, :, None] ** expo).astype(np.float32)
    dd = np.arange(4)[:, None, None]
    pp = np.arange(128)[None, :, None]
    jj = np.arange(TG)[None, None, :]
    masks = (jj >= 128 * dd + pp).astype(BF16NP)
    return gdec, dvec, masks


def _lhsT_slabs(w):
    I, O = w.shape
    return np.ascontiguousarray(
        w.reshape(I // 128, 128, O // 128, 128).transpose(2, 1, 0, 3))


def _pieces(slabs, k):
    NO, _, NI, _ = slabs.shape
    return np.ascontiguousarray(
        slabs.reshape(NO // k, k, 128, NI, 128).transpose(0, 2, 1, 3, 4))


def _vec_pc(v):
    n = v.shape[0] // 128
    return np.ascontiguousarray(v.reshape(n, 128).T)


def prep_core_inputs(inputs, core):
    f = lambda k: np.asarray(inputs[k], dtype=np.float32)
    X = f("L0_attn_X")
    xc = X[2 * core:2 * core + 2]
    xin = np.ascontiguousarray(
        xc.reshape(BLOC, S, H, 128).transpose(2, 3, 0, 1).reshape(H, 128, T))

    perm = _perm_deint()
    m = {"xin": xin}
    for i in range(2):
        WQ = f("L0_attn_WQ")[i][:, :, perm]
        WK = f("L0_attn_WK")[i][:, :, perm]
        WV = f("L0_attn_WV")[i]
        m[f"wq{i}"] = _lhsT_slabs(WQ.transpose(1, 0, 2).reshape(D, H * HD))
        m[f"wk{i}"] = _lhsT_slabs(WK.transpose(1, 0, 2).reshape(D, H * HD))
        m[f"wv{i}"] = np.ascontiguousarray(
            WV.transpose(1, 0, 2).reshape(D, H * HD).reshape(8, 128, 1024))
        m[f"wg{i}"] = _pieces(_lhsT_slabs(f("L0_attn_WG")[i]), 4)
        m[f"wo{i}"] = _pieces(_lhsT_slabs(f("L0_attn_WO")[i]), 4)
        m[f"fw1{i}"] = _pieces(_lhsT_slabs(f("L0_attn_ffn_w1")[i]), 4)
        m[f"fw2{i}"] = _lhsT_slabs(f("L0_attn_ffn_w2")[i])
        m[f"ln1w{i}"] = _vec_pc(f("L0_attn_ln1_w")[i])
        m[f"ln1b{i}"] = _vec_pc(f("L0_attn_ln1_b")[i])
        m[f"ln2w{i}"] = _vec_pc(f("L0_attn_ln2_w")[i])
        m[f"ln2b{i}"] = _vec_pc(f("L0_attn_ln2_b")[i])
        m[f"gnw{i}"] = _vec_pc(f("L0_attn_gn_w")[i])
        m[f"gnb{i}"] = _vec_pc(f("L0_attn_gn_b")[i])
        m[f"fb1{i}"] = _vec_pc(f("L0_attn_ffn_b1")[i])
        m[f"fb2{i}"] = _vec_pc(f("L0_attn_ffn_b2")[i])

    m["m1"] = _pieces(_lhsT_slabs(np.ascontiguousarray(f("L0_ffn_w1_w").T)), 2)
    m["m2"] = _pieces(_lhsT_slabs(np.ascontiguousarray(f("L0_ffn_w2_w").T)), 2)
    m["mh1"] = _lhsT_slabs(np.ascontiguousarray(f("L0_ffn_h1_w").T))
    m["mh2"] = _lhsT_slabs(np.ascontiguousarray(f("L0_ffn_h2_w").T))
    m["m3"] = _lhsT_slabs(np.ascontiguousarray(f("L0_ffn_w3_w").T))
    m["ms1"] = _vec_pc(f("L0_ffn_w1_s"))
    m["ms2"] = _vec_pc(f("L0_ffn_w2_s"))
    m["msh1"] = _vec_pc(f("L0_ffn_h1_s"))
    m["msh2"] = _vec_pc(f("L0_ffn_h2_s"))
    m["ms3"] = _vec_pc(f("L0_ffn_w3_s"))

    m["cq"], m["sq_"], m["ck"], m["sk_"] = _xpos_tables()
    m["gdec"], m["dvec"], m["masks"] = _decay_consts()
    return m



class B:
    def __init__(self, nc, block, stack):
        self.nc = nc
        self.block = block
        self.stack = stack
        self.sems = {}
        self.cnt = {}

    def mksem(self, *names):
        for n in names:
            self.sems[n] = self.stack.enter_context(self.nc.semaphore(n))
            self.cnt[n] = 0

    def inc(self, inst, name, by=1):
        inst.then_inc(self.sems[name], by)
        self.cnt[name] += by
        return self.cnt[name]

    def v(self, name):
        return self.cnt[name]

    def sem(self, name):
        return self.sems[name]

    def w(self, eng, name, val):
        if val and val > 0:
            eng.wait_ge(self.sems[name], val)

    def barrier(self, engines=("sync", "tensor", "vector", "scalar")):
        tgt = self.cnt["bar"] + len(engines)
        for e in engines:
            @getattr(self.block, e)
            def _(eng):
                eng.drain()
                eng.sem_inc(self.sems["bar"], 1)
        self.cnt["bar"] = tgt
        for e in engines:
            @getattr(self.block, e)
            def _(eng):
                eng.wait_ge(self.sems["bar"], tgt)


def pipe(b, n, nbanks, emit_group, emit_evac, evac_eng="scalar",
         evac_extra_wait=None):
    cn = {"scalar": "ac", "vector": "ev"}[evac_eng]
    mm_base = b.v("mm")
    ev_base = b.v(cn)

    @b.block.tensor
    def _(tensor):
        for gi in range(n):
            if gi >= nbanks:
                tensor.wait_ge(b.sem(cn), ev_base + gi - nbanks + 1)
            i = emit_group(tensor, gi)
            b.inc(i, "mm")

    @getattr(b.block, evac_eng)
    def _(eng):
        for gi in range(n):
            eng.wait_ge(b.sem("mm"), mm_base + gi + 1)
            if evac_extra_wait:
                evac_extra_wait(eng, gi)
            i = emit_evac(eng, gi)
            b.inc(i, cn)

    return b.v("mm"), b.v(cn)


def ln_smalls(b, sm, pS1, pS2, denom, mm_stats):
    @b.block.vector
    def _(vector):
        vector.wait_ge(b.sem("mm"), mm_stats)
        vector.tensor_scalar_mul(sm["mu"][:], pS1[0:1, :], 1.0 / denom)
        vector.tensor_scalar_mul(sm["aux"][:], pS2[0:1, :], 1.0 / denom)
        vector.tensor_mul(sm["m2"][:], sm["mu"][:], sm["mu"][:])
        i = vector.tensor_sub(sm["var"][:], sm["aux"][:], sm["m2"][:])
        b.inc(i, "ev")
    ev_small = b.v("ev")

    @b.block.scalar
    def _(scalar):
        scalar.wait_ge(b.sem("ev"), ev_small)
        scalar.activation(sm["aux"][:], sm["var"][:], AF.Ln, bias=sm["eps"][0:1, 0:1])
        i = scalar.activation(sm["rstd"][:], sm["aux"][:], AF.Exp,
                              scale=sm["nh"][0:1, 0:1])
        b.inc(i, "ac")
    ac_rstd = b.v("ac")

    @b.block.vector
    def _(vector):
        vector.wait_ge(b.sem("ac"), ac_rstd)
        i = vector.tensor_mul(sm["mr"][:], sm["mu"][:], sm["rstd"][:])
        b.inc(i, "ev")
    return ev_small


def ln_block(b, src_view, nchunk, sqbuf, pS1, pS2, pR, pMR, ones_col, ones_row,
             sm, denom, sq_on_vector=True, pre_stats_wait=None, pre_sq_wait=None):
    if sq_on_vector:
        @b.block.vector
        def _(vector):
            if pre_sq_wait:
                pre_sq_wait(vector)
            i = vector.tensor_mul(sqbuf[:], src_view[:], src_view[:])
            b.inc(i, "ev")
        ev_sq = b.v("ev")

        @b.block.tensor
        def _(tensor):
            if pre_stats_wait:
                pre_stats_wait(tensor)
            for c in range(nchunk):
                tensor.matmul(pS1[0:1, :], ones_col[:], src_view[:, c, :],
                              start=(c == 0), stop=(c == nchunk - 1))
            tensor.wait_ge(b.sem("ev"), ev_sq)
            for c in range(nchunk):
                i = tensor.matmul(pS2[0:1, :], ones_col[:], sqbuf[:, c, :],
                                  start=(c == 0), stop=(c == nchunk - 1))
            b.inc(i, "mm")
    else:
        ac_base = b.v("ac")
        mm_base = b.v("mm")

        @b.block.scalar
        def _(scalar):
            if pre_sq_wait:
                pre_sq_wait(scalar)
            for c in range(nchunk):
                if c >= 2:
                    scalar.wait_ge(b.sem("mm"), mm_base + c - 1)
                i = scalar.activation(sqbuf[:, c % 2, :], src_view[:, c, :],
                                      AF.Square)
                b.inc(i, "ac")

        @b.block.tensor
        def _(tensor):
            if pre_stats_wait:
                pre_stats_wait(tensor)
            for c in range(nchunk):
                tensor.matmul(pS1[0:1, :], ones_col[:], src_view[:, c, :],
                              start=(c == 0), stop=(c == nchunk - 1))
            for c in range(nchunk):
                tensor.wait_ge(b.sem("ac"), ac_base + c + 1)
                i = tensor.matmul(pS2[0:1, :], ones_col[:], sqbuf[:, c % 2, :],
                                  start=(c == 0), stop=(c == nchunk - 1))
                b.inc(i, "mm")
    mm_stats = b.v("mm")

    ev_pfree = ln_smalls(b, sm, pS1, pS2, denom, mm_stats)
    ev_mr = b.v("ev")

    @b.block.tensor
    def _(tensor):
        tensor.wait_ge(b.sem("ev"), ev_mr)
        tensor.matmul(pR[:], ones_row[:], sm["rstd"][0:1, :], start=True, stop=True)
        i = tensor.matmul(pMR[:], ones_row[:], sm["mr"][0:1, :], start=True, stop=True)
        b.inc(i, "mm")
    return b.v("mm"), ev_pfree


def emit_conversions(b, jobs, cvb):
    nq = [0]

    @b.block.gpsimd
    def _(gpsimd):
        for src, dst, key, N, M in jobs:
            nsub = (M + 2047) // 2048
            for n in range(N):
                for s_ in range(nsub):
                    w = min(2048, M - 2048 * s_)
                    i = nq[0]
                    par = i % 2
                    if i >= 2:
                        gpsimd.wait_ge(b.sem("cv"), 16 * (i - 1))
                    ii = gpsimd.dma_start(out=cvb[:, par, 0:w],
                                          in_=src[n][:, 2048 * s_:2048 * s_ + w])
                    b.inc(ii, "cvl", 16)
                    gpsimd.wait_ge(b.sem("cvl"), 16 * (i + 1))
                    ii = gpsimd.dma_start(out=dst[n][:, 2048 * s_:2048 * s_ + w],
                                          in_=cvb[:, par, 0:w])
                    b.inc(ii, "cv", 16)
                    nq[0] += 1
            b.cv_marks[key] = b.v("cv")



def build_program(tap=None):
    nc = bass.Bass("TRN2", target_bir_lowering=False)
    stack = ExitStack()

    def din(name, shape, dt=F32):
        return nc.dram_tensor(name, list(shape), dt, kind="ExternalInput").ap()

    xin = din("xin", [H, 128, T])
    WQd = [din(f"wq{i}", [8, 128, 8, 128]) for i in range(2)]
    WKd = [din(f"wk{i}", [8, 128, 8, 128]) for i in range(2)]
    WVd = [din(f"wv{i}", [8, 128, 1024]) for i in range(2)]
    WGd = [din(f"wg{i}", [2, 128, 4, 8, 128]) for i in range(2)]
    WOd = [din(f"wo{i}", [2, 128, 4, 8, 128]) for i in range(2)]
    FW1 = [din(f"fw1{i}", [8, 128, 4, 8, 128]) for i in range(2)]
    FW2 = [din(f"fw2{i}", [8, 128, 32, 128]) for i in range(2)]
    V8 = lambda nm: [din(f"{nm}{i}", [128, 8]) for i in range(2)]
    LN1W, LN1B, LN2W, LN2B = V8("ln1w"), V8("ln1b"), V8("ln2w"), V8("ln2b")
    GNW, GNB, FB2 = V8("gnw"), V8("gnb"), V8("fb2")
    FB1 = [din(f"fb1{i}", [128, 32]) for i in range(2)]
    M1 = din("m1", [16, 128, 2, 8, 128])
    M2 = din("m2", [16, 128, 2, 8, 128])
    MH1 = din("mh1", [32, 128, 32, 128])
    MH2 = din("mh2", [32, 128, 32, 128])
    M3 = din("m3", [8, 128, 32, 128])
    MS1, MS2 = din("ms1", [128, 32]), din("ms2", [128, 32])
    MSH1, MSH2 = din("msh1", [128, 32]), din("msh2", [128, 32])
    MS3 = din("ms3", [128, 8])
    CQ, SQ_ = din("cq", [128, S]), din("sq_", [128, S])
    CK, SK_ = din("ck", [128, S]), din("sk_", [128, S])
    GDEC = din("gdec", [H, 128, TG])
    DVEC = din("dvec", [128, 8, 16])
    MASKS = din("masks", [4, 128, TG], BF16)

    xout = nc.dram_tensor("xout", [H, 128, T], F32, kind="ExternalOutput").ap()
    skind = "ExternalOutput" if tap else "Internal"
    CVshapes = {
        "fw1_0": [8, 128, 4, 8, 128], "fw2_0": [8, 128, 32, 128],
        "fw1_1": [8, 128, 4, 8, 128], "fw2_1": [8, 128, 32, 128],
        "m1": [16, 128, 2, 8, 128], "m2": [16, 128, 2, 8, 128],
        "mh1": [32, 128, 32, 128], "mh2": [32, 128, 32, 128],
        "m3": [8, 128, 32, 128],
    }
    CV = {k: nc.dram_tensor("cv_" + k, s, BF16, kind="Internal").ap()
          for k, s in CVshapes.items()}
    global HTAP
    HTAP = nc.dram_tensor("htap", [NG, 128, 32, TG], BF16, kind=skind).ap() \
        if tap else None
    resid = nc.dram_tensor("resid", [H, 128, T], F32, kind=skind).ap()
    xnb = nc.dram_tensor("xnb", [H, 128, T], BF16, kind=skind).ap()
    ybuf = nc.dram_tensor("ybuf", [H, 128, T], BF16, kind=skind).ap()

    sb = lambda name, shape, dt=F32: stack.enter_context(
        nc.sbuf_tensor(name, list(shape), dt))
    ones_col = sb("ones_col", [128, 1])
    ones_row = sb("ones_row", [1, 128])
    vln1w = [sb(f"vln1w{i}", [128, 8]) for i in range(2)]
    vln1b = [sb(f"vln1b{i}", [128, 8]) for i in range(2)]
    vln2w = [sb(f"vln2w{i}", [128, 8]) for i in range(2)]
    vln2b = [sb(f"vln2b{i}", [128, 8]) for i in range(2)]
    vgnw = [sb(f"vgnw{i}", [128, 8]) for i in range(2)]
    vgnb = [sb(f"vgnb{i}", [128, 8]) for i in range(2)]
    vfb1 = [sb(f"vfb1{i}", [128, 32]) for i in range(2)]
    vfb2 = [sb(f"vfb2{i}", [128, 8]) for i in range(2)]
    vms1, vms2 = sb("vms1", [128, 32]), sb("vms2", [128, 32])
    vmsh1, vmsh2 = sb("vmsh1", [128, 32]), sb("vmsh2", [128, 32])
    vms3 = sb("vms3", [128, 8])
    vdvec = sb("vdvec", [128, 8, 16])
    sm = {k: sb("sm_" + k, [1, TG]) for k in ("mu", "aux", "m2", "var", "rstd", "mr")}
    sm["eps"] = sb("sm_eps", [1, 1])
    sm["nh"] = sb("sm_nh", [1, 1])
    cvb = sb("cvb", [128, 2, 2048], BF16)

    with nc.Block() as block:
        b = B(nc, block, stack)
        b.mksem("cl", "xl", "wl", "gl", "mm", "ev", "ac", "st", "so", "bar",
                "cvl", "cv")

        @block.sync
        def _(sync):
            pairs = [(MS1, vms1), (MS2, vms2), (MSH1, vmsh1), (MSH2, vmsh2),
                     (MS3, vms3), (DVEC, vdvec)]
            for i in range(2):
                pairs += [(LN1W[i], vln1w[i]), (LN1B[i], vln1b[i]),
                          (LN2W[i], vln2w[i]), (LN2B[i], vln2b[i]),
                          (GNW[i], vgnw[i]), (GNB[i], vgnb[i]),
                          (FB1[i], vfb1[i]), (FB2[i], vfb2[i])]
            for src, dst in pairs:
                b.inc(sync.dma_start(out=dst[:], in_=src), "cl", 16)
        cl_consts = b.v("cl")

        @block.vector
        def _(vector):
            vector.memset(ones_col[:], 1.0)
            vector.memset(ones_row[:], 1.0)
            vector.memset(sm["eps"][:], EPS)
            i = vector.memset(sm["nh"][:], -0.5)
            b.inc(i, "ev")
        ev_ones = b.v("ev")

        for e in ("tensor", "scalar"):
            @getattr(block, e)
            def _(eng):
                eng.wait_ge(b.sem("ev"), ev_ones)
                eng.wait_ge(b.sem("cl"), cl_consts)

        b.cv_marks = {}
        cv_jobs = []

        def flat(ap, N, M):
            return ap.rearrange("a p " + " ".join(
                f"d{i}" for i in range(len(ap.shape) - 2)) + " -> a p (" + " ".join(
                f"d{i}" for i in range(len(ap.shape) - 2)) + ")")

        for key, fsrc in (("fw1_0", FW1[0]), ("fw2_0", FW2[0]),
                          ("fw1_1", FW1[1]), ("fw2_1", FW2[1]),
                          ("m1", M1), ("m2", M2), ("mh1", MH1), ("mh2", MH2),
                          ("m3", M3)):
            shp = CVshapes[key]
            N = shp[0]
            M = int(np.prod(shp[2:]))
            cv_jobs.append((flat(fsrc, N, M), flat(CV[key], N, M), key, N, M))

        with nc.named_scope("p0"):
            _p0(b, nc, xin, resid, xnb, vln1w[0], vln1b[0], ones_col, ones_row, sm)
            emit_conversions(b, cv_jobs, cvb)
        if tap == "p0":
            _finish(b)
            stack.close()
            return nc

        consts = dict(CQ=CQ, SQ_=SQ_, CK=CK, SK_=SK_, GDEC=GDEC, MASKS=MASKS,
                      vdvec=vdvec, resid=resid, xnb=xnb, ybuf=ybuf)
        for li in range(2):
            Wl = dict(wq=WQd[li], wk=WKd[li], wv=WVd[li], wg=WGd[li], wo=WOd[li],
                      fw1=CV[f"fw1_{li}"], fw2=CV[f"fw2_{li}"],
                      cv_fw1=f"fw1_{li}", cv_fw2=f"fw2_{li}",
                      vln2w=vln2w[li], vln2b=vln2b[li],
                      vgnw=vgnw[li], vgnb=vgnb[li],
                      vfb1=vfb1[li], vfb2=vfb2[li],
                      vln1w_next=vln1w[1], vln1b_next=vln1b[1])
            with nc.named_scope(f"p1_{li}"):
                _p1_retention(b, li, nc, Wl, consts, ones_col, ones_row, sm)
            if tap == f"r{li}":
                _finish(b)
                stack.close()
                return nc
            with nc.named_scope(f"p3_{li}"):
                _p3_woffn(b, li, nc, Wl, consts, ones_col, ones_row, sm)
            if tap == f"l{li}":
                _finish(b)
                stack.close()
                return nc

        with nc.named_scope("p4"):
            _p4_mid(b, nc, xin, resid, xnb, ones_col, ones_row, sm)
        if tap == "p4":
            _finish(b)
            stack.close()
            return nc

        Wm = dict(M1=CV["m1"], M2=CV["m2"], MH1=CV["mh1"], MH2=CV["mh2"],
                  M3=CV["m3"],
                  vms1=vms1, vms2=vms2, vmsh1=vmsh1, vmsh2=vmsh2, vms3=vms3,
                  resid=resid, xnb=xnb, xout=xout)
        with nc.named_scope("p5"):
            _p5_mlp(b, nc, Wm, ones_col, ones_row, sm)
        _finish(b)

    stack.close()
    return nc


def _finish(b):
    for e in ("scalar", "sync"):
        @getattr(b.block, e)
        def _(eng):
            eng.wait_ge(b.sem("st"), b.v("st"))
            eng.wait_ge(b.sem("so"), b.v("so"))



def _p0(b, nc, xin, resid, xnb, vw, vb, ones_col, ones_row, sm):
    with ExitStack() as ps_:
        e_ = ps_.enter_context
        xg2 = e_(nc.sbuf_tensor("xg2", [128, 2, H, TG], F32))
        h0t = e_(nc.sbuf_tensor("h0t", [128, H, TG], F32))
        sqg = e_(nc.sbuf_tensor("sqg", [128, H, TG], F32))
        xnt = e_(nc.sbuf_tensor("xnt", [128, H, TG], BF16))
        pS1 = e_(nc.psum_tensor("pS1", [128, 512], F32))
        pS2 = e_(nc.psum_tensor("pS2", [128, 512], F32))
        pR = e_(nc.psum_tensor("pR", [128, 512], F32))
        pMR = e_(nc.psum_tensor("pMR", [128, 512], F32))

        ev_xg_free = {}
        ev_pfree_prev = 0
        st_h0_prev = 0
        st_xn_prev = 0
        ac_xn_prev = 0
        for g in range(NG):
            t0 = g * TG

            @b.block.sync
            def _(sync, _g=g, _t0=t0):
                b.w(sync, "ev", ev_xg_free.get(_g - 2))
                i = sync.dma_start(out=xg2[:, _g % 2],
                                   in_=xin[:, :, _t0:_t0 + TG].rearrange("c p t -> p c t"))
                b.inc(i, "xl", 16)
            xl_g = b.v("xl")
            xg = xg2[:, g % 2]

            pq1 = (lambda _pf=ev_pfree_prev, _xl=xl_g: (lambda tensor: (
                tensor.wait_ge(b.sem("xl"), _xl), b.w(tensor, "ev", _pf))))()
            ps1 = (lambda _xl=xl_g, _ac=ac_xn_prev: (lambda vector: (
                vector.wait_ge(b.sem("xl"), _xl), b.w(vector, "ac", _ac))))()
            mm_bc1, ev_pfree = ln_block(b, xg, H, sqg, pS1, pS2, pR, pMR,
                                        ones_col, ones_row, sm, float(D),
                                        pre_stats_wait=pq1, pre_sq_wait=ps1)

            @b.block.vector
            def _(vector, _mb=mm_bc1, _st=st_h0_prev):
                vector.wait_ge(b.sem("mm"), _mb)
                b.w(vector, "st", _st)
                for c in range(H):
                    vector.tensor_mul(h0t[:, c, :], xg[:, c, :], pR[:])
                    i = vector.tensor_sub(h0t[:, c, :], h0t[:, c, :], pMR[:])
                b.inc(i, "ev")
            ev_xg_free[g] = b.v("ev")
            ev_apply1 = b.v("ev")

            pq2 = (lambda _pf=ev_pfree, _e1=ev_apply1: (lambda tensor: (
                tensor.wait_ge(b.sem("ev"), _e1), b.w(tensor, "ev", _pf))))()
            mm_bc2, ev_pfree2 = ln_block(b, h0t, H, sqg, pS1, pS2, pR, pMR,
                                         ones_col, ones_row, sm, float(D),
                                         pre_stats_wait=pq2)
            ev_pfree_prev = ev_pfree2

            @b.block.vector
            def _(vector, _mb=mm_bc2):
                vector.wait_ge(b.sem("mm"), _mb)
                for c in range(H):
                    vector.tensor_mul(sqg[:, c, :], h0t[:, c, :], pR[:])
                    i = vector.tensor_sub(sqg[:, c, :], sqg[:, c, :], pMR[:])
                b.inc(i, "ev")
            ev_u = b.v("ev")

            @b.block.scalar
            def _(scalar, _e=ev_u, _st=st_xn_prev):
                scalar.wait_ge(b.sem("ev"), _e)
                b.w(scalar, "st", _st)
                for c in range(H):
                    i = scalar.activation(xnt[:, c, :], sqg[:, c, :], AF.Identity,
                                          bias=vb[:, c:c + 1], scale=vw[:, c:c + 1])
                b.inc(i, "ac")
            ac_xn_prev = b.v("ac")
            ac_xn = b.v("ac")

            @b.block.scalar
            def _(scalar, _t0=t0, _ac=ac_xn):
                i = scalar.dma_start(out=resid[:, :, _t0:_t0 + TG]
                                     .rearrange("c p t -> p c t"), in_=h0t[:])
                b.inc(i, "st", 16)
                scalar.wait_ge(b.sem("ac"), _ac)
                i = scalar.dma_start(out=xnb[:, :, _t0:_t0 + TG]
                                     .rearrange("c p t -> p c t"), in_=xnt[:])
                b.inc(i, "st", 16)
            st_h0_prev = b.v("st") - 16
            st_xn_prev = b.v("st")

        @b.block.scalar
        def _(scalar):
            scalar.wait_ge(b.sem("st"), b.v("st"))
        b.barrier()



def _p1_retention(b, li, nc, W, C, ones_col, ones_row, sm):
    with ExitStack() as ps_:
        e_ = ps_.enter_context
        tcq = e_(nc.sbuf_tensor(f"tcq{li}", [128, S], F32))
        tsq = e_(nc.sbuf_tensor(f"tsq{li}", [128, S], F32))
        tck = e_(nc.sbuf_tensor(f"tck{li}", [128, S], F32))
        tsk = e_(nc.sbuf_tensor(f"tsk{li}", [128, S], F32))
        tmsk = e_(nc.sbuf_tensor(f"tmsk{li}", [128, 4, TG], BF16))
        tgd = e_(nc.sbuf_tensor(f"tgd{li}", [128, 2, TG], F32))
        xnh = e_(nc.sbuf_tensor(f"xnh{li}", [128, H, S], BF16))
        wvf = e_(nc.sbuf_tensor(f"wvf{li}", [128, 1024], F32))
        wvb = e_(nc.sbuf_tensor(f"wvb{li}", [128, 8, 1024], BF16))
        wqkf = e_(nc.sbuf_tensor(f"wqkf{li}", [128, 2, 8, 128], F32))
        wqkb = e_(nc.sbuf_tensor(f"wqkb{li}", [128, 2, 2, 8, 128], BF16))
        vsb = e_(nc.sbuf_tensor(f"vsb{li}", [128, 16, 256], BF16))
        qsb = e_(nc.sbuf_tensor(f"qsb{li}", [128, S], BF16))
        ksb = e_(nc.sbuf_tensor(f"ksb{li}", [128, S], BF16))
        scsb = e_(nc.sbuf_tensor(f"scsb{li}", [128, 16, TG], BF16))
        xt1 = e_(nc.sbuf_tensor(f"xt1{li}", [128, TG], F32))
        xt2 = e_(nc.sbuf_tensor(f"xt2{li}", [128, TG], F32))
        ytl = e_(nc.sbuf_tensor(f"ytl{li}", [128, TG], F32))
        sq1 = e_(nc.sbuf_tensor(f"sq1{li}", [128, TG], F32))
        ygn = e_(nc.sbuf_tensor(f"ygn{li}", [128, S], BF16))
        pq0 = e_(nc.psum_tensor(f"pq0{li}", [128, 512], F32))
        pq1 = e_(nc.psum_tensor(f"pq1{li}", [128, 512], F32))
        pq2 = e_(nc.psum_tensor(f"pq2{li}", [128, 512], F32))
        pq3 = e_(nc.psum_tensor(f"pq3{li}", [128, 512], F32))
        psc0 = e_(nc.psum_tensor(f"psc0{li}", [128, 512], F32))
        psc1 = e_(nc.psum_tensor(f"psc1{li}", [128, 512], F32))
        pyh = e_(nc.psum_tensor(f"pyh{li}", [128, 512], F32))
        pst = e_(nc.psum_tensor(f"pst{li}", [128, 512], F32))
        pqs = [pq0, pq1, pq2, pq3]
        pscs = [psc0, psc1]

        @b.block.sync
        def _(sync):
            for src, dst in ((C["CQ"], tcq), (C["SQ_"], tsq),
                             (C["CK"], tck), (C["SK_"], tsk)):
                b.inc(sync.dma_start(out=dst[:], in_=src), "cl", 16)
            i = sync.dma_start(out=tmsk[:], in_=C["MASKS"].rearrange("d p t -> p d t"))
            b.inc(i, "cl", 16)
        cl_tab = b.v("cl")

        ev_wv_prev = 0
        for dc in range(8):
            @b.block.sync
            def _(sync, _dc=dc, _ev=ev_wv_prev):
                b.w(sync, "ev", _ev)
                i = sync.dma_start(out=wvf[:], in_=W["wv"][_dc])
                b.inc(i, "wl", 16)
            wl_v = b.v("wl")

            @b.block.vector
            def _(vector, _dc=dc, _wl=wl_v):
                vector.wait_ge(b.sem("wl"), _wl)
                i = vector.tensor_copy(wvb[:, _dc, :], wvf[:])
                b.inc(i, "ev")
            ev_wv_prev = b.v("ev")
        ev_wv = b.v("ev")

        mm_last_v_use = 0
        ev_gn_prev = 0
        ev_gn_slot = {0: 0, 1: 0}
        ev_qk_evac_prev = 0
        ev_wqkf_prev = 0
        st_ygn_prev = 0

        for bb in range(BLOC):
            tb0 = bb * S

            @b.block.sync
            def _(sync, _tb0=tb0):
                sync.wait_ge(b.sem("st"), b.v("st"))
                b.w(sync, "mm", b.v("mm"))
                i = sync.dma_start(out=xnh[:], in_=C["xnb"][:, :, _tb0:_tb0 + S]
                                   .rearrange("c p t -> p c t"))
                b.inc(i, "xl", 16)
            xl_xn = b.v("xl")

            for hg in range(4):
                def vgrp(tensor, tt, _hg=hg, _xl=xl_xn,
                         _eq=ev_qk_evac_prev, _eg=ev_gn_prev):
                    if tt == 0:
                        tensor.wait_ge(b.sem("xl"), _xl)
                        tensor.wait_ge(b.sem("ev"), ev_wv)
                        b.w(tensor, "ev", _eq)
                        b.w(tensor, "ev", _eg)
                    for dc in range(8):
                        i = tensor.matmul(pqs[tt % 4][:, 0:256],
                                          xnh[:, dc, tt * 128:(tt + 1) * 128],
                                          wvb[:, dc, 256 * _hg:256 * (_hg + 1)],
                                          start=(dc == 0), stop=(dc == 7))
                    return i

                def vevac(eng, tt, _mv=mm_last_v_use):
                    if tt == 0:
                        b.w(eng, "mm", _mv)
                    return eng.activation(vsb[:, tt, :], pqs[tt % 4][:, 0:256],
                                          AF.Copy)

                _, ac_v = pipe(b, 16, 4, vgrp, vevac, "scalar")

                for h in (2 * hg, 2 * hg + 1):
                    hcol = h - 2 * hg
                    par = h % 2

                    @b.block.sync
                    def _(sync, _par=par, _h=h, _ewf=ev_wqkf_prev,
                          _egs=ev_gn_slot[par]):
                        b.w(sync, "ev", _ewf)
                        b.inc(sync.dma_start(out=wqkf[:, 0], in_=W["wq"][_h]), "wl", 16)
                        b.inc(sync.dma_start(out=wqkf[:, 1], in_=W["wk"][_h]), "wl", 16)
                        b.w(sync, "ev", _egs)
                        b.inc(sync.dma_start(out=tgd[:, _par], in_=C["GDEC"][_h]),
                              "cl", 16)
                    wl_qk = b.v("wl")
                    cl_gd = b.v("cl")

                    @b.block.vector
                    def _(vector, _par=par, _wl=wl_qk):
                        vector.wait_ge(b.sem("wl"), _wl)
                        i = vector.tensor_copy(wqkb[:, _par], wqkf[:])
                        b.inc(i, "ev")
                    ev_qk_w = b.v("ev")
                    ev_wqkf_prev = ev_qk_w

                    for qk in range(2):
                        dst = qsb if qk == 0 else ksb
                        ctab = tcq if qk == 0 else tck
                        stab = tsq if qk == 0 else tsk

                        def qgrp(tensor, tt, _qk=qk, _par=par, _evw=ev_qk_w,
                                 _evfree=ev_qk_evac_prev, _evgn=ev_gn_prev,
                                 _acv=ac_v):
                            if tt == 0:
                                tensor.wait_ge(b.sem("ev"), _evw)
                                tensor.wait_ge(b.sem("cl"), cl_tab)
                                b.w(tensor, "ev", _evfree)
                                b.w(tensor, "ev", _evgn)
                                if _qk == 0:
                                    b.w(tensor, "ac", _acv)
                            for dc in range(8):
                                i = tensor.matmul(pqs[tt][:],
                                                  wqkb[:, _par, _qk, dc, :],
                                                  xnh[:, dc, tt * TG:(tt + 1) * TG],
                                                  start=(dc == 0), stop=(dc == 7))
                            return i

                        def qevac(eng, tt, _dst=dst, _ct=ctab, _st=stab):
                            sl = slice(tt * TG, (tt + 1) * TG)
                            ps = pqs[tt]
                            eng.tensor_mul(xt1[:], ps[:], _ct[:, sl])
                            eng.tensor_mul(xt2[0:64, :], ps[64:128, :], _st[0:64, sl])
                            eng.tensor_mul(xt2[64:128, :], ps[0:64, :], _st[64:128, sl])
                            return eng.tensor_add(_dst[:, sl], xt1[:], xt2[:])

                        pipe(b, 4, 4, qgrp, qevac, "vector")
                        ev_qk_evac_prev = b.v("ev")

                    for sc in range(4):
                        s0 = sc * TG
                        ntj = 4 * sc + 4

                        def sgrp(tensor, tj, _s0=s0, _ev=ev_qk_evac_prev):
                            if tj == 0:
                                tensor.wait_ge(b.sem("ev"), _ev)
                            return tensor.matmul(pscs[tj % 2][:],
                                                 ksb[:, tj * 128:(tj + 1) * 128],
                                                 qsb[:, _s0:_s0 + TG],
                                                 start=True, stop=True)

                        def sevac(eng, tj, _sc=sc, _h=h):
                            mi = 4 * _sc - tj + 3
                            return eng.activation(scsb[:, tj, :], pscs[tj % 2][:],
                                                  AF.Identity,
                                                  scale=C["vdvec"][:, _h, mi:mi + 1])

                        ac_sc_base = b.v("ac")
                        _, ac_sc = pipe(b, ntj, 2, sgrp, sevac, "scalar")

                        @b.block.vector
                        def _(vector, _sc=sc, _acb=ac_sc_base):
                            for d in range(4):
                                tj = 4 * _sc + d
                                vector.wait_ge(b.sem("ac"), _acb + tj + 1)
                                i = vector.tensor_mul(scsb[:, tj, :], scsb[:, tj, :],
                                                      tmsk[:, d, :])
                                b.inc(i, "ev")
                        ev_msk = b.v("ev")

                        @b.block.tensor
                        def _(tensor, _hcol=hcol, _ntj=ntj, _acs=ac_sc,
                              _evm=ev_msk, _evgn=ev_gn_prev):
                            tensor.wait_ge(b.sem("ac"), _acs)
                            tensor.wait_ge(b.sem("ev"), _evm)
                            b.w(tensor, "ev", _evgn)
                            for tj in range(_ntj):
                                i = tensor.matmul(
                                    pyh[:], vsb[:, tj, _hcol * 128:(_hcol + 1) * 128],
                                    scsb[:, tj, :],
                                    start=(tj == 0), stop=(tj == _ntj - 1))
                            b.inc(i, "mm")
                        mm_yh = b.v("mm")
                        mm_last_v_use = mm_yh

                        @b.block.vector
                        def _(vector, _par=par, _my=mm_yh, _cg=cl_gd):
                            vector.wait_ge(b.sem("mm"), _my)
                            vector.wait_ge(b.sem("cl"), _cg)
                            i = vector.tensor_mul(ytl[:], pyh[:], tgd[:, _par])
                            b.inc(i, "ev")
                        ev_yt = b.v("ev")

                        @b.block.scalar
                        def _(scalar, _e=ev_yt):
                            scalar.wait_ge(b.sem("ev"), _e)
                            i = scalar.activation(sq1[:], ytl[:], AF.Square)
                            b.inc(i, "ac")
                        ac_sq = b.v("ac")

                        @b.block.tensor
                        def _(tensor, _a=ac_sq, _evgn=ev_gn_prev):
                            tensor.wait_ge(b.sem("ac"), _a)
                            b.w(tensor, "ev", _evgn)
                            tensor.matmul(pst[0:1, :], ones_col[:], ytl[:],
                                          start=True, stop=True)
                            i = tensor.matmul(pq0[0:1, :], ones_col[:], sq1[:],
                                              start=True, stop=True)
                            b.inc(i, "mm")
                        mm_gstat = b.v("mm")

                        ln_smalls(b, sm, pst, pq0, float(HD), mm_gstat)
                        ev_mr = b.v("ev")

                        @b.block.tensor
                        def _(tensor, _em=ev_mr, _evgn=ev_gn_prev):
                            tensor.wait_ge(b.sem("ev"), _em)
                            b.w(tensor, "ev", _evgn)
                            tensor.matmul(pq1[:], ones_row[:], sm["rstd"][0:1, :],
                                          start=True, stop=True)
                            i = tensor.matmul(pq2[:], ones_row[:], sm["mr"][0:1, :],
                                              start=True, stop=True)
                            b.inc(i, "mm")
                        mm_gbc = b.v("mm")

                        @b.block.vector
                        def _(vector, _mb=mm_gbc):
                            vector.wait_ge(b.sem("mm"), _mb)
                            vector.tensor_mul(xt1[:], ytl[:], pq1[:])
                            i = vector.tensor_sub(xt1[:], xt1[:], pq2[:])
                            b.inc(i, "ev")
                        ev_gn_prev = b.v("ev")
                        ev_gn_slot[par] = b.v("ev")
                        ev_gna = b.v("ev")

                        @b.block.scalar
                        def _(scalar, _s0=s0, _h=h, _e=ev_gna, _st=st_ygn_prev):
                            scalar.wait_ge(b.sem("ev"), _e)
                            if _s0 == 0:
                                b.w(scalar, "st", _st)
                            i = scalar.activation(ygn[:, _s0:_s0 + TG], xt1[:],
                                                  AF.Identity,
                                                  bias=W["vgnb"][:, _h:_h + 1],
                                                  scale=W["vgnw"][:, _h:_h + 1])
                            b.inc(i, "ac")
                    ac_gn_all = b.v("ac")

                    @b.block.scalar
                    def _(scalar, _h=h, _tb0=tb0, _a=ac_gn_all):
                        scalar.wait_ge(b.sem("ac"), _a)
                        i = scalar.dma_start(out=C["ybuf"][_h][:, _tb0:_tb0 + S],
                                             in_=ygn[:])
                        b.inc(i, "st", 16)
                    st_ygn_prev = b.v("st")

        @b.block.scalar
        def _(scalar):
            scalar.wait_ge(b.sem("st"), b.v("st"))
        b.barrier()



P3_STAGE = None
HTAP = None


def _p3_woffn(b, li, nc, W, C, ones_col, ones_row, sm):
    resid, xnb, ybuf = C["resid"], C["xnb"], C["ybuf"]
    last = (li == 1)
    with nc.sbuf_tensor(f"wgo{li}", [128, 2, 8, 8, 128], BF16) as wgo:
        with nc.sbuf_tensor(f"wgof{li}", [128, 4, 8, 128], F32) as wgof:
            ev_cv_prev = 0
            for k in range(2):
                wsrc = W["wg"] if k == 0 else W["wo"]
                for hf in range(2):
                    @b.block.sync
                    def _(sync, _hf=hf, _w=wsrc, _ev=ev_cv_prev):
                        b.w(sync, "ev", _ev)
                        i = sync.dma_start(out=wgof[:], in_=_w[_hf])
                        b.inc(i, "wl", 16)
                    wl_w = b.v("wl")

                    @b.block.vector
                    def _(vector, _k=k, _hf=hf, _wl=wl_w):
                        vector.wait_ge(b.sem("wl"), _wl)
                        i = vector.tensor_copy(wgo[:, _k, 4 * _hf:4 * _hf + 4],
                                               wgof[:])
                        b.inc(i, "ev")
                    ev_cv_prev = b.v("ev")
            ev_wgo = b.v("ev")

        with ExitStack() as ps_:
            e_ = ps_.enter_context
            xng2 = e_(nc.sbuf_tensor(f"xng2{li}", [128, 2, H, TG], BF16))
            ybg2 = e_(nc.sbuf_tensor(f"ybg2{li}", [128, 2, H, TG], BF16))
            xrg = e_(nc.sbuf_tensor(f"xrg{li}", [128, H, TG], F32))
            gg = e_(nc.sbuf_tensor(f"gg{li}", [128, H, TG], BF16))
            ysb3 = e_(nc.sbuf_tensor(f"ysb3{li}", [128, H, TG], F32))
            ynt = e_(nc.sbuf_tensor(f"ynt{li}", [128, H, TG], BF16))
            hh = e_(nc.sbuf_tensor(f"hh{li}", [128, 32, TG], BF16))
            sq3 = e_(nc.sbuf_tensor(f"sq3{li}", [128, 2, TG], F32))
            f1b = e_(nc.sbuf_tensor(f"f1b{li}", [128, 2, 4, 8, 128], BF16))
            f2b = e_(nc.sbuf_tensor(f"f2b{li}", [128, 2, 32, 128], BF16))
            pa = e_(nc.psum_tensor(f"pa{li}", [128, 512], F32))
            pb = e_(nc.psum_tensor(f"pb{li}", [128, 512], F32))
            pc = e_(nc.psum_tensor(f"pc{li}", [128, 512], F32))
            pd = e_(nc.psum_tensor(f"pd{li}", [128, 512], F32))
            pe = e_(nc.psum_tensor(f"pe{li}", [128, 512], F32))
            pf = e_(nc.psum_tensor(f"pf{li}", [128, 512], F32))

            ev_pfree_prev = 0
            ev_apply_prev = 0
            mm_xn_used = 0
            ev_pg_prev = {0: 0, 1: 0}
            ev_y_built_prev = 0
            mm_f1_piece = {0: 0, 1: 0}
            mm_f2_piece = {0: 0, 1: 0}
            st_resid_prev = 0
            st_xnn_prev = 0
            st_tap_prev = [0]
            ac_affine_prev = 0
            for g in range(NG):
                t0 = g * TG

                @b.block.sync
                def _(sync, _g=g, _t0=t0, _mx=mm_xn_used,
                      _ey=ev_pg_prev[g % 2], _eyb=ev_y_built_prev):
                    if _g == 0:
                        sync.wait_ge(b.sem("st"), b.v("st"))
                        sync.wait_ge(b.sem("ev"), ev_wgo)
                    b.w(sync, "mm", _mx)
                    i = sync.dma_start(out=xng2[:, _g % 2],
                                       in_=xnb[:, :, _t0:_t0 + TG]
                                       .rearrange("c p t -> p c t"))
                    b.inc(i, "xl", 16)
                    b.w(sync, "ev", _ey)
                    i = sync.dma_start(out=ybg2[:, _g % 2],
                                       in_=ybuf[:, :, _t0:_t0 + TG]
                                       .rearrange("c p t -> p c t"))
                    b.inc(i, "xl", 16)
                    b.w(sync, "ev", _eyb)
                    i = sync.dma_start(out=xrg[:],
                                       in_=resid[:, :, _t0:_t0 + TG]
                                       .rearrange("c p t -> p c t"))
                    b.inc(i, "xl", 16)
                xl_g = b.v("xl")
                xng = xng2[:, g % 2]
                ybg = ybg2[:, g % 2]

                def ggrp(tensor, oc, _xl=xl_g, _ew=ev_wgo):
                    if oc == 0:
                        tensor.wait_ge(b.sem("xl"), _xl - 32)
                        tensor.wait_ge(b.sem("ev"), _ew)
                    for dc in range(8):
                        i = tensor.matmul((pa if oc % 2 == 0 else pb)[:],
                                          wgo[:, 0, oc, dc, :], xng[:, dc, :],
                                          start=(dc == 0), stop=(dc == 7))
                    return i

                def gevac(eng, oc, _stt=globals().get("_ST_TAP", 0) if False else None):
                    if oc == 0 and P3_STAGE in ("g", "p"):
                        b.w(eng, "st", st_tap_prev[0])
                    return eng.activation(gg[:, oc, :],
                                          (pa if oc % 2 == 0 else pb)[:], AF.Silu)

                mm_g, ac_g = pipe(b, H, 2, ggrp, gevac, "scalar")
                mm_xn_used = mm_g

                if P3_STAGE == "g":
                    @b.block.scalar
                    def _(scalar, _t0=t0, _a=ac_g):
                        scalar.wait_ge(b.sem("ac"), _a)
                        i = scalar.dma_start(out=xnb[:, :, _t0:_t0 + TG]
                                             .rearrange("c p t -> p c t"), in_=gg[:])
                        b.inc(i, "st", 16)
                    st_tap_prev[0] = b.v("st")
                    continue

                @b.block.vector
                def _(vector, _a=ac_g, _xl=xl_g):
                    vector.wait_ge(b.sem("ac"), _a)
                    vector.wait_ge(b.sem("xl"), _xl - 16)
                    i = vector.tensor_mul(gg[:], gg[:], ybg[:])
                    b.inc(i, "ev")
                ev_pg_prev[g % 2] = b.v("ev")
                ev_pg = b.v("ev")

                if P3_STAGE == "p":
                    @b.block.scalar
                    def _(scalar, _t0=t0, _e=ev_pg):
                        scalar.wait_ge(b.sem("ev"), _e)
                        i = scalar.dma_start(out=xnb[:, :, _t0:_t0 + TG]
                                             .rearrange("c p t -> p c t"), in_=gg[:])
                        b.inc(i, "st", 16)
                    st_tap_prev[0] = b.v("st")
                    continue

                def wgrp(tensor, oc, _ev=ev_pg, _xl=xl_g):
                    if oc == 0:
                        tensor.wait_ge(b.sem("ev"), _ev)
                        tensor.wait_ge(b.sem("xl"), _xl)
                    for dc in range(8):
                        i = tensor.matmul((pc if oc % 2 == 0 else pd)[:],
                                          wgo[:, 1, oc, dc, :], gg[:, dc, :],
                                          start=(dc == 0), stop=(dc == 7))
                    return i

                def wevac(eng, oc, _st=st_resid_prev):
                    if oc == 0:
                        b.w(eng, "st", _st)
                    return eng.tensor_add(ysb3[:, oc, :],
                                          (pc if oc % 2 == 0 else pd)[:],
                                          xrg[:, oc, :])

                _, ev_y = pipe(b, H, 2, wgrp, wevac, "vector")
                ev_y_built_prev = ev_y

                if P3_STAGE == "y":
                    @b.block.scalar
                    def _(scalar, _t0=t0, _ev=ev_y):
                        scalar.wait_ge(b.sem("ev"), _ev)
                        i = scalar.dma_start(out=resid[:, :, _t0:_t0 + TG]
                                             .rearrange("c p t -> p c t"),
                                             in_=ysb3[:])
                        b.inc(i, "st", 16)
                    st_resid_prev = b.v("st")
                    continue

                pq = (lambda _pf=ev_pfree_prev, _ey=ev_y: (lambda tensor: (
                    tensor.wait_ge(b.sem("ev"), _ey), b.w(tensor, "ev", _pf))))()
                ps = (lambda _ea=ev_apply_prev, _ey=ev_y, _sx=st_xnn_prev:
                      (lambda scalar: (scalar.wait_ge(b.sem("ev"), _ey),
                                       b.w(scalar, "ev", _ea),
                                       b.w(scalar, "st", _sx))))()
                mm_bc, ev_pfree = ln_block(b, ysb3, H, sq3, pe, pf, pe, pf,
                                           ones_col, ones_row, sm, float(D),
                                           sq_on_vector=False,
                                           pre_stats_wait=pq, pre_sq_wait=ps)
                ev_pfree_prev = ev_pfree

                ev_ap_base = b.v("ev")
                ac_aff_base = b.v("ac")

                @b.block.vector
                def _(vector, _mb=mm_bc, _ab=ac_aff_base):
                    vector.wait_ge(b.sem("mm"), _mb)
                    for c in range(H):
                        if c >= 2:
                            vector.wait_ge(b.sem("ac"), _ab + c - 1)
                        vector.tensor_mul(sq3[:, c % 2, :], ysb3[:, c, :], pe[:])
                        i = vector.tensor_sub(sq3[:, c % 2, :], sq3[:, c % 2, :],
                                              pf[:])
                        b.inc(i, "ev")
                ev_apply_prev = b.v("ev")

                @b.block.scalar
                def _(scalar, _ab=ev_ap_base, _sx=st_xnn_prev):
                    b.w(scalar, "st", _sx)
                    for c in range(H):
                        scalar.wait_ge(b.sem("ev"), _ab + c + 1)
                        i = scalar.activation(ynt[:, c, :], sq3[:, c % 2, :],
                                              AF.Identity,
                                              bias=W["vln2b"][:, c:c + 1],
                                              scale=W["vln2w"][:, c:c + 1])
                        b.inc(i, "ac")
                ac_yn = b.v("ac")
                ac_affine_prev = ac_yn

                if P3_STAGE == "yn":
                    @b.block.scalar
                    def _(scalar, _t0=t0, _a=ac_yn):
                        scalar.wait_ge(b.sem("ac"), _a)
                        i = scalar.dma_start(out=xnb[:, :, _t0:_t0 + TG]
                                             .rearrange("c p t -> p c t"),
                                             in_=ynt[:])
                        b.inc(i, "st", 16)
                    st_xnn_prev = b.v("st")
                    continue

                mm_ffn1_base = b.v("mm")
                gl_f1 = {}
                for piece in range(8):
                    @b.block.sync
                    def _(sync, _p=piece, _g=g):
                        if _g == 0 and _p == 0:
                            sync.wait_ge(b.sem("cv"), b.cv_marks[W["cv_fw1"]])
                        key = _p % 2
                        b.w(sync, "mm", mm_f1_piece[key])
                        i = sync.dma_start(out=f1b[:, key], in_=W["fw1"][_p])
                        b.inc(i, "gl", 16)
                        mm_f1_piece[key] = mm_ffn1_base + 4 * (_p + 1)
                    gl_f1[piece] = b.v("gl")

                def f1grp(tensor, hc, _ac=ac_yn):
                    piece, wi = hc // 4, hc % 4
                    if hc == 0:
                        tensor.wait_ge(b.sem("ac"), _ac)
                    if wi == 0:
                        tensor.wait_ge(b.sem("gl"), gl_f1[piece])
                    for dc in range(8):
                        i = tensor.matmul((pa if hc % 2 == 0 else pb)[:],
                                          f1b[:, piece % 2, wi, dc, :],
                                          ynt[:, dc, :],
                                          start=(dc == 0), stop=(dc == 7))
                    return i

                def f1evac(eng, hc):
                    return eng.activation(hh[:, hc, :],
                                          (pa if hc % 2 == 0 else pb)[:], AF.Silu,
                                          bias=W["vfb1"][:, hc:hc + 1])

                _, ac_h = pipe(b, 32, 2, f1grp, f1evac, "scalar")

                if P3_STAGE == "h" and HTAP is not None:
                    @b.block.scalar
                    def _(scalar, _g=g, _a=ac_h):
                        scalar.wait_ge(b.sem("ac"), _a)
                        i = scalar.dma_start(out=HTAP[_g], in_=hh[:])
                        b.inc(i, "st", 16)
                    st_tap_prev[0] = b.v("st")

                if P3_STAGE == "h":
                    @b.block.scalar
                    def _(scalar):
                        b.w(scalar, "st", st_tap_prev[0])

                mm_ffn2_base = b.v("mm")
                gl_f2 = {}
                for oc in range(8):
                    @b.block.sync
                    def _(sync, _oc=oc, _g=g):
                        if _g == 0 and _oc == 0:
                            sync.wait_ge(b.sem("cv"), b.cv_marks[W["cv_fw2"]])
                        key = _oc % 2
                        b.w(sync, "mm", mm_f2_piece[key])
                        i = sync.dma_start(out=f2b[:, key], in_=W["fw2"][_oc])
                        b.inc(i, "gl", 16)
                        mm_f2_piece[key] = mm_ffn2_base + _oc + 1
                    gl_f2[oc] = b.v("gl")

                def f2grp(tensor, oc, _ac=ac_h):
                    if oc == 0:
                        tensor.wait_ge(b.sem("ac"), _ac)
                    tensor.wait_ge(b.sem("gl"), gl_f2[oc])
                    for hc in range(32):
                        i = tensor.matmul((pc if oc % 2 == 0 else pd)[:],
                                          f2b[:, oc % 2, hc, :], hh[:, hc, :],
                                          start=(hc == 0), stop=(hc == 31))
                    return i

                def f2evac(eng, oc):
                    return eng.scalar_tensor_tensor(
                        ysb3[:, oc, :], (pc if oc % 2 == 0 else pd)[:],
                        W["vfb2"][:, oc:oc + 1], ysb3[:, oc, :],
                        op0=ALU.add, op1=ALU.add)

                _, ev_xp = pipe(b, H, 2, f2grp, f2evac, "vector")

                @b.block.scalar
                def _(scalar, _t0=t0, _ev=ev_xp):
                    scalar.wait_ge(b.sem("ev"), _ev)
                    i = scalar.dma_start(out=resid[:, :, _t0:_t0 + TG]
                                         .rearrange("c p t -> p c t"), in_=ysb3[:])
                    b.inc(i, "st", 16)
                st_resid_prev = b.v("st")

                if not last:
                    pq2 = (lambda _pf=ev_pfree_prev, _ex=ev_xp: (lambda tensor: (
                        tensor.wait_ge(b.sem("ev"), _ex), b.w(tensor, "ev", _pf))))()
                    ps2 = (lambda _ea=ev_apply_prev, _ex=ev_xp:
                           (lambda scalar: (scalar.wait_ge(b.sem("ev"), _ex),
                                            b.w(scalar, "ev", _ea))))()
                    mm_bc2, ev_pfree2 = ln_block(b, ysb3, H, sq3, pe, pf, pe, pf,
                                                 ones_col, ones_row, sm, float(D),
                                                 sq_on_vector=False,
                                                 pre_stats_wait=pq2,
                                                 pre_sq_wait=ps2)
                    ev_pfree_prev = ev_pfree2

                    ev_ap2 = b.v("ev")
                    ac_af2 = b.v("ac")

                    @b.block.vector
                    def _(vector, _mb=mm_bc2, _ab=ac_af2):
                        vector.wait_ge(b.sem("mm"), _mb)
                        for c in range(H):
                            if c >= 2:
                                vector.wait_ge(b.sem("ac"), _ab + c - 1)
                            vector.tensor_mul(sq3[:, c % 2, :], ysb3[:, c, :], pe[:])
                            i = vector.tensor_sub(sq3[:, c % 2, :],
                                                  sq3[:, c % 2, :], pf[:])
                            b.inc(i, "ev")
                    ev_apply_prev = b.v("ev")

                    @b.block.scalar
                    def _(scalar, _ab=ev_ap2):
                        for c in range(H):
                            scalar.wait_ge(b.sem("ev"), _ab + c + 1)
                            i = scalar.activation(ynt[:, c, :], sq3[:, c % 2, :],
                                                  AF.Identity,
                                                  bias=W["vln1b_next"][:, c:c + 1],
                                                  scale=W["vln1w_next"][:, c:c + 1])
                            b.inc(i, "ac")
                    ac_xnn = b.v("ac")
                    ac_affine_prev = ac_xnn

                    @b.block.scalar
                    def _(scalar, _t0=t0, _a=ac_xnn):
                        scalar.wait_ge(b.sem("ac"), _a)
                        i = scalar.dma_start(out=xnb[:, :, _t0:_t0 + TG]
                                             .rearrange("c p t -> p c t"), in_=ynt[:])
                        b.inc(i, "st", 16)
                    st_xnn_prev = b.v("st")

            @b.block.scalar
            def _(scalar):
                scalar.wait_ge(b.sem("st"), b.v("st"))
            b.barrier()



def _p4_mid(b, nc, xin, resid, xnb, ones_col, ones_row, sm):
    with ExitStack() as ps_:
        e_ = ps_.enter_context
        ag2 = e_(nc.sbuf_tensor("ag2", [128, 2, H, TG], F32))
        x0g2 = e_(nc.sbuf_tensor("x0g2", [128, 2, H, TG], F32))
        sqg4 = e_(nc.sbuf_tensor("sqg4", [128, H, TG], F32))
        x1t = e_(nc.sbuf_tensor("x1t", [128, H, TG], F32))
        xmt = e_(nc.sbuf_tensor("xmt", [128, H, TG], BF16))
        qS1 = e_(nc.psum_tensor("qS1", [128, 512], F32))
        qS2 = e_(nc.psum_tensor("qS2", [128, 512], F32))
        qR = e_(nc.psum_tensor("qR", [128, 512], F32))
        qMR = e_(nc.psum_tensor("qMR", [128, 512], F32))

        ev_slot_free = {}
        ev_pfree_prev = 0
        st_x1_prev = 0
        st_xm_prev = 0
        for g in range(NG):
            t0 = g * TG

            @b.block.sync
            def _(sync, _g=g, _t0=t0):
                if _g == 0:
                    sync.wait_ge(b.sem("st"), b.v("st"))
                b.w(sync, "ev", ev_slot_free.get(_g - 2))
                i = sync.dma_start(out=ag2[:, _g % 2],
                                   in_=resid[:, :, _t0:_t0 + TG]
                                   .rearrange("c p t -> p c t"))
                b.inc(i, "xl", 16)
                i = sync.dma_start(out=x0g2[:, _g % 2],
                                   in_=xin[:, :, _t0:_t0 + TG]
                                   .rearrange("c p t -> p c t"))
                b.inc(i, "xl", 16)
            xl_g = b.v("xl")
            ag = ag2[:, g % 2]
            x0g = x0g2[:, g % 2]

            pq = (lambda _pf=ev_pfree_prev, _xl=xl_g: (lambda tensor: (
                tensor.wait_ge(b.sem("xl"), _xl - 16), b.w(tensor, "ev", _pf))))()
            ps = (lambda _xl=xl_g: (lambda vector:
                  vector.wait_ge(b.sem("xl"), _xl - 16)))()
            mm_bc, ev_pfree = ln_block(b, ag, H, sqg4, qS1, qS2, qR, qMR,
                                       ones_col, ones_row, sm, float(D),
                                       pre_stats_wait=pq, pre_sq_wait=ps)

            @b.block.vector
            def _(vector, _mb=mm_bc, _st=st_x1_prev, _xl=xl_g):
                vector.wait_ge(b.sem("mm"), _mb)
                vector.wait_ge(b.sem("xl"), _xl)
                b.w(vector, "st", _st)
                for c in range(H):
                    vector.tensor_mul(x1t[:, c, :], ag[:, c, :], qR[:])
                    vector.tensor_sub(x1t[:, c, :], x1t[:, c, :], qMR[:])
                    i = vector.tensor_add(x1t[:, c, :], x1t[:, c, :], x0g[:, c, :])
                b.inc(i, "ev")
            ev_slot_free[g] = b.v("ev")
            ev_x1 = b.v("ev")

            pq2 = (lambda _pf=ev_pfree, _e1=ev_x1: (lambda tensor: (
                tensor.wait_ge(b.sem("ev"), _e1), b.w(tensor, "ev", _pf))))()
            mm_bc2, ev_pfree2 = ln_block(b, x1t, H, sqg4, qS1, qS2, qR, qMR,
                                         ones_col, ones_row, sm, float(D),
                                         pre_stats_wait=pq2)
            ev_pfree_prev = ev_pfree2

            @b.block.vector
            def _(vector, _mb=mm_bc2, _st=st_xm_prev):
                vector.wait_ge(b.sem("mm"), _mb)
                b.w(vector, "st", _st)
                for c in range(H):
                    vector.tensor_mul(sqg4[:, c, :], x1t[:, c, :], qR[:])
                    i = vector.tensor_sub(xmt[:, c, :], sqg4[:, c, :], qMR[:])
                b.inc(i, "ev")
            ev_xm = b.v("ev")

            @b.block.scalar
            def _(scalar, _t0=t0, _e=ev_xm):
                scalar.wait_ge(b.sem("ev"), _e)
                i = scalar.dma_start(out=resid[:, :, _t0:_t0 + TG]
                                     .rearrange("c p t -> p c t"), in_=x1t[:])
                b.inc(i, "st", 16)
                i = scalar.dma_start(out=xnb[:, :, _t0:_t0 + TG]
                                     .rearrange("c p t -> p c t"), in_=xmt[:])
                b.inc(i, "st", 16)
            st_x1_prev = b.v("st") - 16
            st_xm_prev = b.v("st")

        @b.block.scalar
        def _(scalar):
            scalar.wait_ge(b.sem("st"), b.v("st"))
        b.barrier()



def _p5_mlp(b, nc, W, ones_col, ones_row, sm):
    resid, xnb, xout = W["resid"], W["xnb"], W["xout"]
    with ExitStack() as ps_:
        e_ = ps_.enter_context
        xmg2 = e_(nc.sbuf_tensor("xmg2", [128, 2, H, TG], BF16))
        x1g = e_(nc.sbuf_tensor("x1g", [128, H, TG], F32))
        ham = e_(nc.sbuf_tensor("ham", [128, 32, TG], BF16))
        hbm = e_(nc.sbuf_tensor("hbm", [128, 32, TG], BF16))
        tmp2k = e_(nc.sbuf_tensor("tmp2k", [128, 2, TG], BF16))
        mot = e_(nc.sbuf_tensor("mot", [128, H, TG], F32))
        sq5 = e_(nc.sbuf_tensor("sq5", [128, 2, TG], F32))
        w12b = e_(nc.sbuf_tensor("w12b", [128, 2, 2, 2, 8, 128], BF16))
        whb = e_(nc.sbuf_tensor("whb", [128, 2, 32, 128], BF16))
        w3b = e_(nc.sbuf_tensor("w3b", [128, 32, 128], BF16))
        ma = e_(nc.psum_tensor("ma", [128, 512], F32))
        mb = e_(nc.psum_tensor("mb", [128, 512], F32))
        mc = e_(nc.psum_tensor("mc", [128, 512], F32))
        md = e_(nc.psum_tensor("md", [128, 512], F32))
        me = e_(nc.psum_tensor("me", [128, 512], F32))
        mf = e_(nc.psum_tensor("mf", [128, 512], F32))

        cv_all = b.v("cv")
        mm_w12_piece = {0: 0, 1: 0}
        mm_wh_slab = {0: 0, 1: 0}
        mm_w3_used = 0
        ev_pfree_prev = 0
        ev_apply_prev = 0
        ev_x1_used_prev = 0
        mm_xm_used = 0
        st_o_prev = 0
        for g in range(NG):
            t0 = g * TG

            @b.block.sync
            def _(sync, _g=g, _t0=t0, _mx=mm_xm_used, _e1=ev_x1_used_prev):
                if _g == 0:
                    sync.wait_ge(b.sem("st"), b.v("st"))
                b.w(sync, "mm", _mx)
                i = sync.dma_start(out=xmg2[:, _g % 2],
                                   in_=xnb[:, :, _t0:_t0 + TG]
                                   .rearrange("c p t -> p c t"))
                b.inc(i, "xl", 16)
                b.w(sync, "ev", _e1)
                i = sync.dma_start(out=x1g[:],
                                   in_=resid[:, :, _t0:_t0 + TG]
                                   .rearrange("c p t -> p c t"))
                b.inc(i, "xl", 16)
            xl_g = b.v("xl")
            xmg = xmg2[:, g % 2]

            mm_tl_base = b.v("mm")
            gl_12 = {}
            for piece in range(16):
                @b.block.sync
                def _(sync, _p=piece, _g=g):
                    if _g == 0 and _p == 0:
                        sync.wait_ge(b.sem("cv"), cv_all)
                    key = _p % 2
                    b.w(sync, "mm", mm_w12_piece[key])
                    i = sync.dma_start(out=w12b[:, key, 0], in_=W["M1"][_p])
                    b.inc(i, "gl", 16)
                    i = sync.dma_start(out=w12b[:, key, 1], in_=W["M2"][_p])
                    b.inc(i, "gl", 16)
                    mm_w12_piece[key] = mm_tl_base + 4 * (_p + 1)
                gl_12[piece] = b.v("gl")

            ac_tl_base = b.v("ac")
            ev_tl_base = b.v("ev")

            @b.block.tensor
            def _(tensor, _xl=xl_g):
                tensor.wait_ge(b.sem("xl"), _xl - 16)
                for hc in range(32):
                    piece, wi = hc // 2, hc % 2
                    if wi == 0:
                        tensor.wait_ge(b.sem("gl"), gl_12[piece])
                    if hc >= 2:
                        tensor.wait_ge(b.sem("ev"), ev_tl_base + hc - 1)
                    pA = ma if hc % 2 == 0 else mc
                    pB = mb if hc % 2 == 0 else md
                    for dc in range(8):
                        i = tensor.matmul(pA[:], w12b[:, piece % 2, 0, wi, dc, :],
                                          xmg[:, dc, :], start=(dc == 0),
                                          stop=(dc == 7))
                    b.inc(i, "mm")
                    for dc in range(8):
                        i = tensor.matmul(pB[:], w12b[:, piece % 2, 1, wi, dc, :],
                                          xmg[:, dc, :], start=(dc == 0),
                                          stop=(dc == 7))
                    b.inc(i, "mm")
            mm_xm_used = b.v("mm")

            @b.block.scalar
            def _(scalar):
                for hc in range(32):
                    scalar.wait_ge(b.sem("mm"), mm_tl_base + 2 * hc + 1)
                    i = scalar.activation(tmp2k[:, hc % 2, :],
                                          (ma if hc % 2 == 0 else mc)[:], AF.Silu,
                                          scale=W["vms1"][:, hc:hc + 1])
                    b.inc(i, "ac")

            @b.block.vector
            def _(vector):
                for hc in range(32):
                    vector.wait_ge(b.sem("ac"), ac_tl_base + hc + 1)
                    vector.wait_ge(b.sem("mm"), mm_tl_base + 2 * hc + 2)
                    i = vector.scalar_tensor_tensor(
                        ham[:, hc, :], (mb if hc % 2 == 0 else md)[:],
                        W["vms2"][:, hc:hc + 1], tmp2k[:, hc % 2, :],
                        op0=ALU.mult, op1=ALU.mult)
                    b.inc(i, "ev")
            ev_ha = b.v("ev")

            def hid(wdram, scales, src, dst, ev_src_ready):
                mm_hid_base = b.v("mm")
                gl_h = {}
                for oc in range(32):
                    @b.block.sync
                    def _(sync, _oc=oc):
                        key = _oc % 2
                        b.w(sync, "mm", mm_wh_slab[key])
                        i = sync.dma_start(out=whb[:, key], in_=wdram[_oc])
                        b.inc(i, "gl", 16)
                        mm_wh_slab[key] = mm_hid_base + _oc + 1
                    gl_h[oc] = b.v("gl")

                def hgrp(tensor, oc, _esr=ev_src_ready):
                    if oc == 0:
                        tensor.wait_ge(b.sem("ev"), _esr)
                    tensor.wait_ge(b.sem("gl"), gl_h[oc])
                    for hc in range(32):
                        i = tensor.matmul((mc if oc % 2 == 0 else md)[:],
                                          whb[:, oc % 2, hc, :], src[:, hc, :],
                                          start=(hc == 0), stop=(hc == 31))
                    return i

                def hevac(eng, oc):
                    return eng.activation(dst[:, oc, :],
                                          (mc if oc % 2 == 0 else md)[:], AF.Gelu,
                                          scale=scales[:, oc:oc + 1])

                return pipe(b, 32, 2, hgrp, hevac, "scalar")

            _, ac_h1 = hid(W["MH1"], W["vmsh1"], ham, hbm, ev_ha)

            @b.block.vector
            def _(vector, _a=ac_h1):
                vector.wait_ge(b.sem("ac"), _a)
                i = vector.engine_nop()
                b.inc(i, "ev")
            ev_h1 = b.v("ev")
            _, ac_h2 = hid(W["MH2"], W["vmsh2"], hbm, ham, ev_h1)

            @b.block.vector
            def _(vector, _a=ac_h2):
                vector.wait_ge(b.sem("ac"), _a)
                i = vector.engine_nop()
                b.inc(i, "ev")
            ev_h2 = b.v("ev")

            mm_m3_base = b.v("mm")
            gl_3 = {}
            for oc in range(8):
                @b.block.sync
                def _(sync, _oc=oc):
                    b.w(sync, "mm", mm_w3_used if _oc == 0 else mm_m3_base + _oc)
                    i = sync.dma_start(out=w3b[:], in_=W["M3"][_oc])
                    b.inc(i, "gl", 16)
                gl_3[oc] = b.v("gl")

            def m3grp(tensor, oc, _e=ev_h2):
                if oc == 0:
                    tensor.wait_ge(b.sem("ev"), _e)
                tensor.wait_ge(b.sem("gl"), gl_3[oc])
                for hc in range(32):
                    i = tensor.matmul((me if oc % 2 == 0 else mf)[:],
                                      w3b[:, hc, :], ham[:, hc, :],
                                      start=(hc == 0), stop=(hc == 31))
                return i

            def m3evac(eng, oc):
                return eng.activation(mot[:, oc, :],
                                      (me if oc % 2 == 0 else mf)[:], AF.Identity,
                                      scale=W["vms3"][:, oc:oc + 1])

            mm_m3, ac_mo = pipe(b, H, 2, m3grp, m3evac, "scalar")
            mm_w3_used = mm_m3

            pq = (lambda _pf=ev_pfree_prev, _am=ac_mo: (lambda tensor: (
                tensor.wait_ge(b.sem("ac"), _am), b.w(tensor, "ev", _pf))))()
            ps = (lambda _ea=ev_apply_prev, _am=ac_mo:
                  (lambda scalar: (scalar.wait_ge(b.sem("ac"), _am),
                                   b.w(scalar, "ev", _ea))))()
            mm_bco, ev_pfree = ln_block(b, mot, H, sq5, me, mf, me, mf,
                                        ones_col, ones_row, sm, float(D),
                                        sq_on_vector=False,
                                        pre_stats_wait=pq, pre_sq_wait=ps)
            ev_pfree_prev = ev_pfree

            @b.block.vector
            def _(vector, _mb=mm_bco, _xl=xl_g, _st=st_o_prev):
                vector.wait_ge(b.sem("mm"), _mb)
                vector.wait_ge(b.sem("xl"), _xl)
                b.w(vector, "so", _st)
                for c in range(H):
                    vector.tensor_mul(sq5[:, 0, :], mot[:, c, :], me[:])
                    vector.tensor_sub(sq5[:, 1, :], sq5[:, 0, :], mf[:])
                    i = vector.tensor_add(mot[:, c, :], sq5[:, 1, :], x1g[:, c, :])
                    b.inc(i, "ev")
            ev_apply_prev = b.v("ev")
            ev_x1_used_prev = b.v("ev")
            ev_o = b.v("ev")

            @b.block.scalar
            def _(scalar, _t0=t0, _e=ev_o):
                scalar.wait_ge(b.sem("ev"), _e)
                i = scalar.dma_start(out=xout[:, :, _t0:_t0 + TG]
                                     .rearrange("c p t -> p c t"), in_=mot[:])
                b.inc(i, "so", 16)
            st_o_prev = b.v("so")

        @b.block.scalar
        def _(scalar):
            scalar.wait_ge(b.sem("so"), b.v("so"))
        b.barrier()



_CACHED = {}


def get_nc(tap=None):
    if tap not in _CACHED:
        _CACHED[tap] = build_program(tap)
    return _CACHED[tap]


def kernel(**inputs):
    nc = get_nc()
    in_maps = [prep_core_inputs(inputs, k) for k in range(NCORES)]
    res = run_bass_kernel_spmd(nc, in_maps, core_ids=list(range(NCORES)))
    outs = []
    for k in range(NCORES):
        o = np.asarray(res.results[k]["xout"])
        o = o.reshape(H, 128, BLOC, S).transpose(2, 3, 0, 1).reshape(BLOC, S, D)
        outs.append(o)
    return np.concatenate(outs, axis=0).astype(np.float32)


# version 05
# speedup vs baseline: 1.0493x; 1.0493x over previous
import numpy as np
import ml_dtypes
from contextlib import ExitStack

import concourse.bass as bass
import concourse.mybir as mybir
from concourse.bass_utils import run_bass_kernel_spmd

F32 = mybir.dt.float32
BF16 = mybir.dt.bfloat16
AF = mybir.ActivationFunctionType
ALU = mybir.AluOpType

H = 8
HD = 128
D = 1024
FFND = 4096
S = 2048
BLOC = 2
T = BLOC * S
TG = 512
NG = T // TG
EPS = 1e-5
NCORES = 8

BF16NP = ml_dtypes.bfloat16



def _perm_deint():
    return np.concatenate([np.arange(0, HD, 2), np.arange(1, HD, 2)])


def _xpos_tables():
    SCALE_BASE = 512
    scale_vec = (np.arange(0, HD, 2) + 0.4 * HD) / (1.4 * HD)
    power = (np.arange(S, dtype=np.float64) / SCALE_BASE)[:, None]
    scale = scale_vec[None, :] ** power
    half = HD // 2
    inv_freq = 1.0 / 10000.0 ** (np.arange(half, dtype=np.float64) / half)
    ang = np.arange(S, dtype=np.float64)[:, None] * inv_freq[None, :]
    sin, cos = np.sin(ang), np.cos(ang)

    def dupT(m):
        return np.concatenate([m.T, m.T], axis=0)

    cosq = dupT(cos * scale)
    sinq = dupT(sin * scale)
    cosk = dupT(cos / scale)
    sink = dupT(sin / scale)
    sinq[0:64] *= -1.0
    sink[0:64] *= -1.0
    return (cosq.astype(np.float32), sinq.astype(np.float32),
            cosk.astype(np.float32), sink.astype(np.float32))


def _decay_consts():
    gam = 1.0 - np.exp(np.linspace(np.log(1.0 / 32), np.log(1.0 / 512), H))
    j = np.arange(TG, dtype=np.float64)
    gdec = np.broadcast_to(gam[:, None, None] ** j[None, None, :],
                           (H, 128, TG)).astype(np.float32)
    p = np.arange(128, dtype=np.float64)
    mi = np.arange(16, dtype=np.float64)
    expo = 128.0 * (mi[None, None, :] - 3.0) - p[:, None, None]
    dvec = (gam[None, :, None] ** expo).astype(np.float32)
    dd = np.arange(4)[:, None, None]
    pp = np.arange(128)[None, :, None]
    jj = np.arange(TG)[None, None, :]
    masks = (jj >= 128 * dd + pp).astype(BF16NP)
    return gdec, dvec, masks


def _lhsT_slabs(w):
    I, O = w.shape
    return np.ascontiguousarray(
        w.reshape(I // 128, 128, O // 128, 128).transpose(2, 1, 0, 3))


def _pieces(slabs, k):
    NO, _, NI, _ = slabs.shape
    return np.ascontiguousarray(
        slabs.reshape(NO // k, k, 128, NI, 128).transpose(0, 2, 1, 3, 4))


def _vec_pc(v):
    n = v.shape[0] // 128
    return np.ascontiguousarray(v.reshape(n, 128).T)


def prep_core_inputs(inputs, core):
    f = lambda k: np.asarray(inputs[k], dtype=np.float32)
    X = f("L0_attn_X")
    xc = X[2 * core:2 * core + 2]
    xin = np.ascontiguousarray(
        xc.reshape(BLOC, S, H, 128).transpose(2, 3, 0, 1).reshape(H, 128, T))

    perm = _perm_deint()
    m = {"xin": xin}
    for i in range(2):
        WQ = f("L0_attn_WQ")[i][:, :, perm]
        WK = f("L0_attn_WK")[i][:, :, perm]
        WV = f("L0_attn_WV")[i]
        m[f"wq{i}"] = _lhsT_slabs(WQ.transpose(1, 0, 2).reshape(D, H * HD))
        m[f"wk{i}"] = _lhsT_slabs(WK.transpose(1, 0, 2).reshape(D, H * HD))
        m[f"wv{i}"] = np.ascontiguousarray(
            WV.transpose(1, 0, 2).reshape(D, H * HD).reshape(8, 128, 1024))
        m[f"wg{i}"] = _pieces(_lhsT_slabs(f("L0_attn_WG")[i]), 4)
        m[f"wo{i}"] = _pieces(_lhsT_slabs(f("L0_attn_WO")[i]), 4)
        m[f"fw1{i}"] = _pieces(_lhsT_slabs(f("L0_attn_ffn_w1")[i]), 4)
        m[f"fw2{i}"] = _lhsT_slabs(f("L0_attn_ffn_w2")[i])
        m[f"ln1w{i}"] = _vec_pc(f("L0_attn_ln1_w")[i])
        m[f"ln1b{i}"] = _vec_pc(f("L0_attn_ln1_b")[i])
        m[f"ln2w{i}"] = _vec_pc(f("L0_attn_ln2_w")[i])
        m[f"ln2b{i}"] = _vec_pc(f("L0_attn_ln2_b")[i])
        m[f"gnw{i}"] = _vec_pc(f("L0_attn_gn_w")[i])
        m[f"gnb{i}"] = _vec_pc(f("L0_attn_gn_b")[i])
        m[f"fb1{i}"] = _vec_pc(f("L0_attn_ffn_b1")[i])
        m[f"fb2{i}"] = _vec_pc(f("L0_attn_ffn_b2")[i])

    m["m1"] = _pieces(_lhsT_slabs(np.ascontiguousarray(f("L0_ffn_w1_w").T)), 2)
    m["m2"] = _pieces(_lhsT_slabs(np.ascontiguousarray(f("L0_ffn_w2_w").T)), 2)
    m["mh1"] = _lhsT_slabs(np.ascontiguousarray(f("L0_ffn_h1_w").T))
    m["mh2"] = _lhsT_slabs(np.ascontiguousarray(f("L0_ffn_h2_w").T))
    m["m3"] = _lhsT_slabs(np.ascontiguousarray(f("L0_ffn_w3_w").T))
    m["ms1"] = _vec_pc(f("L0_ffn_w1_s"))
    m["ms2"] = _vec_pc(f("L0_ffn_w2_s"))
    m["msh1"] = _vec_pc(f("L0_ffn_h1_s"))
    m["msh2"] = _vec_pc(f("L0_ffn_h2_s"))
    m["ms3"] = _vec_pc(f("L0_ffn_w3_s"))

    m["cq"], m["sq_"], m["ck"], m["sk_"] = _xpos_tables()
    m["gdec"], m["dvec"], m["masks"] = _decay_consts()
    return m



class B:
    def __init__(self, nc, block, stack):
        self.nc = nc
        self.block = block
        self.stack = stack
        self.sems = {}
        self.cnt = {}

    def mksem(self, *names):
        for n in names:
            self.sems[n] = self.stack.enter_context(self.nc.semaphore(n))
            self.cnt[n] = 0

    def inc(self, inst, name, by=1):
        inst.then_inc(self.sems[name], by)
        self.cnt[name] += by
        return self.cnt[name]

    def v(self, name):
        return self.cnt[name]

    def sem(self, name):
        return self.sems[name]

    def w(self, eng, name, val):
        if val and val > 0:
            eng.wait_ge(self.sems[name], val)

    def barrier(self, engines=("sync", "tensor", "vector", "scalar")):
        tgt = self.cnt["bar"] + len(engines)
        for e in engines:
            @getattr(self.block, e)
            def _(eng):
                eng.drain()
                eng.sem_inc(self.sems["bar"], 1)
        self.cnt["bar"] = tgt
        for e in engines:
            @getattr(self.block, e)
            def _(eng):
                eng.wait_ge(self.sems["bar"], tgt)


def pipe(b, n, nbanks, emit_group, emit_evac, evac_eng="scalar",
         evac_extra_wait=None):
    cn = {"scalar": "ac", "vector": "ev"}[evac_eng]
    mm_base = b.v("mm")
    ev_base = b.v(cn)

    @b.block.tensor
    def _(tensor):
        for gi in range(n):
            if gi >= nbanks:
                tensor.wait_ge(b.sem(cn), ev_base + gi - nbanks + 1)
            i = emit_group(tensor, gi)
            b.inc(i, "mm")

    @getattr(b.block, evac_eng)
    def _(eng):
        for gi in range(n):
            eng.wait_ge(b.sem("mm"), mm_base + gi + 1)
            if evac_extra_wait:
                evac_extra_wait(eng, gi)
            i = emit_evac(eng, gi)
            b.inc(i, cn)

    return b.v("mm"), b.v(cn)


def ln_smalls(b, sm, pS1, pS2, denom, mm_stats):
    @b.block.vector
    def _(vector):
        vector.wait_ge(b.sem("mm"), mm_stats)
        vector.tensor_scalar_mul(sm["mu"][:], pS1[0:1, :], 1.0 / denom)
        vector.tensor_scalar_mul(sm["aux"][:], pS2[0:1, :], 1.0 / denom)
        vector.tensor_mul(sm["m2"][:], sm["mu"][:], sm["mu"][:])
        i = vector.tensor_sub(sm["var"][:], sm["aux"][:], sm["m2"][:])
        b.inc(i, "ev")
    ev_small = b.v("ev")

    @b.block.scalar
    def _(scalar):
        scalar.wait_ge(b.sem("ev"), ev_small)
        scalar.activation(sm["aux"][:], sm["var"][:], AF.Ln, bias=sm["eps"][0:1, 0:1])
        i = scalar.activation(sm["rstd"][:], sm["aux"][:], AF.Exp,
                              scale=sm["nh"][0:1, 0:1])
        b.inc(i, "ac")
    ac_rstd = b.v("ac")

    @b.block.vector
    def _(vector):
        vector.wait_ge(b.sem("ac"), ac_rstd)
        i = vector.tensor_mul(sm["mr"][:], sm["mu"][:], sm["rstd"][:])
        b.inc(i, "ev")
    return ev_small


def ln_block(b, src_view, nchunk, sqbuf, pS1, pS2, pR, pMR, ones_col, ones_row,
             sm, denom, sq_on_vector=True, pre_stats_wait=None, pre_sq_wait=None,
             ones_col_b=None):
    if sq_on_vector:
        @b.block.vector
        def _(vector):
            if pre_sq_wait:
                pre_sq_wait(vector)
            i = vector.tensor_mul(sqbuf[:], src_view[:], src_view[:])
            b.inc(i, "ev")
        ev_sq = b.v("ev")

        @b.block.tensor
        def _(tensor):
            if pre_stats_wait:
                pre_stats_wait(tensor)
            for c in range(nchunk):
                tensor.matmul(pS1[0:1, :], ones_col[:], src_view[:, c, :],
                              start=(c == 0), stop=(c == nchunk - 1))
            tensor.wait_ge(b.sem("ev"), ev_sq)
            for c in range(nchunk):
                i = tensor.matmul(pS2[0:1, :], ones_col[:], sqbuf[:, c, :],
                                  start=(c == 0), stop=(c == nchunk - 1))
            b.inc(i, "mm")
    else:
        ac_base = b.v("ac")
        mm_base = b.v("mm")

        @b.block.scalar
        def _(scalar):
            if pre_sq_wait:
                pre_sq_wait(scalar)
            for c in range(nchunk):
                if c >= 2:
                    scalar.wait_ge(b.sem("mm"), mm_base + c - 1)
                i = scalar.activation(sqbuf[:, c % 2, :], src_view[:, c, :],
                                      AF.Square)
                b.inc(i, "ac")

        @b.block.tensor
        def _(tensor):
            if pre_stats_wait:
                pre_stats_wait(tensor)
            for c in range(nchunk):
                tensor.matmul(pS1[0:1, :], ones_col[:], src_view[:, c, :],
                              start=(c == 0), stop=(c == nchunk - 1))
            oc2 = ones_col_b if ones_col_b is not None else ones_col
            for c in range(nchunk):
                tensor.wait_ge(b.sem("ac"), ac_base + c + 1)
                i = tensor.matmul(pS2[0:1, :], oc2[:], sqbuf[:, c % 2, :],
                                  start=(c == 0), stop=(c == nchunk - 1))
                b.inc(i, "mm")
    mm_stats = b.v("mm")

    ev_pfree = ln_smalls(b, sm, pS1, pS2, denom, mm_stats)
    ev_mr = b.v("ev")

    @b.block.tensor
    def _(tensor):
        tensor.wait_ge(b.sem("ev"), ev_mr)
        tensor.matmul(pR[:], ones_row[:], sm["rstd"][0:1, :], start=True, stop=True)
        i = tensor.matmul(pMR[:], ones_row[:], sm["mr"][0:1, :], start=True, stop=True)
        b.inc(i, "mm")
    return b.v("mm"), ev_pfree


def emit_conversions(b, jobs, cvb):
    nq = [0]

    @b.block.gpsimd
    def _(gpsimd):
        for src, dst, key, N, M in jobs:
            nsub = (M + 2047) // 2048
            for n in range(N):
                for s_ in range(nsub):
                    w = min(2048, M - 2048 * s_)
                    i = nq[0]
                    par = i % 2
                    if i >= 2:
                        gpsimd.wait_ge(b.sem("cv"), 16 * (i - 1))
                    ii = gpsimd.dma_start(out=cvb[:, par, 0:w],
                                          in_=src[n][:, 2048 * s_:2048 * s_ + w])
                    b.inc(ii, "cvl", 16)
                    gpsimd.wait_ge(b.sem("cvl"), 16 * (i + 1))
                    ii = gpsimd.dma_start(out=dst[n][:, 2048 * s_:2048 * s_ + w],
                                          in_=cvb[:, par, 0:w])
                    b.inc(ii, "cv", 16)
                    nq[0] += 1
            b.cv_marks[key] = b.v("cv")



def build_program(tap=None):
    nc = bass.Bass("TRN2", target_bir_lowering=False)
    stack = ExitStack()

    def din(name, shape, dt=F32):
        return nc.dram_tensor(name, list(shape), dt, kind="ExternalInput").ap()

    xin = din("xin", [H, 128, T])
    WQd = [din(f"wq{i}", [8, 128, 8, 128]) for i in range(2)]
    WKd = [din(f"wk{i}", [8, 128, 8, 128]) for i in range(2)]
    WVd = [din(f"wv{i}", [8, 128, 1024]) for i in range(2)]
    WGd = [din(f"wg{i}", [2, 128, 4, 8, 128]) for i in range(2)]
    WOd = [din(f"wo{i}", [2, 128, 4, 8, 128]) for i in range(2)]
    FW1 = [din(f"fw1{i}", [8, 128, 4, 8, 128]) for i in range(2)]
    FW2 = [din(f"fw2{i}", [8, 128, 32, 128]) for i in range(2)]
    V8 = lambda nm: [din(f"{nm}{i}", [128, 8]) for i in range(2)]
    LN1W, LN1B, LN2W, LN2B = V8("ln1w"), V8("ln1b"), V8("ln2w"), V8("ln2b")
    GNW, GNB, FB2 = V8("gnw"), V8("gnb"), V8("fb2")
    FB1 = [din(f"fb1{i}", [128, 32]) for i in range(2)]
    M1 = din("m1", [16, 128, 2, 8, 128])
    M2 = din("m2", [16, 128, 2, 8, 128])
    MH1 = din("mh1", [32, 128, 32, 128])
    MH2 = din("mh2", [32, 128, 32, 128])
    M3 = din("m3", [8, 128, 32, 128])
    MS1, MS2 = din("ms1", [128, 32]), din("ms2", [128, 32])
    MSH1, MSH2 = din("msh1", [128, 32]), din("msh2", [128, 32])
    MS3 = din("ms3", [128, 8])
    CQ, SQ_ = din("cq", [128, S]), din("sq_", [128, S])
    CK, SK_ = din("ck", [128, S]), din("sk_", [128, S])
    GDEC = din("gdec", [H, 128, TG])
    DVEC = din("dvec", [128, 8, 16])
    MASKS = din("masks", [4, 128, TG], BF16)

    xout = nc.dram_tensor("xout", [H, 128, T], F32, kind="ExternalOutput").ap()
    skind = "ExternalOutput" if tap else "Internal"
    CVshapes = {
        "fw1_0": [8, 128, 4, 8, 128], "fw2_0": [8, 128, 32, 128],
        "fw1_1": [8, 128, 4, 8, 128], "fw2_1": [8, 128, 32, 128],
        "m1": [16, 128, 2, 8, 128], "m2": [16, 128, 2, 8, 128],
        "mh1": [32, 128, 32, 128], "mh2": [32, 128, 32, 128],
        "m3": [8, 128, 32, 128],
    }
    CV = {k: nc.dram_tensor("cv_" + k, s, BF16, kind="Internal").ap()
          for k, s in CVshapes.items()}
    global HTAP
    HTAP = nc.dram_tensor("htap", [NG, 128, 32, TG], BF16, kind=skind).ap() \
        if tap else None
    resid = nc.dram_tensor("resid", [H, 128, T], F32, kind=skind).ap()
    xnb = nc.dram_tensor("xnb", [H, 128, T], BF16, kind=skind).ap()
    ybuf = nc.dram_tensor("ybuf", [H, 128, T], BF16, kind=skind).ap()

    sb = lambda name, shape, dt=F32: stack.enter_context(
        nc.sbuf_tensor(name, list(shape), dt))
    ones_col = sb("ones_col", [128, 1])
    ones_colb = sb("ones_colb", [128, 1], BF16)
    ones_row = sb("ones_row", [1, 128])
    vln1w = [sb(f"vln1w{i}", [128, 8]) for i in range(2)]
    vln1b = [sb(f"vln1b{i}", [128, 8]) for i in range(2)]
    vln2w = [sb(f"vln2w{i}", [128, 8]) for i in range(2)]
    vln2b = [sb(f"vln2b{i}", [128, 8]) for i in range(2)]
    vgnw = [sb(f"vgnw{i}", [128, 8]) for i in range(2)]
    vgnb = [sb(f"vgnb{i}", [128, 8]) for i in range(2)]
    vfb1 = [sb(f"vfb1{i}", [128, 32]) for i in range(2)]
    vfb2 = [sb(f"vfb2{i}", [128, 8]) for i in range(2)]
    vms1, vms2 = sb("vms1", [128, 32]), sb("vms2", [128, 32])
    vmsh1, vmsh2 = sb("vmsh1", [128, 32]), sb("vmsh2", [128, 32])
    vms3 = sb("vms3", [128, 8])
    vdvec = sb("vdvec", [128, 8, 16])
    sm = {k: sb("sm_" + k, [1, TG]) for k in ("mu", "aux", "m2", "var", "rstd", "mr")}
    sm["eps"] = sb("sm_eps", [1, 1])
    sm["nh"] = sb("sm_nh", [1, 1])
    cvb = sb("cvb", [128, 2, 2048], BF16)

    with nc.Block() as block:
        b = B(nc, block, stack)
        b.ones_colb = ones_colb
        b.mksem("cl", "xl", "wl", "gl", "mm", "ev", "ac", "st", "so", "bar",
                "cvl", "cv")

        @block.sync
        def _(sync):
            pairs = [(MS1, vms1), (MS2, vms2), (MSH1, vmsh1), (MSH2, vmsh2),
                     (MS3, vms3), (DVEC, vdvec)]
            for i in range(2):
                pairs += [(LN1W[i], vln1w[i]), (LN1B[i], vln1b[i]),
                          (LN2W[i], vln2w[i]), (LN2B[i], vln2b[i]),
                          (GNW[i], vgnw[i]), (GNB[i], vgnb[i]),
                          (FB1[i], vfb1[i]), (FB2[i], vfb2[i])]
            for src, dst in pairs:
                b.inc(sync.dma_start(out=dst[:], in_=src), "cl", 16)
        cl_consts = b.v("cl")

        @block.vector
        def _(vector):
            vector.memset(ones_col[:], 1.0)
            vector.memset(ones_colb[:], 1.0)
            vector.memset(ones_row[:], 1.0)
            vector.memset(sm["eps"][:], EPS)
            i = vector.memset(sm["nh"][:], -0.5)
            b.inc(i, "ev")
        ev_ones = b.v("ev")

        for e in ("tensor", "scalar"):
            @getattr(block, e)
            def _(eng):
                eng.wait_ge(b.sem("ev"), ev_ones)
                eng.wait_ge(b.sem("cl"), cl_consts)

        b.cv_marks = {}
        cv_jobs = []

        def flat(ap, N, M):
            return ap.rearrange("a p " + " ".join(
                f"d{i}" for i in range(len(ap.shape) - 2)) + " -> a p (" + " ".join(
                f"d{i}" for i in range(len(ap.shape) - 2)) + ")")

        for key, fsrc in (("fw1_0", FW1[0]), ("fw2_0", FW2[0]),
                          ("fw1_1", FW1[1]), ("fw2_1", FW2[1]),
                          ("m1", M1), ("m2", M2), ("mh1", MH1), ("mh2", MH2),
                          ("m3", M3)):
            shp = CVshapes[key]
            N = shp[0]
            M = int(np.prod(shp[2:]))
            cv_jobs.append((flat(fsrc, N, M), flat(CV[key], N, M), key, N, M))

        with nc.named_scope("p0"):
            _p0(b, nc, xin, resid, xnb, vln1w[0], vln1b[0], ones_col, ones_row, sm)
            emit_conversions(b, cv_jobs, cvb)
        if tap == "p0":
            _finish(b)
            stack.close()
            return nc

        consts = dict(CQ=CQ, SQ_=SQ_, CK=CK, SK_=SK_, GDEC=GDEC, MASKS=MASKS,
                      vdvec=vdvec, resid=resid, xnb=xnb, ybuf=ybuf)
        for li in range(2):
            Wl = dict(wq=WQd[li], wk=WKd[li], wv=WVd[li], wg=WGd[li], wo=WOd[li],
                      fw1=CV[f"fw1_{li}"], fw2=CV[f"fw2_{li}"],
                      cv_fw1=f"fw1_{li}", cv_fw2=f"fw2_{li}",
                      vln2w=vln2w[li], vln2b=vln2b[li],
                      vgnw=vgnw[li], vgnb=vgnb[li],
                      vfb1=vfb1[li], vfb2=vfb2[li],
                      vln1w_next=vln1w[1], vln1b_next=vln1b[1])
            with nc.named_scope(f"p1_{li}"):
                _p1_retention(b, li, nc, Wl, consts, ones_col, ones_row, sm)
            if tap == f"r{li}":
                _finish(b)
                stack.close()
                return nc
            with nc.named_scope(f"p3_{li}"):
                _p3_woffn(b, li, nc, Wl, consts, ones_col, ones_row, sm)
            if tap == f"l{li}":
                _finish(b)
                stack.close()
                return nc

        with nc.named_scope("p4"):
            _p4_mid(b, nc, xin, resid, xnb, ones_col, ones_row, sm)
        if tap == "p4":
            _finish(b)
            stack.close()
            return nc

        Wm = dict(M1=CV["m1"], M2=CV["m2"], MH1=CV["mh1"], MH2=CV["mh2"],
                  M3=CV["m3"],
                  vms1=vms1, vms2=vms2, vmsh1=vmsh1, vmsh2=vmsh2, vms3=vms3,
                  resid=resid, xnb=xnb, xout=xout)
        with nc.named_scope("p5"):
            _p5_mlp(b, nc, Wm, ones_col, ones_row, sm)
        _finish(b)

    stack.close()
    return nc


def _finish(b):
    for e in ("scalar", "sync"):
        @getattr(b.block, e)
        def _(eng):
            eng.wait_ge(b.sem("st"), b.v("st"))
            eng.wait_ge(b.sem("so"), b.v("so"))



def _p0(b, nc, xin, resid, xnb, vw, vb, ones_col, ones_row, sm):
    with ExitStack() as ps_:
        e_ = ps_.enter_context
        xg2 = e_(nc.sbuf_tensor("xg2", [128, 2, H, TG], F32))
        h0t = e_(nc.sbuf_tensor("h0t", [128, H, TG], F32))
        sqg = e_(nc.sbuf_tensor("sqg", [128, H, TG], F32))
        xnt = e_(nc.sbuf_tensor("xnt", [128, H, TG], BF16))
        pS1 = e_(nc.psum_tensor("pS1", [128, 512], F32))
        pS2 = e_(nc.psum_tensor("pS2", [128, 512], F32))
        pR = e_(nc.psum_tensor("pR", [128, 512], F32))
        pMR = e_(nc.psum_tensor("pMR", [128, 512], F32))

        ev_xg_free = {}
        ev_pfree_prev = 0
        st_h0_prev = 0
        st_xn_prev = 0
        ac_xn_prev = 0
        for g in range(NG):
            t0 = g * TG

            @b.block.sync
            def _(sync, _g=g, _t0=t0):
                b.w(sync, "ev", ev_xg_free.get(_g - 2))
                i = sync.dma_start(out=xg2[:, _g % 2],
                                   in_=xin[:, :, _t0:_t0 + TG].rearrange("c p t -> p c t"))
                b.inc(i, "xl", 16)
            xl_g = b.v("xl")
            xg = xg2[:, g % 2]

            pq1 = (lambda _pf=ev_pfree_prev, _xl=xl_g: (lambda tensor: (
                tensor.wait_ge(b.sem("xl"), _xl), b.w(tensor, "ev", _pf))))()
            ps1 = (lambda _xl=xl_g, _ac=ac_xn_prev: (lambda vector: (
                vector.wait_ge(b.sem("xl"), _xl), b.w(vector, "ac", _ac))))()
            mm_bc1, ev_pfree = ln_block(b, xg, H, sqg, pS1, pS2, pR, pMR,
                                        ones_col, ones_row, sm, float(D),
                                        pre_stats_wait=pq1, pre_sq_wait=ps1)

            @b.block.vector
            def _(vector, _mb=mm_bc1, _st=st_h0_prev):
                vector.wait_ge(b.sem("mm"), _mb)
                b.w(vector, "st", _st)
                for c in range(H):
                    vector.tensor_mul(h0t[:, c, :], xg[:, c, :], pR[:])
                    i = vector.tensor_sub(h0t[:, c, :], h0t[:, c, :], pMR[:])
                b.inc(i, "ev")
            ev_xg_free[g] = b.v("ev")
            ev_apply1 = b.v("ev")

            pq2 = (lambda _pf=ev_pfree, _e1=ev_apply1: (lambda tensor: (
                tensor.wait_ge(b.sem("ev"), _e1), b.w(tensor, "ev", _pf))))()
            mm_bc2, ev_pfree2 = ln_block(b, h0t, H, sqg, pS1, pS2, pR, pMR,
                                         ones_col, ones_row, sm, float(D),
                                         pre_stats_wait=pq2)
            ev_pfree_prev = ev_pfree2

            @b.block.vector
            def _(vector, _mb=mm_bc2):
                vector.wait_ge(b.sem("mm"), _mb)
                for c in range(H):
                    vector.tensor_mul(sqg[:, c, :], h0t[:, c, :], pR[:])
                    i = vector.tensor_sub(sqg[:, c, :], sqg[:, c, :], pMR[:])
                b.inc(i, "ev")
            ev_u = b.v("ev")

            @b.block.scalar
            def _(scalar, _e=ev_u, _st=st_xn_prev):
                scalar.wait_ge(b.sem("ev"), _e)
                b.w(scalar, "st", _st)
                for c in range(H):
                    i = scalar.activation(xnt[:, c, :], sqg[:, c, :], AF.Identity,
                                          bias=vb[:, c:c + 1], scale=vw[:, c:c + 1])
                b.inc(i, "ac")
            ac_xn_prev = b.v("ac")
            ac_xn = b.v("ac")

            @b.block.scalar
            def _(scalar, _t0=t0, _ac=ac_xn):
                i = scalar.dma_start(out=resid[:, :, _t0:_t0 + TG]
                                     .rearrange("c p t -> p c t"), in_=h0t[:])
                b.inc(i, "st", 16)
                scalar.wait_ge(b.sem("ac"), _ac)
                i = scalar.dma_start(out=xnb[:, :, _t0:_t0 + TG]
                                     .rearrange("c p t -> p c t"), in_=xnt[:])
                b.inc(i, "st", 16)
            st_h0_prev = b.v("st") - 16
            st_xn_prev = b.v("st")

        @b.block.scalar
        def _(scalar):
            scalar.wait_ge(b.sem("st"), b.v("st"))
        b.barrier()



def _p1_retention(b, li, nc, W, C, ones_col, ones_row, sm):
    with ExitStack() as ps_:
        e_ = ps_.enter_context
        tcq = e_(nc.sbuf_tensor(f"tcq{li}", [128, S], F32))
        tsq = e_(nc.sbuf_tensor(f"tsq{li}", [128, S], F32))
        tck = e_(nc.sbuf_tensor(f"tck{li}", [128, S], F32))
        tsk = e_(nc.sbuf_tensor(f"tsk{li}", [128, S], F32))
        tmsk = e_(nc.sbuf_tensor(f"tmsk{li}", [128, 4, TG], BF16))
        tgd = e_(nc.sbuf_tensor(f"tgd{li}", [128, 2, TG], F32))
        xnh = e_(nc.sbuf_tensor(f"xnh{li}", [128, H, S], BF16))
        wvf = e_(nc.sbuf_tensor(f"wvf{li}", [128, 1024], F32))
        wvb = e_(nc.sbuf_tensor(f"wvb{li}", [128, 8, 1024], BF16))
        wqkf = e_(nc.sbuf_tensor(f"wqkf{li}", [128, 2, 8, 128], F32))
        wqkb = e_(nc.sbuf_tensor(f"wqkb{li}", [128, 2, 2, 8, 128], BF16))
        vsb = e_(nc.sbuf_tensor(f"vsb{li}", [128, 16, 256], BF16))
        qsb = e_(nc.sbuf_tensor(f"qsb{li}", [128, S], BF16))
        ksb = e_(nc.sbuf_tensor(f"ksb{li}", [128, S], BF16))
        scsb = e_(nc.sbuf_tensor(f"scsb{li}", [128, 16, TG], BF16))
        xt1 = e_(nc.sbuf_tensor(f"xt1{li}", [128, TG], F32))
        xt2 = e_(nc.sbuf_tensor(f"xt2{li}", [128, TG], F32))
        ytl = e_(nc.sbuf_tensor(f"ytl{li}", [128, TG], F32))
        sq1 = e_(nc.sbuf_tensor(f"sq1{li}", [128, TG], F32))
        ygn = e_(nc.sbuf_tensor(f"ygn{li}", [128, S], BF16))
        pq0 = e_(nc.psum_tensor(f"pq0{li}", [128, 512], F32))
        pq1 = e_(nc.psum_tensor(f"pq1{li}", [128, 512], F32))
        pq2 = e_(nc.psum_tensor(f"pq2{li}", [128, 512], F32))
        pq3 = e_(nc.psum_tensor(f"pq3{li}", [128, 512], F32))
        psc0 = e_(nc.psum_tensor(f"psc0{li}", [128, 512], F32))
        psc1 = e_(nc.psum_tensor(f"psc1{li}", [128, 512], F32))
        pyh = e_(nc.psum_tensor(f"pyh{li}", [128, 512], F32))
        pst = e_(nc.psum_tensor(f"pst{li}", [128, 512], F32))
        pqs = [pq0, pq1, pq2, pq3]
        pscs = [psc0, psc1]

        @b.block.sync
        def _(sync):
            for src, dst in ((C["CQ"], tcq), (C["SQ_"], tsq),
                             (C["CK"], tck), (C["SK_"], tsk)):
                b.inc(sync.dma_start(out=dst[:], in_=src), "cl", 16)
            i = sync.dma_start(out=tmsk[:], in_=C["MASKS"].rearrange("d p t -> p d t"))
            b.inc(i, "cl", 16)
        cl_tab = b.v("cl")

        ev_wv_prev = 0
        for dc in range(8):
            @b.block.sync
            def _(sync, _dc=dc, _ev=ev_wv_prev):
                b.w(sync, "ev", _ev)
                i = sync.dma_start(out=wvf[:], in_=W["wv"][_dc])
                b.inc(i, "wl", 16)
            wl_v = b.v("wl")

            @b.block.vector
            def _(vector, _dc=dc, _wl=wl_v):
                vector.wait_ge(b.sem("wl"), _wl)
                i = vector.tensor_copy(wvb[:, _dc, :], wvf[:])
                b.inc(i, "ev")
            ev_wv_prev = b.v("ev")
        ev_wv = b.v("ev")

        mm_last_v_use = 0
        ev_gn_prev = 0
        ev_gn_slot = {0: 0, 1: 0}
        ev_qk_evac_prev = 0
        ev_wqkf_prev = 0
        st_ygn_prev = 0

        for bb in range(BLOC):
            tb0 = bb * S

            @b.block.sync
            def _(sync, _tb0=tb0):
                sync.wait_ge(b.sem("st"), b.v("st"))
                b.w(sync, "mm", b.v("mm"))
                i = sync.dma_start(out=xnh[:], in_=C["xnb"][:, :, _tb0:_tb0 + S]
                                   .rearrange("c p t -> p c t"))
                b.inc(i, "xl", 16)
            xl_xn = b.v("xl")

            for hg in range(4):
                def vgrp(tensor, tt, _hg=hg, _xl=xl_xn,
                         _eq=ev_qk_evac_prev, _eg=ev_gn_prev):
                    if tt == 0:
                        tensor.wait_ge(b.sem("xl"), _xl)
                        tensor.wait_ge(b.sem("ev"), ev_wv)
                        b.w(tensor, "ev", _eq)
                        b.w(tensor, "ev", _eg)
                    for dc in range(8):
                        i = tensor.matmul(pqs[tt % 4][:, 0:256],
                                          xnh[:, dc, tt * 128:(tt + 1) * 128],
                                          wvb[:, dc, 256 * _hg:256 * (_hg + 1)],
                                          start=(dc == 0), stop=(dc == 7))
                    return i

                def vevac(eng, tt, _mv=mm_last_v_use):
                    if tt == 0:
                        b.w(eng, "mm", _mv)
                    return eng.activation(vsb[:, tt, :], pqs[tt % 4][:, 0:256],
                                          AF.Copy)

                _, ac_v = pipe(b, 16, 4, vgrp, vevac, "scalar")

                for h in (2 * hg, 2 * hg + 1):
                    hcol = h - 2 * hg
                    par = h % 2

                    @b.block.sync
                    def _(sync, _par=par, _h=h, _ewf=ev_wqkf_prev,
                          _egs=ev_gn_slot[par]):
                        b.w(sync, "ev", _ewf)
                        b.inc(sync.dma_start(out=wqkf[:, 0], in_=W["wq"][_h]), "wl", 16)
                        b.inc(sync.dma_start(out=wqkf[:, 1], in_=W["wk"][_h]), "wl", 16)
                        b.w(sync, "ev", _egs)
                        b.inc(sync.dma_start(out=tgd[:, _par], in_=C["GDEC"][_h]),
                              "cl", 16)
                    wl_qk = b.v("wl")
                    cl_gd = b.v("cl")

                    @b.block.vector
                    def _(vector, _par=par, _wl=wl_qk):
                        vector.wait_ge(b.sem("wl"), _wl)
                        i = vector.tensor_copy(wqkb[:, _par], wqkf[:])
                        b.inc(i, "ev")
                    ev_qk_w = b.v("ev")
                    ev_wqkf_prev = ev_qk_w

                    for qk in range(2):
                        dst = qsb if qk == 0 else ksb
                        ctab = tcq if qk == 0 else tck
                        stab = tsq if qk == 0 else tsk

                        def qgrp(tensor, tt, _qk=qk, _par=par, _evw=ev_qk_w,
                                 _evfree=ev_qk_evac_prev, _evgn=ev_gn_prev,
                                 _acv=ac_v):
                            if tt == 0:
                                tensor.wait_ge(b.sem("ev"), _evw)
                                tensor.wait_ge(b.sem("cl"), cl_tab)
                                b.w(tensor, "ev", _evfree)
                                b.w(tensor, "ev", _evgn)
                                if _qk == 0:
                                    b.w(tensor, "ac", _acv)
                            for dc in range(8):
                                i = tensor.matmul(pqs[tt][:],
                                                  wqkb[:, _par, _qk, dc, :],
                                                  xnh[:, dc, tt * TG:(tt + 1) * TG],
                                                  start=(dc == 0), stop=(dc == 7))
                            return i

                        def qevac(eng, tt, _dst=dst, _ct=ctab, _st=stab):
                            sl = slice(tt * TG, (tt + 1) * TG)
                            ps = pqs[tt]
                            eng.tensor_mul(xt1[:], ps[:], _ct[:, sl])
                            eng.tensor_mul(xt2[0:64, :], ps[64:128, :], _st[0:64, sl])
                            eng.tensor_mul(xt2[64:128, :], ps[0:64, :], _st[64:128, sl])
                            return eng.tensor_add(_dst[:, sl], xt1[:], xt2[:])

                        pipe(b, 4, 4, qgrp, qevac, "vector")
                        ev_qk_evac_prev = b.v("ev")

                    for sc in range(4):
                        s0 = sc * TG
                        ntj = 4 * sc + 4

                        def sgrp(tensor, tj, _s0=s0, _ev=ev_qk_evac_prev):
                            if tj == 0:
                                tensor.wait_ge(b.sem("ev"), _ev)
                            return tensor.matmul(pscs[tj % 2][:],
                                                 ksb[:, tj * 128:(tj + 1) * 128],
                                                 qsb[:, _s0:_s0 + TG],
                                                 start=True, stop=True)

                        def sevac(eng, tj, _sc=sc, _h=h):
                            mi = 4 * _sc - tj + 3
                            return eng.activation(scsb[:, tj, :], pscs[tj % 2][:],
                                                  AF.Identity,
                                                  scale=C["vdvec"][:, _h, mi:mi + 1])

                        ac_sc_base = b.v("ac")
                        _, ac_sc = pipe(b, ntj, 2, sgrp, sevac, "scalar")

                        @b.block.vector
                        def _(vector, _sc=sc, _acb=ac_sc_base):
                            for d in range(4):
                                tj = 4 * _sc + d
                                vector.wait_ge(b.sem("ac"), _acb + tj + 1)
                                i = vector.tensor_mul(scsb[:, tj, :], scsb[:, tj, :],
                                                      tmsk[:, d, :])
                                b.inc(i, "ev")
                        ev_msk = b.v("ev")

                        @b.block.tensor
                        def _(tensor, _hcol=hcol, _ntj=ntj, _acs=ac_sc,
                              _evm=ev_msk, _evgn=ev_gn_prev):
                            tensor.wait_ge(b.sem("ac"), _acs)
                            tensor.wait_ge(b.sem("ev"), _evm)
                            b.w(tensor, "ev", _evgn)
                            for tj in range(_ntj):
                                i = tensor.matmul(
                                    pyh[:], vsb[:, tj, _hcol * 128:(_hcol + 1) * 128],
                                    scsb[:, tj, :],
                                    start=(tj == 0), stop=(tj == _ntj - 1))
                            b.inc(i, "mm")
                        mm_yh = b.v("mm")
                        mm_last_v_use = mm_yh

                        @b.block.vector
                        def _(vector, _par=par, _my=mm_yh, _cg=cl_gd):
                            vector.wait_ge(b.sem("mm"), _my)
                            vector.wait_ge(b.sem("cl"), _cg)
                            i = vector.tensor_mul(ytl[:], pyh[:], tgd[:, _par])
                            b.inc(i, "ev")
                        ev_yt = b.v("ev")

                        @b.block.scalar
                        def _(scalar, _e=ev_yt):
                            scalar.wait_ge(b.sem("ev"), _e)
                            i = scalar.activation(sq1[:], ytl[:], AF.Square)
                            b.inc(i, "ac")
                        ac_sq = b.v("ac")

                        @b.block.tensor
                        def _(tensor, _a=ac_sq, _evgn=ev_gn_prev):
                            tensor.wait_ge(b.sem("ac"), _a)
                            b.w(tensor, "ev", _evgn)
                            tensor.matmul(pst[0:1, :], ones_col[:], ytl[:],
                                          start=True, stop=True)
                            i = tensor.matmul(pq0[0:1, :], ones_col[:], sq1[:],
                                              start=True, stop=True)
                            b.inc(i, "mm")
                        mm_gstat = b.v("mm")

                        ln_smalls(b, sm, pst, pq0, float(HD), mm_gstat)
                        ev_mr = b.v("ev")

                        @b.block.tensor
                        def _(tensor, _em=ev_mr, _evgn=ev_gn_prev):
                            tensor.wait_ge(b.sem("ev"), _em)
                            b.w(tensor, "ev", _evgn)
                            tensor.matmul(pq1[:], ones_row[:], sm["rstd"][0:1, :],
                                          start=True, stop=True)
                            i = tensor.matmul(pq2[:], ones_row[:], sm["mr"][0:1, :],
                                              start=True, stop=True)
                            b.inc(i, "mm")
                        mm_gbc = b.v("mm")

                        @b.block.vector
                        def _(vector, _mb=mm_gbc):
                            vector.wait_ge(b.sem("mm"), _mb)
                            vector.tensor_mul(xt1[:], ytl[:], pq1[:])
                            i = vector.tensor_sub(xt1[:], xt1[:], pq2[:])
                            b.inc(i, "ev")
                        ev_gn_prev = b.v("ev")
                        ev_gn_slot[par] = b.v("ev")
                        ev_gna = b.v("ev")

                        @b.block.scalar
                        def _(scalar, _s0=s0, _h=h, _e=ev_gna, _st=st_ygn_prev):
                            scalar.wait_ge(b.sem("ev"), _e)
                            if _s0 == 0:
                                b.w(scalar, "st", _st)
                            i = scalar.activation(ygn[:, _s0:_s0 + TG], xt1[:],
                                                  AF.Identity,
                                                  bias=W["vgnb"][:, _h:_h + 1],
                                                  scale=W["vgnw"][:, _h:_h + 1])
                            b.inc(i, "ac")
                    ac_gn_all = b.v("ac")

                    @b.block.scalar
                    def _(scalar, _h=h, _tb0=tb0, _a=ac_gn_all):
                        scalar.wait_ge(b.sem("ac"), _a)
                        i = scalar.dma_start(out=C["ybuf"][_h][:, _tb0:_tb0 + S],
                                             in_=ygn[:])
                        b.inc(i, "st", 16)
                    st_ygn_prev = b.v("st")

        @b.block.scalar
        def _(scalar):
            scalar.wait_ge(b.sem("st"), b.v("st"))
        b.barrier()



def _p3_woffn(b, li, nc, W, C, ones_col, ones_row, sm):
    resid, xnb, ybuf = C["resid"], C["xnb"], C["ybuf"]
    last = (li == 1)
    with nc.sbuf_tensor(f"wgo{li}", [128, 2, 8, 8, 128], BF16) as wgo:
        with nc.sbuf_tensor(f"wgof{li}", [128, 4, 8, 128], F32) as wgof:
            ev_cv_prev = 0
            for k in range(2):
                wsrc = W["wg"] if k == 0 else W["wo"]
                for hf in range(2):
                    @b.block.sync
                    def _(sync, _hf=hf, _w=wsrc, _ev=ev_cv_prev):
                        b.w(sync, "ev", _ev)
                        i = sync.dma_start(out=wgof[:], in_=_w[_hf])
                        b.inc(i, "wl", 16)
                    wl_w = b.v("wl")

                    @b.block.vector
                    def _(vector, _k=k, _hf=hf, _wl=wl_w):
                        vector.wait_ge(b.sem("wl"), _wl)
                        i = vector.tensor_copy(wgo[:, _k, 4 * _hf:4 * _hf + 4],
                                               wgof[:])
                        b.inc(i, "ev")
                    ev_cv_prev = b.v("ev")
            ev_wgo = b.v("ev")

        with ExitStack() as ps_:
            e_ = ps_.enter_context
            xng2 = e_(nc.sbuf_tensor(f"xng2{li}", [128, 2, H, TG], BF16))
            ybg2 = e_(nc.sbuf_tensor(f"ybg2{li}", [128, 2, H, TG], BF16))
            xrg = e_(nc.sbuf_tensor(f"xrg{li}", [128, H, TG], F32))
            gg = e_(nc.sbuf_tensor(f"gg{li}", [128, H, TG], BF16))
            ysb3 = e_(nc.sbuf_tensor(f"ysb3{li}", [128, H, TG], F32))
            ynt = e_(nc.sbuf_tensor(f"ynt{li}", [128, H, TG], BF16))
            hh = e_(nc.sbuf_tensor(f"hh{li}", [128, 32, TG], BF16))
            sq3 = e_(nc.sbuf_tensor(f"sq3{li}", [128, 2, TG], BF16))
            ap3 = e_(nc.sbuf_tensor(f"ap3{li}", [128, 2, TG], F32))
            f1b = e_(nc.sbuf_tensor(f"f1b{li}", [128, 2, 4, 8, 128], BF16))
            f2b = e_(nc.sbuf_tensor(f"f2b{li}", [128, 2, 32, 128], BF16))
            pa = e_(nc.psum_tensor(f"pa{li}", [128, 512], F32))
            pb = e_(nc.psum_tensor(f"pb{li}", [128, 512], F32))
            pc = e_(nc.psum_tensor(f"pc{li}", [128, 512], F32))
            pd = e_(nc.psum_tensor(f"pd{li}", [128, 512], F32))
            pe = e_(nc.psum_tensor(f"pe{li}", [128, 512], F32))
            pf = e_(nc.psum_tensor(f"pf{li}", [128, 512], F32))
            pg = e_(nc.psum_tensor(f"pg{li}", [128, 512], F32))
            ph = e_(nc.psum_tensor(f"ph{li}", [128, 512], F32))

            stv = {"ev_pfree": 0, "ev_apply": 0, "st_resid": 0, "st_xnn": 0,
                   "ev_pg": {0: 0, 1: 0}, "mm_xn": {0: 0, 1: 0}, "ev_y": 0,
                   "ac_g": 0}
            mm_f1_piece = {0: 0, 1: 0}
            mm_f2_piece = {0: 0, 1: 0}

            def emit_loads(g):
                t0 = g * TG

                @b.block.sync
                def _(sync, _g=g, _t0=t0):
                    if _g == 0:
                        sync.wait_ge(b.sem("st"), b.v("st"))
                        sync.wait_ge(b.sem("ev"), ev_wgo)
                    b.w(sync, "mm", stv["mm_xn"][_g % 2])
                    i = sync.dma_start(out=xng2[:, _g % 2],
                                       in_=xnb[:, :, _t0:_t0 + TG]
                                       .rearrange("c p t -> p c t"))
                    b.inc(i, "xl", 16)
                    b.w(sync, "ev", stv["ev_pg"][_g % 2])
                    i = sync.dma_start(out=ybg2[:, _g % 2],
                                       in_=ybuf[:, :, _t0:_t0 + TG]
                                       .rearrange("c p t -> p c t"))
                    b.inc(i, "xl", 16)
                    b.w(sync, "ev", stv["ev_y"])
                    i = sync.dma_start(out=xrg[:],
                                       in_=resid[:, :, _t0:_t0 + TG]
                                       .rearrange("c p t -> p c t"))
                    b.inc(i, "xl", 16)
                return b.v("xl")

            def emit_G(g, xl_g):
                xng = xng2[:, g % 2]

                def ggrp(tensor, oc, _xl=xl_g, _ew=ev_wgo):
                    if oc == 0:
                        tensor.wait_ge(b.sem("xl"), _xl - 32)
                        tensor.wait_ge(b.sem("ev"), _ew)
                    for dc in range(8):
                        i = tensor.matmul((pe if oc % 2 == 0 else pf)[:],
                                          wgo[:, 0, oc, dc, :], xng[:, dc, :],
                                          start=(dc == 0), stop=(dc == 7))
                    return i

                def gevac(eng, oc):
                    return eng.activation(gg[:, oc, :],
                                          (pe if oc % 2 == 0 else pf)[:], AF.Silu)

                mm_g, ac_g = pipe(b, H, 2, ggrp, gevac, "scalar")
                stv["mm_xn"][g % 2] = mm_g
                stv["ac_g"] = ac_g

            xl_cur = emit_loads(0)
            emit_G(0, xl_cur)
            for g in range(NG):
                t0 = g * TG
                xl_g = xl_cur
                ybg = ybg2[:, g % 2]

                @b.block.vector
                def _(vector, _a=stv["ac_g"], _xl=xl_g):
                    vector.wait_ge(b.sem("ac"), _a)
                    vector.wait_ge(b.sem("xl"), _xl - 16)
                    i = vector.tensor_mul(gg[:], gg[:], ybg[:])
                    b.inc(i, "ev")
                stv["ev_pg"][g % 2] = b.v("ev")
                ev_pg = b.v("ev")

                def wgrp(tensor, oc, _ev=ev_pg, _xl=xl_g):
                    if oc == 0:
                        tensor.wait_ge(b.sem("ev"), _ev)
                        tensor.wait_ge(b.sem("xl"), _xl)
                    for dc in range(8):
                        i = tensor.matmul((pc if oc % 2 == 0 else pd)[:],
                                          wgo[:, 1, oc, dc, :], gg[:, dc, :],
                                          start=(dc == 0), stop=(dc == 7))
                    return i

                def wevac(eng, oc, _st=stv["st_resid"]):
                    if oc == 0:
                        b.w(eng, "st", _st)
                    return eng.tensor_add(ysb3[:, oc, :],
                                          (pc if oc % 2 == 0 else pd)[:],
                                          xrg[:, oc, :])

                _, ev_y = pipe(b, H, 2, wgrp, wevac, "vector")
                stv["ev_y"] = ev_y

                if g + 1 < NG:
                    xl_cur = emit_loads(g + 1)
                    emit_G(g + 1, xl_cur)

                pq = (lambda _pf=stv["ev_pfree"], _ey=ev_y: (lambda tensor: (
                    tensor.wait_ge(b.sem("ev"), _ey), b.w(tensor, "ev", _pf))))()
                ps = (lambda _ea=stv["ev_apply"], _ey=ev_y, _sx=stv["st_xnn"]:
                      (lambda scalar: (scalar.wait_ge(b.sem("ev"), _ey),
                                       b.w(scalar, "ev", _ea),
                                       b.w(scalar, "st", _sx))))()
                mm_bc, ev_pfree = ln_block(b, ysb3, H, sq3, pg, ph, pg, ph,
                                           ones_col, ones_row, sm, float(D),
                                           sq_on_vector=False,
                                           pre_stats_wait=pq, pre_sq_wait=ps,
                                           ones_col_b=b.ones_colb)
                stv["ev_pfree"] = ev_pfree

                ev_ap_base = b.v("ev")
                ac_aff_base = b.v("ac")

                @b.block.vector
                def _(vector, _mb=mm_bc, _ab=ac_aff_base):
                    vector.wait_ge(b.sem("mm"), _mb)
                    for c in range(H):
                        if c >= 2:
                            vector.wait_ge(b.sem("ac"), _ab + c - 1)
                        vector.tensor_mul(ap3[:, c % 2, :], ysb3[:, c, :], pg[:])
                        i = vector.tensor_sub(ap3[:, c % 2, :], ap3[:, c % 2, :],
                                              ph[:])
                        b.inc(i, "ev")
                stv["ev_apply"] = b.v("ev")

                @b.block.scalar
                def _(scalar, _ab=ev_ap_base, _sx=stv["st_xnn"]):
                    b.w(scalar, "st", _sx)
                    for c in range(H):
                        scalar.wait_ge(b.sem("ev"), _ab + c + 1)
                        i = scalar.activation(ynt[:, c, :], ap3[:, c % 2, :],
                                              AF.Identity,
                                              bias=W["vln2b"][:, c:c + 1],
                                              scale=W["vln2w"][:, c:c + 1])
                        b.inc(i, "ac")
                ac_yn = b.v("ac")

                mm_ffn1_base = b.v("mm")
                gl_f1 = {}
                for piece in range(8):
                    @b.block.sync
                    def _(sync, _p=piece, _g=g):
                        if _g == 0 and _p == 0:
                            sync.wait_ge(b.sem("cv"), b.cv_marks[W["cv_fw1"]])
                        key = _p % 2
                        b.w(sync, "mm", mm_f1_piece[key])
                        i = sync.dma_start(out=f1b[:, key], in_=W["fw1"][_p])
                        b.inc(i, "gl", 16)
                        mm_f1_piece[key] = mm_ffn1_base + 4 * (_p + 1)
                    gl_f1[piece] = b.v("gl")

                def f1grp(tensor, hc, _ac=ac_yn):
                    piece, wi = hc // 4, hc % 4
                    if hc == 0:
                        tensor.wait_ge(b.sem("ac"), _ac)
                    if wi == 0:
                        tensor.wait_ge(b.sem("gl"), gl_f1[piece])
                    for dc in range(8):
                        i = tensor.matmul((pa if hc % 2 == 0 else pb)[:],
                                          f1b[:, piece % 2, wi, dc, :],
                                          ynt[:, dc, :],
                                          start=(dc == 0), stop=(dc == 7))
                    return i

                def f1evac(eng, hc):
                    return eng.activation(hh[:, hc, :],
                                          (pa if hc % 2 == 0 else pb)[:], AF.Silu,
                                          bias=W["vfb1"][:, hc:hc + 1])

                _, ac_h = pipe(b, 32, 2, f1grp, f1evac, "scalar")

                mm_ffn2_base = b.v("mm")
                gl_f2 = {}
                for oc in range(8):
                    @b.block.sync
                    def _(sync, _oc=oc, _g=g):
                        if _g == 0 and _oc == 0:
                            sync.wait_ge(b.sem("cv"), b.cv_marks[W["cv_fw2"]])
                        key = _oc % 2
                        b.w(sync, "mm", mm_f2_piece[key])
                        i = sync.dma_start(out=f2b[:, key], in_=W["fw2"][_oc])
                        b.inc(i, "gl", 16)
                        mm_f2_piece[key] = mm_ffn2_base + _oc + 1
                    gl_f2[oc] = b.v("gl")

                def f2grp(tensor, oc, _ac=ac_h):
                    if oc == 0:
                        tensor.wait_ge(b.sem("ac"), _ac)
                    tensor.wait_ge(b.sem("gl"), gl_f2[oc])
                    for hc in range(32):
                        i = tensor.matmul((pc if oc % 2 == 0 else pd)[:],
                                          f2b[:, oc % 2, hc, :], hh[:, hc, :],
                                          start=(hc == 0), stop=(hc == 31))
                    return i

                def f2evac(eng, oc):
                    return eng.scalar_tensor_tensor(
                        ysb3[:, oc, :], (pc if oc % 2 == 0 else pd)[:],
                        W["vfb2"][:, oc:oc + 1], ysb3[:, oc, :],
                        op0=ALU.add, op1=ALU.add)

                _, ev_xp = pipe(b, H, 2, f2grp, f2evac, "vector")

                @b.block.scalar
                def _(scalar, _t0=t0, _ev=ev_xp):
                    scalar.wait_ge(b.sem("ev"), _ev)
                    i = scalar.dma_start(out=resid[:, :, _t0:_t0 + TG]
                                         .rearrange("c p t -> p c t"), in_=ysb3[:])
                    b.inc(i, "st", 16)
                stv["st_resid"] = b.v("st")

                if not last:
                    pq2 = (lambda _pf=stv["ev_pfree"], _ex=ev_xp: (lambda tensor: (
                        tensor.wait_ge(b.sem("ev"), _ex),
                        b.w(tensor, "ev", _pf))))()
                    ps2 = (lambda _ea=stv["ev_apply"], _ex=ev_xp:
                           (lambda scalar: (scalar.wait_ge(b.sem("ev"), _ex),
                                            b.w(scalar, "ev", _ea))))()
                    mm_bc2, ev_pfree2 = ln_block(b, ysb3, H, sq3, pg, ph, pg, ph,
                                                 ones_col, ones_row, sm, float(D),
                                                 sq_on_vector=False,
                                                 pre_stats_wait=pq2,
                                                 pre_sq_wait=ps2,
                                                 ones_col_b=b.ones_colb)
                    stv["ev_pfree"] = ev_pfree2

                    ev_ap2 = b.v("ev")
                    ac_af2 = b.v("ac")

                    @b.block.vector
                    def _(vector, _mb=mm_bc2, _ab=ac_af2):
                        vector.wait_ge(b.sem("mm"), _mb)
                        for c in range(H):
                            if c >= 2:
                                vector.wait_ge(b.sem("ac"), _ab + c - 1)
                            vector.tensor_mul(ap3[:, c % 2, :], ysb3[:, c, :],
                                              pg[:])
                            i = vector.tensor_sub(ap3[:, c % 2, :],
                                                  ap3[:, c % 2, :], ph[:])
                            b.inc(i, "ev")
                    stv["ev_apply"] = b.v("ev")

                    @b.block.scalar
                    def _(scalar, _ab=ev_ap2):
                        for c in range(H):
                            scalar.wait_ge(b.sem("ev"), _ab + c + 1)
                            i = scalar.activation(ynt[:, c, :], ap3[:, c % 2, :],
                                                  AF.Identity,
                                                  bias=W["vln1b_next"][:, c:c + 1],
                                                  scale=W["vln1w_next"][:, c:c + 1])
                            b.inc(i, "ac")
                    ac_xnn = b.v("ac")

                    @b.block.scalar
                    def _(scalar, _t0=t0, _a=ac_xnn):
                        scalar.wait_ge(b.sem("ac"), _a)
                        i = scalar.dma_start(out=xnb[:, :, _t0:_t0 + TG]
                                             .rearrange("c p t -> p c t"),
                                             in_=ynt[:])
                        b.inc(i, "st", 16)
                    stv["st_xnn"] = b.v("st")

            @b.block.scalar
            def _(scalar):
                scalar.wait_ge(b.sem("st"), b.v("st"))
            b.barrier()



def _p4_mid(b, nc, xin, resid, xnb, ones_col, ones_row, sm):
    with ExitStack() as ps_:
        e_ = ps_.enter_context
        ag2 = e_(nc.sbuf_tensor("ag2", [128, 2, H, TG], F32))
        x0g2 = e_(nc.sbuf_tensor("x0g2", [128, 2, H, TG], F32))
        sqg4 = e_(nc.sbuf_tensor("sqg4", [128, H, TG], F32))
        x1t = e_(nc.sbuf_tensor("x1t", [128, H, TG], F32))
        xmt = e_(nc.sbuf_tensor("xmt", [128, H, TG], BF16))
        qS1 = e_(nc.psum_tensor("qS1", [128, 512], F32))
        qS2 = e_(nc.psum_tensor("qS2", [128, 512], F32))
        qR = e_(nc.psum_tensor("qR", [128, 512], F32))
        qMR = e_(nc.psum_tensor("qMR", [128, 512], F32))

        ev_slot_free = {}
        ev_pfree_prev = 0
        st_x1_prev = 0
        st_xm_prev = 0
        for g in range(NG):
            t0 = g * TG

            @b.block.sync
            def _(sync, _g=g, _t0=t0):
                if _g == 0:
                    sync.wait_ge(b.sem("st"), b.v("st"))
                b.w(sync, "ev", ev_slot_free.get(_g - 2))
                i = sync.dma_start(out=ag2[:, _g % 2],
                                   in_=resid[:, :, _t0:_t0 + TG]
                                   .rearrange("c p t -> p c t"))
                b.inc(i, "xl", 16)
                i = sync.dma_start(out=x0g2[:, _g % 2],
                                   in_=xin[:, :, _t0:_t0 + TG]
                                   .rearrange("c p t -> p c t"))
                b.inc(i, "xl", 16)
            xl_g = b.v("xl")
            ag = ag2[:, g % 2]
            x0g = x0g2[:, g % 2]

            pq = (lambda _pf=ev_pfree_prev, _xl=xl_g: (lambda tensor: (
                tensor.wait_ge(b.sem("xl"), _xl - 16), b.w(tensor, "ev", _pf))))()
            ps = (lambda _xl=xl_g: (lambda vector:
                  vector.wait_ge(b.sem("xl"), _xl - 16)))()
            mm_bc, ev_pfree = ln_block(b, ag, H, sqg4, qS1, qS2, qR, qMR,
                                       ones_col, ones_row, sm, float(D),
                                       pre_stats_wait=pq, pre_sq_wait=ps)

            @b.block.vector
            def _(vector, _mb=mm_bc, _st=st_x1_prev, _xl=xl_g):
                vector.wait_ge(b.sem("mm"), _mb)
                vector.wait_ge(b.sem("xl"), _xl)
                b.w(vector, "st", _st)
                for c in range(H):
                    vector.tensor_mul(x1t[:, c, :], ag[:, c, :], qR[:])
                    vector.tensor_sub(x1t[:, c, :], x1t[:, c, :], qMR[:])
                    i = vector.tensor_add(x1t[:, c, :], x1t[:, c, :], x0g[:, c, :])
                b.inc(i, "ev")
            ev_slot_free[g] = b.v("ev")
            ev_x1 = b.v("ev")

            pq2 = (lambda _pf=ev_pfree, _e1=ev_x1: (lambda tensor: (
                tensor.wait_ge(b.sem("ev"), _e1), b.w(tensor, "ev", _pf))))()
            mm_bc2, ev_pfree2 = ln_block(b, x1t, H, sqg4, qS1, qS2, qR, qMR,
                                         ones_col, ones_row, sm, float(D),
                                         pre_stats_wait=pq2)
            ev_pfree_prev = ev_pfree2

            @b.block.vector
            def _(vector, _mb=mm_bc2, _st=st_xm_prev):
                vector.wait_ge(b.sem("mm"), _mb)
                b.w(vector, "st", _st)
                for c in range(H):
                    vector.tensor_mul(sqg4[:, c, :], x1t[:, c, :], qR[:])
                    i = vector.tensor_sub(xmt[:, c, :], sqg4[:, c, :], qMR[:])
                b.inc(i, "ev")
            ev_xm = b.v("ev")

            @b.block.scalar
            def _(scalar, _t0=t0, _e=ev_xm):
                scalar.wait_ge(b.sem("ev"), _e)
                i = scalar.dma_start(out=resid[:, :, _t0:_t0 + TG]
                                     .rearrange("c p t -> p c t"), in_=x1t[:])
                b.inc(i, "st", 16)
                i = scalar.dma_start(out=xnb[:, :, _t0:_t0 + TG]
                                     .rearrange("c p t -> p c t"), in_=xmt[:])
                b.inc(i, "st", 16)
            st_x1_prev = b.v("st") - 16
            st_xm_prev = b.v("st")

        @b.block.scalar
        def _(scalar):
            scalar.wait_ge(b.sem("st"), b.v("st"))
        b.barrier()



def _p5_mlp(b, nc, W, ones_col, ones_row, sm):
    resid, xnb, xout = W["resid"], W["xnb"], W["xout"]
    with ExitStack() as ps_:
        e_ = ps_.enter_context
        xmg2 = e_(nc.sbuf_tensor("xmg2", [128, 2, H, TG], BF16))
        x1g2 = e_(nc.sbuf_tensor("x1g2", [128, 2, H, TG], F32))
        ham = e_(nc.sbuf_tensor("ham", [128, 32, TG], BF16))
        hbm = e_(nc.sbuf_tensor("hbm", [128, 32, TG], BF16))
        tmp2k = e_(nc.sbuf_tensor("tmp2k", [128, 2, TG], BF16))
        mot = e_(nc.sbuf_tensor("mot", [128, H, TG], F32))
        sqf5 = e_(nc.sbuf_tensor("sqf5", [128, H, TG], BF16))
        ap5 = e_(nc.sbuf_tensor("ap5", [128, 2, TG], F32))
        w12b = e_(nc.sbuf_tensor("w12b", [128, 2, 2, 2, 8, 128], BF16))
        whb = e_(nc.sbuf_tensor("whb", [128, 2, 32, 128], BF16))
        w3b = e_(nc.sbuf_tensor("w3b", [128, 32, 128], BF16))
        ma = e_(nc.psum_tensor("ma", [128, 512], F32))
        mb = e_(nc.psum_tensor("mb", [128, 512], F32))
        mc = e_(nc.psum_tensor("mc", [128, 512], F32))
        md = e_(nc.psum_tensor("md", [128, 512], F32))
        me = e_(nc.psum_tensor("me", [128, 512], F32))
        mf = e_(nc.psum_tensor("mf", [128, 512], F32))

        cv_all = b.v("cv")
        mm_w12_piece = {0: 0, 1: 0}
        mm_wh_slab = {0: 0, 1: 0}
        st_ = {"mm_w3_used": 0, "ev_pfree_prev": 0, "ev_apply_prev": 0,
               "ev_fin_prev": 0, "so_prev": 0, "ac_affine_prev": 0, "xl_x1": 0}
        mm_xm_used = 0
        fin_pend = None
        for g in range(NG):
            t0 = g * TG

            @b.block.sync
            def _(sync, _g=g, _t0=t0, _mx=mm_xm_used):
                if _g == 0:
                    sync.wait_ge(b.sem("st"), b.v("st"))
                b.w(sync, "mm", _mx)
                i = sync.dma_start(out=xmg2[:, _g % 2],
                                   in_=xnb[:, :, _t0:_t0 + TG]
                                   .rearrange("c p t -> p c t"))
                b.inc(i, "xl", 16)
                if _g == 0:
                    i = sync.dma_start(out=x1g2[:, 0],
                                       in_=resid[:, :, _t0:_t0 + TG]
                                       .rearrange("c p t -> p c t"))
                    b.inc(i, "xl", 16)
            xl_g = b.v("xl")
            if g == 0:
                st_["xl_x1"] = xl_g
            xmg = xmg2[:, g % 2]

            mm_tl_base = b.v("mm")
            gl_12 = {}
            for piece in range(16):
                @b.block.sync
                def _(sync, _p=piece, _g=g):
                    if _g == 0 and _p == 0:
                        sync.wait_ge(b.sem("cv"), cv_all)
                    key = _p % 2
                    b.w(sync, "mm", mm_w12_piece[key])
                    i = sync.dma_start(out=w12b[:, key, 0], in_=W["M1"][_p])
                    b.inc(i, "gl", 16)
                    i = sync.dma_start(out=w12b[:, key, 1], in_=W["M2"][_p])
                    b.inc(i, "gl", 16)
                    mm_w12_piece[key] = mm_tl_base + 4 * (_p + 1)
                gl_12[piece] = b.v("gl")

            ac_tl_base = b.v("ac")
            ev_tl_base = b.v("ev")

            @b.block.tensor
            def _(tensor, _xl=xl_g, _g=g):
                tensor.wait_ge(b.sem("xl"), _xl - (16 if _g == 0 else 0))
                for hc in range(32):
                    piece, wi = hc // 2, hc % 2
                    if wi == 0:
                        tensor.wait_ge(b.sem("gl"), gl_12[piece])
                    if hc >= 2:
                        tensor.wait_ge(b.sem("ev"), ev_tl_base + hc - 1)
                    pA = ma if hc % 2 == 0 else mc
                    pB = mb if hc % 2 == 0 else md
                    for dc in range(8):
                        i = tensor.matmul(pA[:], w12b[:, piece % 2, 0, wi, dc, :],
                                          xmg[:, dc, :], start=(dc == 0),
                                          stop=(dc == 7))
                    b.inc(i, "mm")
                    for dc in range(8):
                        i = tensor.matmul(pB[:], w12b[:, piece % 2, 1, wi, dc, :],
                                          xmg[:, dc, :], start=(dc == 0),
                                          stop=(dc == 7))
                    b.inc(i, "mm")
            mm_xm_used = b.v("mm")

            if fin_pend is not None:
                _fa_ev = fin_pend[0]()
                _fb_mm = fin_pend[1](_fa_ev)

            @b.block.scalar
            def _(scalar):
                for hc in range(32):
                    scalar.wait_ge(b.sem("mm"), mm_tl_base + 2 * hc + 1)
                    i = scalar.activation(tmp2k[:, hc % 2, :],
                                          (ma if hc % 2 == 0 else mc)[:], AF.Silu,
                                          scale=W["vms1"][:, hc:hc + 1])
                    b.inc(i, "ac")

            @b.block.vector
            def _(vector):
                for hc in range(32):
                    vector.wait_ge(b.sem("ac"), ac_tl_base + hc + 1)
                    vector.wait_ge(b.sem("mm"), mm_tl_base + 2 * hc + 2)
                    i = vector.scalar_tensor_tensor(
                        ham[:, hc, :], (mb if hc % 2 == 0 else md)[:],
                        W["vms2"][:, hc:hc + 1], tmp2k[:, hc % 2, :],
                        op0=ALU.mult, op1=ALU.mult)
                    b.inc(i, "ev")
            ev_ha = b.v("ev")

            if fin_pend is not None:
                _fb_ev = fin_pend[2](_fb_mm)

            def hid(wdram, scales, src, dst, ev_src_ready):
                mm_hid_base = b.v("mm")
                gl_h = {}
                for oc in range(32):
                    @b.block.sync
                    def _(sync, _oc=oc):
                        key = _oc % 2
                        b.w(sync, "mm", mm_wh_slab[key])
                        i = sync.dma_start(out=whb[:, key], in_=wdram[_oc])
                        b.inc(i, "gl", 16)
                        mm_wh_slab[key] = mm_hid_base + _oc + 1
                    gl_h[oc] = b.v("gl")

                def hgrp(tensor, oc, _esr=ev_src_ready):
                    if oc == 0:
                        tensor.wait_ge(b.sem("ev"), _esr)
                    tensor.wait_ge(b.sem("gl"), gl_h[oc])
                    for hc in range(32):
                        i = tensor.matmul((mc if oc % 2 == 0 else md)[:],
                                          whb[:, oc % 2, hc, :], src[:, hc, :],
                                          start=(hc == 0), stop=(hc == 31))
                    return i

                def hevac(eng, oc):
                    return eng.activation(dst[:, oc, :],
                                          (mc if oc % 2 == 0 else md)[:], AF.Gelu,
                                          scale=scales[:, oc:oc + 1])

                return pipe(b, 32, 2, hgrp, hevac, "scalar")

            _, ac_h1 = hid(W["MH1"], W["vmsh1"], ham, hbm, ev_ha)

            if fin_pend is not None:
                fin_pend[3](_fb_ev)
                fin_pend = None

            @b.block.vector
            def _(vector, _a=ac_h1):
                vector.wait_ge(b.sem("ac"), _a)
                i = vector.engine_nop()
                b.inc(i, "ev")
            ev_h1 = b.v("ev")
            _, ac_h2 = hid(W["MH2"], W["vmsh2"], hbm, ham, ev_h1)

            @b.block.vector
            def _(vector, _a=ac_h2):
                vector.wait_ge(b.sem("ac"), _a)
                i = vector.engine_nop()
                b.inc(i, "ev")
            ev_h2 = b.v("ev")

            mm_m3_base = b.v("mm")
            gl_3 = {}
            for oc in range(8):
                @b.block.sync
                def _(sync, _oc=oc):
                    b.w(sync, "mm",
                        st_["mm_w3_used"] if _oc == 0 else mm_m3_base + _oc)
                    i = sync.dma_start(out=w3b[:], in_=W["M3"][_oc])
                    b.inc(i, "gl", 16)
                gl_3[oc] = b.v("gl")

            def m3grp(tensor, oc, _e=ev_h2, _ef=st_["ev_fin_prev"]):
                if oc == 0:
                    tensor.wait_ge(b.sem("ev"), _e)
                    b.w(tensor, "ev", _ef)
                tensor.wait_ge(b.sem("gl"), gl_3[oc])
                for hc in range(32):
                    i = tensor.matmul((me if oc % 2 == 0 else mf)[:],
                                      w3b[:, hc, :], ham[:, hc, :],
                                      start=(hc == 0), stop=(hc == 31))
                return i

            def m3evac(eng, oc):
                return eng.activation(mot[:, oc, :],
                                      (me if oc % 2 == 0 else mf)[:], AF.Identity,
                                      scale=W["vms3"][:, oc:oc + 1])

            mm_m3, ac_mo = pipe(b, H, 2, m3grp, m3evac, "scalar")
            st_["mm_w3_used"] = mm_m3

            def fin_a(_ac_mo=ac_mo, _ea=st_["ev_apply_prev"]):
                @b.block.vector
                def _(vector):
                    vector.wait_ge(b.sem("ac"), _ac_mo)
                    b.w(vector, "ev", _ea)
                    i = vector.tensor_mul(sqf5[:], mot[:], mot[:])
                    b.inc(i, "ev")
                return b.v("ev")

            def fin_b1(ev_sq, _ac_mo=ac_mo, _pf=st_["ev_pfree_prev"]):
                @b.block.tensor
                def _(tensor):
                    tensor.wait_ge(b.sem("ac"), _ac_mo)
                    b.w(tensor, "ev", _pf)
                    for c in range(H):
                        tensor.matmul(me[0:1, :], ones_col[:], mot[:, c, :],
                                      start=(c == 0), stop=(c == H - 1))
                    tensor.wait_ge(b.sem("ev"), ev_sq)
                    for c in range(H):
                        i = tensor.matmul(mf[0:1, :], b.ones_colb[:],
                                          sqf5[:, c, :],
                                          start=(c == 0), stop=(c == H - 1))
                    b.inc(i, "mm")
                return b.v("mm")

            def fin_b2(mm_stats):
                ev_pfree = ln_smalls(b, sm, me, mf, float(D), mm_stats)
                st_["ev_pfree_prev"] = ev_pfree
                return b.v("ev")

            def fin_c(ev_mr, _g=g, _t0=t0, _xl1=st_["xl_x1"]):
                @b.block.tensor
                def _(tensor):
                    tensor.wait_ge(b.sem("ev"), ev_mr)
                    tensor.matmul(me[:], ones_row[:], sm["rstd"][0:1, :],
                                  start=True, stop=True)
                    i = tensor.matmul(mf[:], ones_row[:], sm["mr"][0:1, :],
                                      start=True, stop=True)
                    b.inc(i, "mm")
                mm_bco = b.v("mm")

                @b.block.vector
                def _(vector, _mb=mm_bco, _st=st_["so_prev"]):
                    vector.wait_ge(b.sem("mm"), _mb)
                    vector.wait_ge(b.sem("xl"), _xl1)
                    b.w(vector, "so", _st)
                    for c in range(H):
                        vector.tensor_mul(ap5[:, 0, :], mot[:, c, :], me[:])
                        vector.tensor_sub(ap5[:, 1, :], ap5[:, 0, :], mf[:])
                        i = vector.tensor_add(mot[:, c, :], ap5[:, 1, :],
                                              x1g2[:, _g % 2, c, :])
                        b.inc(i, "ev")
                st_["ev_apply_prev"] = b.v("ev")
                st_["ev_fin_prev"] = b.v("ev")
                ev_o = b.v("ev")

                @b.block.scalar
                def _(scalar, _e=ev_o):
                    scalar.wait_ge(b.sem("ev"), _e)
                    i = scalar.dma_start(out=xout[:, :, _t0:_t0 + TG]
                                         .rearrange("c p t -> p c t"), in_=mot[:])
                    b.inc(i, "so", 16)
                st_["so_prev"] = b.v("so")

                if _g + 1 < NG:
                    @b.block.sync
                    def _(sync, _e=ev_o):
                        sync.wait_ge(b.sem("ev"), _e)
                        i = sync.dma_start(
                            out=x1g2[:, (_g + 1) % 2],
                            in_=resid[:, :, _t0 + TG:_t0 + 2 * TG]
                            .rearrange("c p t -> p c t"))
                        b.inc(i, "xl", 16)
                    st_["xl_x1"] = b.v("xl")

            fin_pend = (fin_a, fin_b1, fin_b2, fin_c)

        _fa_ev = fin_pend[0]()
        _fb_mm = fin_pend[1](_fa_ev)
        _fb_ev = fin_pend[2](_fb_mm)
        fin_pend[3](_fb_ev)

        @b.block.scalar
        def _(scalar):
            scalar.wait_ge(b.sem("so"), b.v("so"))
        b.barrier()



_CACHED = {}


def get_nc(tap=None):
    if tap not in _CACHED:
        _CACHED[tap] = build_program(tap)
    return _CACHED[tap]


def kernel(**inputs):
    nc = get_nc()
    in_maps = [prep_core_inputs(inputs, k) for k in range(NCORES)]
    res = run_bass_kernel_spmd(nc, in_maps, core_ids=list(range(NCORES)))
    outs = []
    for k in range(NCORES):
        o = np.asarray(res.results[k]["xout"])
        o = o.reshape(H, 128, BLOC, S).transpose(2, 3, 0, 1).reshape(BLOC, S, D)
        outs.append(o)
    return np.concatenate(outs, axis=0).astype(np.float32)


# version 06
# speedup vs baseline: 1.1014x; 1.0496x over previous
import numpy as np
import ml_dtypes
from contextlib import ExitStack

import concourse.bass as bass
import concourse.mybir as mybir
from concourse.bass_utils import run_bass_kernel_spmd

F32 = mybir.dt.float32
BF16 = mybir.dt.bfloat16
AF = mybir.ActivationFunctionType
ALU = mybir.AluOpType

H = 8
HD = 128
D = 1024
FFND = 4096
S = 2048
BLOC = 2
T = BLOC * S
TG = 512
NG = T // TG
EPS = 1e-5
NCORES = 8

BF16NP = ml_dtypes.bfloat16



def _perm_deint():
    return np.concatenate([np.arange(0, HD, 2), np.arange(1, HD, 2)])


def _xpos_tables():
    SCALE_BASE = 512
    scale_vec = (np.arange(0, HD, 2) + 0.4 * HD) / (1.4 * HD)
    power = (np.arange(S, dtype=np.float64) / SCALE_BASE)[:, None]
    scale = scale_vec[None, :] ** power
    half = HD // 2
    inv_freq = 1.0 / 10000.0 ** (np.arange(half, dtype=np.float64) / half)
    ang = np.arange(S, dtype=np.float64)[:, None] * inv_freq[None, :]
    sin, cos = np.sin(ang), np.cos(ang)

    def dupT(m):
        return np.concatenate([m.T, m.T], axis=0)

    cosq = dupT(cos * scale)
    sinq = dupT(sin * scale)
    cosk = dupT(cos / scale)
    sink = dupT(sin / scale)
    sinq[0:64] *= -1.0
    sink[0:64] *= -1.0
    return (cosq.astype(np.float32), sinq.astype(np.float32),
            cosk.astype(np.float32), sink.astype(np.float32))


def _decay_consts():
    gam = 1.0 - np.exp(np.linspace(np.log(1.0 / 32), np.log(1.0 / 512), H))
    j = np.arange(TG, dtype=np.float64)
    gdec = np.broadcast_to(gam[:, None, None] ** j[None, None, :],
                           (H, 128, TG)).astype(np.float32)
    p = np.arange(128, dtype=np.float64)
    mi = np.arange(16, dtype=np.float64)
    expo = 128.0 * (mi[None, None, :] - 3.0) - p[:, None, None]
    dvec = (gam[None, :, None] ** expo).astype(np.float32)
    dd = np.arange(4)[:, None, None]
    pp = np.arange(128)[None, :, None]
    jj = np.arange(TG)[None, None, :]
    masks = (jj >= 128 * dd + pp).astype(BF16NP)
    return gdec, dvec, masks


def _lhsT_slabs(w):
    I, O = w.shape
    return np.ascontiguousarray(
        w.reshape(I // 128, 128, O // 128, 128).transpose(2, 1, 0, 3))


def _pieces(slabs, k):
    NO, _, NI, _ = slabs.shape
    return np.ascontiguousarray(
        slabs.reshape(NO // k, k, 128, NI, 128).transpose(0, 2, 1, 3, 4))


def _vec_pc(v):
    n = v.shape[0] // 128
    return np.ascontiguousarray(v.reshape(n, 128).T)


def prep_core_inputs(inputs, core):
    f = lambda k: np.asarray(inputs[k], dtype=np.float32)
    X = f("L0_attn_X")
    xc = X[2 * core:2 * core + 2]
    xin = np.ascontiguousarray(
        xc.reshape(BLOC, S, H, 128).transpose(2, 3, 0, 1).reshape(H, 128, T))

    perm = _perm_deint()
    m = {"xin": xin}
    for i in range(2):
        WQ = f("L0_attn_WQ")[i][:, :, perm]
        WK = f("L0_attn_WK")[i][:, :, perm]
        WV = f("L0_attn_WV")[i]
        m[f"wq{i}"] = _lhsT_slabs(WQ.transpose(1, 0, 2).reshape(D, H * HD))
        m[f"wk{i}"] = _lhsT_slabs(WK.transpose(1, 0, 2).reshape(D, H * HD))
        m[f"wv{i}"] = np.ascontiguousarray(
            WV.transpose(1, 0, 2).reshape(D, H * HD).reshape(8, 128, 1024))
        m[f"wg{i}"] = _pieces(_lhsT_slabs(f("L0_attn_WG")[i]), 4)
        m[f"wo{i}"] = _pieces(_lhsT_slabs(f("L0_attn_WO")[i]), 4)
        m[f"fw1{i}"] = _pieces(_lhsT_slabs(f("L0_attn_ffn_w1")[i]), 4)
        m[f"fw2{i}"] = _lhsT_slabs(f("L0_attn_ffn_w2")[i])
        m[f"ln1w{i}"] = _vec_pc(f("L0_attn_ln1_w")[i])
        m[f"ln1b{i}"] = _vec_pc(f("L0_attn_ln1_b")[i])
        m[f"ln2w{i}"] = _vec_pc(f("L0_attn_ln2_w")[i])
        m[f"ln2b{i}"] = _vec_pc(f("L0_attn_ln2_b")[i])
        m[f"gnw{i}"] = _vec_pc(f("L0_attn_gn_w")[i])
        m[f"gnb{i}"] = _vec_pc(f("L0_attn_gn_b")[i])
        m[f"fb1{i}"] = _vec_pc(f("L0_attn_ffn_b1")[i])
        m[f"fb2{i}"] = _vec_pc(f("L0_attn_ffn_b2")[i])

    m["m1"] = _pieces(_lhsT_slabs(np.ascontiguousarray(f("L0_ffn_w1_w").T)), 2)
    m["m2"] = _pieces(_lhsT_slabs(np.ascontiguousarray(f("L0_ffn_w2_w").T)), 2)
    m["mh1"] = _lhsT_slabs(np.ascontiguousarray(f("L0_ffn_h1_w").T))
    m["mh2"] = _lhsT_slabs(np.ascontiguousarray(f("L0_ffn_h2_w").T))
    m["m3"] = _lhsT_slabs(np.ascontiguousarray(f("L0_ffn_w3_w").T))
    m["ms1"] = _vec_pc(f("L0_ffn_w1_s"))
    m["ms2"] = _vec_pc(f("L0_ffn_w2_s"))
    m["msh1"] = _vec_pc(f("L0_ffn_h1_s"))
    m["msh2"] = _vec_pc(f("L0_ffn_h2_s"))
    m["ms3"] = _vec_pc(f("L0_ffn_w3_s"))

    m["cq"], m["sq_"], m["ck"], m["sk_"] = _xpos_tables()
    m["gdec"], m["dvec"], m["masks"] = _decay_consts()
    return m



class B:
    def __init__(self, nc, block, stack):
        self.nc = nc
        self.block = block
        self.stack = stack
        self.sems = {}
        self.cnt = {}

    def mksem(self, *names):
        for n in names:
            self.sems[n] = self.stack.enter_context(self.nc.semaphore(n))
            self.cnt[n] = 0

    def inc(self, inst, name, by=1):
        inst.then_inc(self.sems[name], by)
        self.cnt[name] += by
        return self.cnt[name]

    def v(self, name):
        return self.cnt[name]

    def sem(self, name):
        return self.sems[name]

    def w(self, eng, name, val):
        if val and val > 0:
            eng.wait_ge(self.sems[name], val)

    def barrier(self, engines=("sync", "tensor", "vector", "scalar")):
        tgt = self.cnt["bar"] + len(engines)
        for e in engines:
            @getattr(self.block, e)
            def _(eng):
                eng.drain()
                eng.sem_inc(self.sems["bar"], 1)
        self.cnt["bar"] = tgt
        for e in engines:
            @getattr(self.block, e)
            def _(eng):
                eng.wait_ge(self.sems["bar"], tgt)


def pipe(b, n, nbanks, emit_group, emit_evac, evac_eng="scalar",
         evac_extra_wait=None):
    cn = {"scalar": "ac", "vector": "ev"}[evac_eng]
    mm_base = b.v("mm")
    ev_base = b.v(cn)

    @b.block.tensor
    def _(tensor):
        for gi in range(n):
            if gi >= nbanks:
                tensor.wait_ge(b.sem(cn), ev_base + gi - nbanks + 1)
            i = emit_group(tensor, gi)
            b.inc(i, "mm")

    @getattr(b.block, evac_eng)
    def _(eng):
        for gi in range(n):
            eng.wait_ge(b.sem("mm"), mm_base + gi + 1)
            if evac_extra_wait:
                evac_extra_wait(eng, gi)
            i = emit_evac(eng, gi)
            b.inc(i, cn)

    return b.v("mm"), b.v(cn)


def ln_smalls(b, sm, pS1, pS2, denom, mm_stats):
    @b.block.vector
    def _(vector):
        vector.wait_ge(b.sem("mm"), mm_stats)
        vector.tensor_scalar_mul(sm["mu"][:], pS1[0:1, :], 1.0 / denom)
        vector.tensor_scalar_mul(sm["aux"][:], pS2[0:1, :], 1.0 / denom)
        vector.tensor_mul(sm["m2"][:], sm["mu"][:], sm["mu"][:])
        i = vector.tensor_sub(sm["var"][:], sm["aux"][:], sm["m2"][:])
        b.inc(i, "ev")
    ev_small = b.v("ev")

    @b.block.scalar
    def _(scalar):
        scalar.wait_ge(b.sem("ev"), ev_small)
        scalar.activation(sm["aux"][:], sm["var"][:], AF.Ln, bias=sm["eps"][0:1, 0:1])
        i = scalar.activation(sm["rstd"][:], sm["aux"][:], AF.Exp,
                              scale=sm["nh"][0:1, 0:1])
        b.inc(i, "ac")
    ac_rstd = b.v("ac")

    @b.block.vector
    def _(vector):
        vector.wait_ge(b.sem("ac"), ac_rstd)
        i = vector.tensor_mul(sm["mr"][:], sm["mu"][:], sm["rstd"][:])
        b.inc(i, "ev")
    return ev_small


def ln_block(b, src_view, nchunk, sqbuf, pS1, pS2, pR, pMR, ones_col, ones_row,
             sm, denom, sq_on_vector=True, pre_stats_wait=None, pre_sq_wait=None,
             ones_col_b=None):
    if sq_on_vector:
        @b.block.vector
        def _(vector):
            if pre_sq_wait:
                pre_sq_wait(vector)
            i = vector.tensor_mul(sqbuf[:], src_view[:], src_view[:])
            b.inc(i, "ev")
        ev_sq = b.v("ev")

        @b.block.tensor
        def _(tensor):
            if pre_stats_wait:
                pre_stats_wait(tensor)
            for c in range(nchunk):
                tensor.matmul(pS1[0:1, :], ones_col[:], src_view[:, c, :],
                              start=(c == 0), stop=(c == nchunk - 1))
            tensor.wait_ge(b.sem("ev"), ev_sq)
            for c in range(nchunk):
                i = tensor.matmul(pS2[0:1, :], ones_col[:], sqbuf[:, c, :],
                                  start=(c == 0), stop=(c == nchunk - 1))
            b.inc(i, "mm")
    else:
        ac_base = b.v("ac")
        mm_base = b.v("mm")

        @b.block.scalar
        def _(scalar):
            if pre_sq_wait:
                pre_sq_wait(scalar)
            for c in range(nchunk):
                if c >= 2:
                    scalar.wait_ge(b.sem("mm"), mm_base + c - 1)
                i = scalar.activation(sqbuf[:, c % 2, :], src_view[:, c, :],
                                      AF.Square)
                b.inc(i, "ac")

        @b.block.tensor
        def _(tensor):
            if pre_stats_wait:
                pre_stats_wait(tensor)
            for c in range(nchunk):
                tensor.matmul(pS1[0:1, :], ones_col[:], src_view[:, c, :],
                              start=(c == 0), stop=(c == nchunk - 1))
            oc2 = ones_col_b if ones_col_b is not None else ones_col
            for c in range(nchunk):
                tensor.wait_ge(b.sem("ac"), ac_base + c + 1)
                i = tensor.matmul(pS2[0:1, :], oc2[:], sqbuf[:, c % 2, :],
                                  start=(c == 0), stop=(c == nchunk - 1))
                b.inc(i, "mm")
    mm_stats = b.v("mm")

    ev_pfree = ln_smalls(b, sm, pS1, pS2, denom, mm_stats)
    ev_mr = b.v("ev")

    @b.block.tensor
    def _(tensor):
        tensor.wait_ge(b.sem("ev"), ev_mr)
        tensor.matmul(pR[:], ones_row[:], sm["rstd"][0:1, :], start=True, stop=True)
        i = tensor.matmul(pMR[:], ones_row[:], sm["mr"][0:1, :], start=True, stop=True)
        b.inc(i, "mm")
    return b.v("mm"), ev_pfree


def emit_conversions(b, jobs, cvb):
    nq = [0]

    @b.block.gpsimd
    def _(gpsimd):
        for src, dst, key, N, M in jobs:
            nsub = (M + 2047) // 2048
            for n in range(N):
                for s_ in range(nsub):
                    w = min(2048, M - 2048 * s_)
                    i = nq[0]
                    par = i % 2
                    if i >= 2:
                        gpsimd.wait_ge(b.sem("cv"), 16 * (i - 1))
                    ii = gpsimd.dma_start(out=cvb[:, par, 0:w],
                                          in_=src[n][:, 2048 * s_:2048 * s_ + w])
                    b.inc(ii, "cvl", 16)
                    gpsimd.wait_ge(b.sem("cvl"), 16 * (i + 1))
                    ii = gpsimd.dma_start(out=dst[n][:, 2048 * s_:2048 * s_ + w],
                                          in_=cvb[:, par, 0:w])
                    b.inc(ii, "cv", 16)
                    nq[0] += 1
            b.cv_marks[key] = b.v("cv")



def build_program(tap=None):
    nc = bass.Bass("TRN2", target_bir_lowering=False)
    stack = ExitStack()

    def din(name, shape, dt=F32):
        return nc.dram_tensor(name, list(shape), dt, kind="ExternalInput").ap()

    xin = din("xin", [H, 128, T])
    WQd = [din(f"wq{i}", [8, 128, 8, 128]) for i in range(2)]
    WKd = [din(f"wk{i}", [8, 128, 8, 128]) for i in range(2)]
    WVd = [din(f"wv{i}", [8, 128, 1024]) for i in range(2)]
    WGd = [din(f"wg{i}", [2, 128, 4, 8, 128]) for i in range(2)]
    WOd = [din(f"wo{i}", [2, 128, 4, 8, 128]) for i in range(2)]
    FW1 = [din(f"fw1{i}", [8, 128, 4, 8, 128]) for i in range(2)]
    FW2 = [din(f"fw2{i}", [8, 128, 32, 128]) for i in range(2)]
    V8 = lambda nm: [din(f"{nm}{i}", [128, 8]) for i in range(2)]
    LN1W, LN1B, LN2W, LN2B = V8("ln1w"), V8("ln1b"), V8("ln2w"), V8("ln2b")
    GNW, GNB, FB2 = V8("gnw"), V8("gnb"), V8("fb2")
    FB1 = [din(f"fb1{i}", [128, 32]) for i in range(2)]
    M1 = din("m1", [16, 128, 2, 8, 128])
    M2 = din("m2", [16, 128, 2, 8, 128])
    MH1 = din("mh1", [32, 128, 32, 128])
    MH2 = din("mh2", [32, 128, 32, 128])
    M3 = din("m3", [8, 128, 32, 128])
    MS1, MS2 = din("ms1", [128, 32]), din("ms2", [128, 32])
    MSH1, MSH2 = din("msh1", [128, 32]), din("msh2", [128, 32])
    MS3 = din("ms3", [128, 8])
    CQ, SQ_ = din("cq", [128, S]), din("sq_", [128, S])
    CK, SK_ = din("ck", [128, S]), din("sk_", [128, S])
    GDEC = din("gdec", [H, 128, TG])
    DVEC = din("dvec", [128, 8, 16])
    MASKS = din("masks", [4, 128, TG], BF16)

    xout = nc.dram_tensor("xout", [H, 128, T], F32, kind="ExternalOutput").ap()
    skind = "ExternalOutput" if tap else "Internal"
    CVshapes = {
        "fw1_0": [8, 128, 4, 8, 128], "fw2_0": [8, 128, 32, 128],
        "fw1_1": [8, 128, 4, 8, 128], "fw2_1": [8, 128, 32, 128],
        "m1": [16, 128, 2, 8, 128], "m2": [16, 128, 2, 8, 128],
        "mh1": [32, 128, 32, 128], "mh2": [32, 128, 32, 128],
        "m3": [8, 128, 32, 128],
    }
    CV = {k: nc.dram_tensor("cv_" + k, s, BF16, kind="Internal").ap()
          for k, s in CVshapes.items()}
    global HTAP
    HTAP = nc.dram_tensor("htap", [NG, 128, 32, TG], BF16, kind=skind).ap() \
        if tap else None
    resid = nc.dram_tensor("resid", [H, 128, T], F32, kind=skind).ap()
    xnb = nc.dram_tensor("xnb", [H, 128, T], BF16, kind=skind).ap()
    ybuf = nc.dram_tensor("ybuf", [H, 128, T], BF16, kind=skind).ap()

    sb = lambda name, shape, dt=F32: stack.enter_context(
        nc.sbuf_tensor(name, list(shape), dt))
    ones_col = sb("ones_col", [128, 1])
    ones_colb = sb("ones_colb", [128, 1], BF16)
    ones_row = sb("ones_row", [1, 128])
    vln1w = [sb(f"vln1w{i}", [128, 8]) for i in range(2)]
    vln1b = [sb(f"vln1b{i}", [128, 8]) for i in range(2)]
    vln2w = [sb(f"vln2w{i}", [128, 8]) for i in range(2)]
    vln2b = [sb(f"vln2b{i}", [128, 8]) for i in range(2)]
    vgnw = [sb(f"vgnw{i}", [128, 8]) for i in range(2)]
    vgnb = [sb(f"vgnb{i}", [128, 8]) for i in range(2)]
    vfb1 = [sb(f"vfb1{i}", [128, 32]) for i in range(2)]
    vfb2 = [sb(f"vfb2{i}", [128, 8]) for i in range(2)]
    vms1, vms2 = sb("vms1", [128, 32]), sb("vms2", [128, 32])
    vmsh1, vmsh2 = sb("vmsh1", [128, 32]), sb("vmsh2", [128, 32])
    vms3 = sb("vms3", [128, 8])
    vdvec = sb("vdvec", [128, 8, 16])
    sm = {k: sb("sm_" + k, [1, TG]) for k in ("mu", "aux", "m2", "var", "rstd", "mr")}
    sm["eps"] = sb("sm_eps", [1, 1])
    sm["nh"] = sb("sm_nh", [1, 1])
    cvb = sb("cvb", [128, 2, 2048], BF16)

    with nc.Block() as block:
        b = B(nc, block, stack)
        b.ones_colb = ones_colb
        b.mksem("cl", "xl", "wl", "gl", "mm", "ev", "ac", "st", "so", "bar",
                "cvl", "cv", "g2", "x2")

        @block.sync
        def _(sync):
            pairs = [(MS1, vms1), (MS2, vms2), (MSH1, vmsh1), (MSH2, vmsh2),
                     (MS3, vms3), (DVEC, vdvec)]
            for i in range(2):
                pairs += [(LN1W[i], vln1w[i]), (LN1B[i], vln1b[i]),
                          (LN2W[i], vln2w[i]), (LN2B[i], vln2b[i]),
                          (GNW[i], vgnw[i]), (GNB[i], vgnb[i]),
                          (FB1[i], vfb1[i]), (FB2[i], vfb2[i])]
            for src, dst in pairs:
                b.inc(sync.dma_start(out=dst[:], in_=src), "cl", 16)
        cl_consts = b.v("cl")

        @block.vector
        def _(vector):
            vector.memset(ones_col[:], 1.0)
            vector.memset(ones_colb[:], 1.0)
            vector.memset(ones_row[:], 1.0)
            vector.memset(sm["eps"][:], EPS)
            i = vector.memset(sm["nh"][:], -0.5)
            b.inc(i, "ev")
        ev_ones = b.v("ev")

        for e in ("tensor", "scalar"):
            @getattr(block, e)
            def _(eng):
                eng.wait_ge(b.sem("ev"), ev_ones)
                eng.wait_ge(b.sem("cl"), cl_consts)

        b.cv_marks = {}
        cv_jobs = []

        def flat(ap, N, M):
            return ap.rearrange("a p " + " ".join(
                f"d{i}" for i in range(len(ap.shape) - 2)) + " -> a p (" + " ".join(
                f"d{i}" for i in range(len(ap.shape) - 2)) + ")")

        for key, fsrc in (("fw1_0", FW1[0]), ("fw2_0", FW2[0]),
                          ("fw1_1", FW1[1]), ("fw2_1", FW2[1]),
                          ("m1", M1), ("m2", M2), ("mh1", MH1), ("mh2", MH2),
                          ("m3", M3)):
            shp = CVshapes[key]
            N = shp[0]
            M = int(np.prod(shp[2:]))
            cv_jobs.append((flat(fsrc, N, M), flat(CV[key], N, M), key, N, M))

        with nc.named_scope("p0"):
            _p0(b, nc, xin, resid, xnb, vln1w[0], vln1b[0], ones_col, ones_row, sm)
            emit_conversions(b, cv_jobs, cvb)
        if tap == "p0":
            _finish(b)
            stack.close()
            return nc

        consts = dict(CQ=CQ, SQ_=SQ_, CK=CK, SK_=SK_, GDEC=GDEC, MASKS=MASKS,
                      vdvec=vdvec, resid=resid, xnb=xnb, ybuf=ybuf)
        for li in range(2):
            Wl = dict(wq=WQd[li], wk=WKd[li], wv=WVd[li], wg=WGd[li], wo=WOd[li],
                      fw1=CV[f"fw1_{li}"], fw2=CV[f"fw2_{li}"],
                      cv_fw1=f"fw1_{li}", cv_fw2=f"fw2_{li}",
                      vln2w=vln2w[li], vln2b=vln2b[li],
                      vgnw=vgnw[li], vgnb=vgnb[li],
                      vfb1=vfb1[li], vfb2=vfb2[li],
                      vln1w_next=vln1w[1], vln1b_next=vln1b[1])
            with nc.named_scope(f"p1_{li}"):
                _p1_retention(b, li, nc, Wl, consts, ones_col, ones_row, sm)
            if tap == f"r{li}":
                _finish(b)
                stack.close()
                return nc
            with nc.named_scope(f"p3_{li}"):
                _p3_woffn(b, li, nc, Wl, consts, ones_col, ones_row, sm)
            if tap == f"l{li}":
                _finish(b)
                stack.close()
                return nc

        with nc.named_scope("p4"):
            _p4_mid(b, nc, xin, resid, xnb, ones_col, ones_row, sm)
        if tap == "p4":
            _finish(b)
            stack.close()
            return nc

        Wm = dict(M1=CV["m1"], M2=CV["m2"], MH1=CV["mh1"], MH2=CV["mh2"],
                  M3=CV["m3"],
                  vms1=vms1, vms2=vms2, vmsh1=vmsh1, vmsh2=vmsh2, vms3=vms3,
                  resid=resid, xnb=xnb, xout=xout)
        with nc.named_scope("p5"):
            _p5_mlp(b, nc, Wm, ones_col, ones_row, sm)
        _finish(b)

    stack.close()
    return nc


def _finish(b):
    for e in ("scalar", "sync"):
        @getattr(b.block, e)
        def _(eng):
            eng.wait_ge(b.sem("st"), b.v("st"))
            eng.wait_ge(b.sem("so"), b.v("so"))



def _p0(b, nc, xin, resid, xnb, vw, vb, ones_col, ones_row, sm):
    with ExitStack() as ps_:
        e_ = ps_.enter_context
        xg2 = e_(nc.sbuf_tensor("xg2", [128, 2, H, TG], F32))
        h0t = e_(nc.sbuf_tensor("h0t", [128, H, TG], F32))
        sqg = e_(nc.sbuf_tensor("sqg", [128, H, TG], F32))
        xnt = e_(nc.sbuf_tensor("xnt", [128, H, TG], BF16))
        pS1 = e_(nc.psum_tensor("pS1", [128, 512], F32))
        pS2 = e_(nc.psum_tensor("pS2", [128, 512], F32))
        pR = e_(nc.psum_tensor("pR", [128, 512], F32))
        pMR = e_(nc.psum_tensor("pMR", [128, 512], F32))

        ev_xg_free = {}
        ev_pfree_prev = 0
        st_h0_prev = 0
        st_xn_prev = 0
        ac_xn_prev = 0
        for g in range(NG):
            t0 = g * TG

            @b.block.sync
            def _(sync, _g=g, _t0=t0):
                b.w(sync, "ev", ev_xg_free.get(_g - 2))
                i = sync.dma_start(out=xg2[:, _g % 2],
                                   in_=xin[:, :, _t0:_t0 + TG].rearrange("c p t -> p c t"))
                b.inc(i, "xl", 16)
            xl_g = b.v("xl")
            xg = xg2[:, g % 2]

            pq1 = (lambda _pf=ev_pfree_prev, _xl=xl_g: (lambda tensor: (
                tensor.wait_ge(b.sem("xl"), _xl), b.w(tensor, "ev", _pf))))()
            ps1 = (lambda _xl=xl_g, _ac=ac_xn_prev: (lambda vector: (
                vector.wait_ge(b.sem("xl"), _xl), b.w(vector, "ac", _ac))))()
            mm_bc1, ev_pfree = ln_block(b, xg, H, sqg, pS1, pS2, pR, pMR,
                                        ones_col, ones_row, sm, float(D),
                                        pre_stats_wait=pq1, pre_sq_wait=ps1)

            @b.block.vector
            def _(vector, _mb=mm_bc1, _st=st_h0_prev):
                vector.wait_ge(b.sem("mm"), _mb)
                b.w(vector, "st", _st)
                for c in range(H):
                    vector.tensor_mul(h0t[:, c, :], xg[:, c, :], pR[:])
                    i = vector.tensor_sub(h0t[:, c, :], h0t[:, c, :], pMR[:])
                b.inc(i, "ev")
            ev_xg_free[g] = b.v("ev")
            ev_apply1 = b.v("ev")

            pq2 = (lambda _pf=ev_pfree, _e1=ev_apply1: (lambda tensor: (
                tensor.wait_ge(b.sem("ev"), _e1), b.w(tensor, "ev", _pf))))()
            mm_bc2, ev_pfree2 = ln_block(b, h0t, H, sqg, pS1, pS2, pR, pMR,
                                         ones_col, ones_row, sm, float(D),
                                         pre_stats_wait=pq2)
            ev_pfree_prev = ev_pfree2

            @b.block.vector
            def _(vector, _mb=mm_bc2):
                vector.wait_ge(b.sem("mm"), _mb)
                for c in range(H):
                    vector.tensor_mul(sqg[:, c, :], h0t[:, c, :], pR[:])
                    i = vector.tensor_sub(sqg[:, c, :], sqg[:, c, :], pMR[:])
                b.inc(i, "ev")
            ev_u = b.v("ev")

            @b.block.scalar
            def _(scalar, _e=ev_u, _st=st_xn_prev):
                scalar.wait_ge(b.sem("ev"), _e)
                b.w(scalar, "st", _st)
                for c in range(H):
                    i = scalar.activation(xnt[:, c, :], sqg[:, c, :], AF.Identity,
                                          bias=vb[:, c:c + 1], scale=vw[:, c:c + 1])
                b.inc(i, "ac")
            ac_xn_prev = b.v("ac")
            ac_xn = b.v("ac")

            @b.block.scalar
            def _(scalar, _t0=t0, _ac=ac_xn):
                i = scalar.dma_start(out=resid[:, :, _t0:_t0 + TG]
                                     .rearrange("c p t -> p c t"), in_=h0t[:])
                b.inc(i, "st", 16)
                scalar.wait_ge(b.sem("ac"), _ac)
                i = scalar.dma_start(out=xnb[:, :, _t0:_t0 + TG]
                                     .rearrange("c p t -> p c t"), in_=xnt[:])
                b.inc(i, "st", 16)
            st_h0_prev = b.v("st") - 16
            st_xn_prev = b.v("st")

        @b.block.scalar
        def _(scalar):
            scalar.wait_ge(b.sem("st"), b.v("st"))
        b.barrier()



def _p1_retention(b, li, nc, W, C, ones_col, ones_row, sm):
    with ExitStack() as ps_:
        e_ = ps_.enter_context
        tcq = e_(nc.sbuf_tensor(f"tcq{li}", [128, S], F32))
        tsq = e_(nc.sbuf_tensor(f"tsq{li}", [128, S], F32))
        tck = e_(nc.sbuf_tensor(f"tck{li}", [128, S], F32))
        tsk = e_(nc.sbuf_tensor(f"tsk{li}", [128, S], F32))
        tmsk = e_(nc.sbuf_tensor(f"tmsk{li}", [128, 4, TG], BF16))
        tgd = e_(nc.sbuf_tensor(f"tgd{li}", [128, 2, TG], F32))
        xnh = e_(nc.sbuf_tensor(f"xnh{li}", [128, H, S], BF16))
        wvf = e_(nc.sbuf_tensor(f"wvf{li}", [128, 1024], F32))
        wvb = e_(nc.sbuf_tensor(f"wvb{li}", [128, 8, 1024], BF16))
        wqkf = e_(nc.sbuf_tensor(f"wqkf{li}", [128, 2, 8, 128], F32))
        wqkb = e_(nc.sbuf_tensor(f"wqkb{li}", [128, 2, 2, 8, 128], BF16))
        vsb = e_(nc.sbuf_tensor(f"vsb{li}", [128, 16, 256], BF16))
        qsb = e_(nc.sbuf_tensor(f"qsb{li}", [128, S], BF16))
        ksb = e_(nc.sbuf_tensor(f"ksb{li}", [128, S], BF16))
        scsb = e_(nc.sbuf_tensor(f"scsb{li}", [128, 16, TG], BF16))
        xt1 = e_(nc.sbuf_tensor(f"xt1{li}", [128, TG], F32))
        xt2 = e_(nc.sbuf_tensor(f"xt2{li}", [128, TG], F32))
        ytl = e_(nc.sbuf_tensor(f"ytl{li}", [128, TG], F32))
        sq1 = e_(nc.sbuf_tensor(f"sq1{li}", [128, TG], F32))
        ygn = e_(nc.sbuf_tensor(f"ygn{li}", [128, S], BF16))
        pq0 = e_(nc.psum_tensor(f"pq0{li}", [128, 512], F32))
        pq1 = e_(nc.psum_tensor(f"pq1{li}", [128, 512], F32))
        pq2 = e_(nc.psum_tensor(f"pq2{li}", [128, 512], F32))
        pq3 = e_(nc.psum_tensor(f"pq3{li}", [128, 512], F32))
        psc0 = e_(nc.psum_tensor(f"psc0{li}", [128, 512], F32))
        psc1 = e_(nc.psum_tensor(f"psc1{li}", [128, 512], F32))
        pyh = e_(nc.psum_tensor(f"pyh{li}", [128, 512], F32))
        pst = e_(nc.psum_tensor(f"pst{li}", [128, 512], F32))
        pqs = [pq0, pq1, pq2, pq3]
        pscs = [psc0, psc1]

        @b.block.sync
        def _(sync):
            for src, dst in ((C["CQ"], tcq), (C["SQ_"], tsq),
                             (C["CK"], tck), (C["SK_"], tsk)):
                b.inc(sync.dma_start(out=dst[:], in_=src), "cl", 16)
            i = sync.dma_start(out=tmsk[:], in_=C["MASKS"].rearrange("d p t -> p d t"))
            b.inc(i, "cl", 16)
        cl_tab = b.v("cl")

        ev_wv_prev = 0
        for dc in range(8):
            @b.block.sync
            def _(sync, _dc=dc, _ev=ev_wv_prev):
                b.w(sync, "ev", _ev)
                i = sync.dma_start(out=wvf[:], in_=W["wv"][_dc])
                b.inc(i, "wl", 16)
            wl_v = b.v("wl")

            @b.block.vector
            def _(vector, _dc=dc, _wl=wl_v):
                vector.wait_ge(b.sem("wl"), _wl)
                i = vector.tensor_copy(wvb[:, _dc, :], wvf[:])
                b.inc(i, "ev")
            ev_wv_prev = b.v("ev")
        ev_wv = b.v("ev")

        mm_last_v_use = 0
        ev_gn_prev = 0
        ev_gn_slot = {0: 0, 1: 0}
        ev_qk_evac_prev = 0
        ev_wqkf_prev = 0
        st_ygn_prev = 0

        for bb in range(BLOC):
            tb0 = bb * S

            @b.block.sync
            def _(sync, _tb0=tb0):
                sync.wait_ge(b.sem("st"), b.v("st"))
                b.w(sync, "mm", b.v("mm"))
                i = sync.dma_start(out=xnh[:], in_=C["xnb"][:, :, _tb0:_tb0 + S]
                                   .rearrange("c p t -> p c t"))
                b.inc(i, "xl", 16)
            xl_xn = b.v("xl")

            for hg in range(4):
                def vgrp(tensor, tt, _hg=hg, _xl=xl_xn,
                         _eq=ev_qk_evac_prev, _eg=ev_gn_prev):
                    if tt == 0:
                        tensor.wait_ge(b.sem("xl"), _xl)
                        tensor.wait_ge(b.sem("ev"), ev_wv)
                        b.w(tensor, "ev", _eq)
                        b.w(tensor, "ev", _eg)
                    for dc in range(8):
                        i = tensor.matmul(pqs[tt % 4][:, 0:256],
                                          xnh[:, dc, tt * 128:(tt + 1) * 128],
                                          wvb[:, dc, 256 * _hg:256 * (_hg + 1)],
                                          start=(dc == 0), stop=(dc == 7))
                    return i

                def vevac(eng, tt, _mv=mm_last_v_use):
                    if tt == 0:
                        b.w(eng, "mm", _mv)
                    return eng.activation(vsb[:, tt, :], pqs[tt % 4][:, 0:256],
                                          AF.Copy)

                _, ac_v = pipe(b, 16, 4, vgrp, vevac, "scalar")

                for h in (2 * hg, 2 * hg + 1):
                    hcol = h - 2 * hg
                    par = h % 2

                    @b.block.sync
                    def _(sync, _par=par, _h=h, _ewf=ev_wqkf_prev,
                          _egs=ev_gn_slot[par]):
                        b.w(sync, "ev", _ewf)
                        b.inc(sync.dma_start(out=wqkf[:, 0], in_=W["wq"][_h]), "wl", 16)
                        b.inc(sync.dma_start(out=wqkf[:, 1], in_=W["wk"][_h]), "wl", 16)
                        b.w(sync, "ev", _egs)
                        b.inc(sync.dma_start(out=tgd[:, _par], in_=C["GDEC"][_h]),
                              "cl", 16)
                    wl_qk = b.v("wl")
                    cl_gd = b.v("cl")

                    @b.block.vector
                    def _(vector, _par=par, _wl=wl_qk):
                        vector.wait_ge(b.sem("wl"), _wl)
                        i = vector.tensor_copy(wqkb[:, _par], wqkf[:])
                        b.inc(i, "ev")
                    ev_qk_w = b.v("ev")
                    ev_wqkf_prev = ev_qk_w

                    for qk in range(2):
                        dst = qsb if qk == 0 else ksb
                        ctab = tcq if qk == 0 else tck
                        stab = tsq if qk == 0 else tsk

                        def qgrp(tensor, tt, _qk=qk, _par=par, _evw=ev_qk_w,
                                 _evfree=ev_qk_evac_prev, _evgn=ev_gn_prev,
                                 _acv=ac_v):
                            if tt == 0:
                                tensor.wait_ge(b.sem("ev"), _evw)
                                tensor.wait_ge(b.sem("cl"), cl_tab)
                                b.w(tensor, "ev", _evfree)
                                b.w(tensor, "ev", _evgn)
                                if _qk == 0:
                                    b.w(tensor, "ac", _acv)
                            for dc in range(8):
                                i = tensor.matmul(pqs[tt][:],
                                                  wqkb[:, _par, _qk, dc, :],
                                                  xnh[:, dc, tt * TG:(tt + 1) * TG],
                                                  start=(dc == 0), stop=(dc == 7))
                            return i

                        def qevac(eng, tt, _dst=dst, _ct=ctab, _st=stab):
                            sl = slice(tt * TG, (tt + 1) * TG)
                            ps = pqs[tt]
                            eng.tensor_mul(xt1[:], ps[:], _ct[:, sl])
                            eng.tensor_mul(xt2[0:64, :], ps[64:128, :], _st[0:64, sl])
                            eng.tensor_mul(xt2[64:128, :], ps[0:64, :], _st[64:128, sl])
                            return eng.tensor_add(_dst[:, sl], xt1[:], xt2[:])

                        pipe(b, 4, 4, qgrp, qevac, "vector")
                        ev_qk_evac_prev = b.v("ev")

                    for sc in range(4):
                        s0 = sc * TG
                        ntj = 4 * sc + 4

                        def sgrp(tensor, tj, _s0=s0, _ev=ev_qk_evac_prev):
                            if tj == 0:
                                tensor.wait_ge(b.sem("ev"), _ev)
                            return tensor.matmul(pscs[tj % 2][:],
                                                 ksb[:, tj * 128:(tj + 1) * 128],
                                                 qsb[:, _s0:_s0 + TG],
                                                 start=True, stop=True)

                        def sevac(eng, tj, _sc=sc, _h=h):
                            mi = 4 * _sc - tj + 3
                            return eng.activation(scsb[:, tj, :], pscs[tj % 2][:],
                                                  AF.Identity,
                                                  scale=C["vdvec"][:, _h, mi:mi + 1])

                        ac_sc_base = b.v("ac")
                        _, ac_sc = pipe(b, ntj, 2, sgrp, sevac, "scalar")

                        @b.block.vector
                        def _(vector, _sc=sc, _acb=ac_sc_base):
                            for d in range(4):
                                tj = 4 * _sc + d
                                vector.wait_ge(b.sem("ac"), _acb + tj + 1)
                                i = vector.tensor_mul(scsb[:, tj, :], scsb[:, tj, :],
                                                      tmsk[:, d, :])
                                b.inc(i, "ev")
                        ev_msk = b.v("ev")

                        @b.block.tensor
                        def _(tensor, _hcol=hcol, _ntj=ntj, _acs=ac_sc,
                              _evm=ev_msk, _evgn=ev_gn_prev):
                            tensor.wait_ge(b.sem("ac"), _acs)
                            tensor.wait_ge(b.sem("ev"), _evm)
                            b.w(tensor, "ev", _evgn)
                            for tj in range(_ntj):
                                i = tensor.matmul(
                                    pyh[:], vsb[:, tj, _hcol * 128:(_hcol + 1) * 128],
                                    scsb[:, tj, :],
                                    start=(tj == 0), stop=(tj == _ntj - 1))
                            b.inc(i, "mm")
                        mm_yh = b.v("mm")
                        mm_last_v_use = mm_yh

                        @b.block.vector
                        def _(vector, _par=par, _my=mm_yh, _cg=cl_gd):
                            vector.wait_ge(b.sem("mm"), _my)
                            vector.wait_ge(b.sem("cl"), _cg)
                            i = vector.tensor_mul(ytl[:], pyh[:], tgd[:, _par])
                            b.inc(i, "ev")
                        ev_yt = b.v("ev")

                        @b.block.scalar
                        def _(scalar, _e=ev_yt):
                            scalar.wait_ge(b.sem("ev"), _e)
                            i = scalar.activation(sq1[:], ytl[:], AF.Square)
                            b.inc(i, "ac")
                        ac_sq = b.v("ac")

                        @b.block.tensor
                        def _(tensor, _a=ac_sq, _evgn=ev_gn_prev):
                            tensor.wait_ge(b.sem("ac"), _a)
                            b.w(tensor, "ev", _evgn)
                            tensor.matmul(pst[0:1, :], ones_col[:], ytl[:],
                                          start=True, stop=True)
                            i = tensor.matmul(pq0[0:1, :], ones_col[:], sq1[:],
                                              start=True, stop=True)
                            b.inc(i, "mm")
                        mm_gstat = b.v("mm")

                        ln_smalls(b, sm, pst, pq0, float(HD), mm_gstat)
                        ev_mr = b.v("ev")

                        @b.block.tensor
                        def _(tensor, _em=ev_mr, _evgn=ev_gn_prev):
                            tensor.wait_ge(b.sem("ev"), _em)
                            b.w(tensor, "ev", _evgn)
                            tensor.matmul(pq1[:], ones_row[:], sm["rstd"][0:1, :],
                                          start=True, stop=True)
                            i = tensor.matmul(pq2[:], ones_row[:], sm["mr"][0:1, :],
                                              start=True, stop=True)
                            b.inc(i, "mm")
                        mm_gbc = b.v("mm")

                        @b.block.vector
                        def _(vector, _mb=mm_gbc):
                            vector.wait_ge(b.sem("mm"), _mb)
                            vector.tensor_mul(xt1[:], ytl[:], pq1[:])
                            i = vector.tensor_sub(xt1[:], xt1[:], pq2[:])
                            b.inc(i, "ev")
                        ev_gn_prev = b.v("ev")
                        ev_gn_slot[par] = b.v("ev")
                        ev_gna = b.v("ev")

                        @b.block.scalar
                        def _(scalar, _s0=s0, _h=h, _e=ev_gna, _st=st_ygn_prev):
                            scalar.wait_ge(b.sem("ev"), _e)
                            if _s0 == 0:
                                b.w(scalar, "st", _st)
                            i = scalar.activation(ygn[:, _s0:_s0 + TG], xt1[:],
                                                  AF.Identity,
                                                  bias=W["vgnb"][:, _h:_h + 1],
                                                  scale=W["vgnw"][:, _h:_h + 1])
                            b.inc(i, "ac")
                    ac_gn_all = b.v("ac")

                    @b.block.scalar
                    def _(scalar, _h=h, _tb0=tb0, _a=ac_gn_all):
                        scalar.wait_ge(b.sem("ac"), _a)
                        i = scalar.dma_start(out=C["ybuf"][_h][:, _tb0:_tb0 + S],
                                             in_=ygn[:])
                        b.inc(i, "st", 16)
                    st_ygn_prev = b.v("st")

        @b.block.scalar
        def _(scalar):
            scalar.wait_ge(b.sem("st"), b.v("st"))
        b.barrier()



def _p3_woffn(b, li, nc, W, C, ones_col, ones_row, sm):
    resid, xnb, ybuf = C["resid"], C["xnb"], C["ybuf"]
    last = (li == 1)
    with nc.sbuf_tensor(f"wgo{li}", [128, 2, 8, 8, 128], BF16) as wgo:
        with nc.sbuf_tensor(f"wgof{li}", [128, 4, 8, 128], F32) as wgof:
            ev_cv_prev = 0
            for k in range(2):
                wsrc = W["wg"] if k == 0 else W["wo"]
                for hf in range(2):
                    @b.block.sync
                    def _(sync, _hf=hf, _w=wsrc, _ev=ev_cv_prev):
                        b.w(sync, "ev", _ev)
                        i = sync.dma_start(out=wgof[:], in_=_w[_hf])
                        b.inc(i, "wl", 16)
                    wl_w = b.v("wl")

                    @b.block.vector
                    def _(vector, _k=k, _hf=hf, _wl=wl_w):
                        vector.wait_ge(b.sem("wl"), _wl)
                        i = vector.tensor_copy(wgo[:, _k, 4 * _hf:4 * _hf + 4],
                                               wgof[:])
                        b.inc(i, "ev")
                    ev_cv_prev = b.v("ev")
            ev_wgo = b.v("ev")

        with ExitStack() as ps_:
            e_ = ps_.enter_context
            xng2 = e_(nc.sbuf_tensor(f"xng2{li}", [128, 2, H, TG], BF16))
            ybg2 = e_(nc.sbuf_tensor(f"ybg2{li}", [128, 2, H, TG], BF16))
            xrg = e_(nc.sbuf_tensor(f"xrg{li}", [128, H, TG], F32))
            gg = e_(nc.sbuf_tensor(f"gg{li}", [128, H, TG], BF16))
            ysb3 = e_(nc.sbuf_tensor(f"ysb3{li}", [128, H, TG], F32))
            ynt = e_(nc.sbuf_tensor(f"ynt{li}", [128, H, TG], BF16))
            hh = e_(nc.sbuf_tensor(f"hh{li}", [128, 32, TG], BF16))
            sq3 = e_(nc.sbuf_tensor(f"sq3{li}", [128, 2, TG], BF16))
            ap3 = e_(nc.sbuf_tensor(f"ap3{li}", [128, 2, TG], F32))
            f1b = e_(nc.sbuf_tensor(f"f1b{li}", [128, 2, 4, 8, 128], BF16))
            f2b = e_(nc.sbuf_tensor(f"f2b{li}", [128, 2, 32, 128], BF16))
            pa = e_(nc.psum_tensor(f"pa{li}", [128, 512], F32))
            pb = e_(nc.psum_tensor(f"pb{li}", [128, 512], F32))
            pc = e_(nc.psum_tensor(f"pc{li}", [128, 512], F32))
            pd = e_(nc.psum_tensor(f"pd{li}", [128, 512], F32))
            pe = e_(nc.psum_tensor(f"pe{li}", [128, 512], F32))
            pf = e_(nc.psum_tensor(f"pf{li}", [128, 512], F32))
            pg = e_(nc.psum_tensor(f"pg{li}", [128, 512], F32))
            ph = e_(nc.psum_tensor(f"ph{li}", [128, 512], F32))

            stv = {"ev_pfree": 0, "ev_apply": 0, "st_resid": 0, "st_xnn": 0,
                   "ev_pg": {0: 0, 1: 0}, "mm_xn": {0: 0, 1: 0}, "ev_y": 0,
                   "ac_g": 0}
            mm_f1_piece = {0: 0, 1: 0}
            mm_f2_piece = {0: 0, 1: 0}

            def emit_loads(g):
                t0 = g * TG

                @b.block.sync
                def _(sync, _g=g, _t0=t0):
                    if _g == 0:
                        sync.wait_ge(b.sem("st"), b.v("st"))
                        sync.wait_ge(b.sem("ev"), ev_wgo)
                    b.w(sync, "mm", stv["mm_xn"][_g % 2])
                    i = sync.dma_start(out=xng2[:, _g % 2],
                                       in_=xnb[:, :, _t0:_t0 + TG]
                                       .rearrange("c p t -> p c t"))
                    b.inc(i, "xl", 16)
                    b.w(sync, "ev", stv["ev_pg"][_g % 2])
                    i = sync.dma_start(out=ybg2[:, _g % 2],
                                       in_=ybuf[:, :, _t0:_t0 + TG]
                                       .rearrange("c p t -> p c t"))
                    b.inc(i, "xl", 16)
                    b.w(sync, "ev", stv["ev_y"])
                    i = sync.dma_start(out=xrg[:],
                                       in_=resid[:, :, _t0:_t0 + TG]
                                       .rearrange("c p t -> p c t"))
                    b.inc(i, "xl", 16)
                return b.v("xl")

            def emit_G(g, xl_g):
                xng = xng2[:, g % 2]

                def ggrp(tensor, oc, _xl=xl_g, _ew=ev_wgo):
                    if oc == 0:
                        tensor.wait_ge(b.sem("xl"), _xl - 32)
                        tensor.wait_ge(b.sem("ev"), _ew)
                    for dc in range(8):
                        i = tensor.matmul((pe if oc % 2 == 0 else pf)[:],
                                          wgo[:, 0, oc, dc, :], xng[:, dc, :],
                                          start=(dc == 0), stop=(dc == 7))
                    return i

                def gevac(eng, oc):
                    return eng.activation(gg[:, oc, :],
                                          (pe if oc % 2 == 0 else pf)[:], AF.Silu)

                mm_g, ac_g = pipe(b, H, 2, ggrp, gevac, "scalar")
                stv["mm_xn"][g % 2] = mm_g
                stv["ac_g"] = ac_g

            xl_cur = emit_loads(0)
            emit_G(0, xl_cur)
            for g in range(NG):
                t0 = g * TG
                xl_g = xl_cur
                ybg = ybg2[:, g % 2]

                @b.block.vector
                def _(vector, _a=stv["ac_g"], _xl=xl_g):
                    vector.wait_ge(b.sem("ac"), _a)
                    vector.wait_ge(b.sem("xl"), _xl - 16)
                    i = vector.tensor_mul(gg[:], gg[:], ybg[:])
                    b.inc(i, "ev")
                stv["ev_pg"][g % 2] = b.v("ev")
                ev_pg = b.v("ev")

                def wgrp(tensor, oc, _ev=ev_pg, _xl=xl_g):
                    if oc == 0:
                        tensor.wait_ge(b.sem("ev"), _ev)
                        tensor.wait_ge(b.sem("xl"), _xl)
                    for dc in range(8):
                        i = tensor.matmul((pc if oc % 2 == 0 else pd)[:],
                                          wgo[:, 1, oc, dc, :], gg[:, dc, :],
                                          start=(dc == 0), stop=(dc == 7))
                    return i

                def wevac(eng, oc, _st=stv["st_resid"]):
                    if oc == 0:
                        b.w(eng, "st", _st)
                    return eng.tensor_add(ysb3[:, oc, :],
                                          (pc if oc % 2 == 0 else pd)[:],
                                          xrg[:, oc, :])

                _, ev_y = pipe(b, H, 2, wgrp, wevac, "vector")
                stv["ev_y"] = ev_y

                if g + 1 < NG:
                    xl_cur = emit_loads(g + 1)
                    emit_G(g + 1, xl_cur)

                pq = (lambda _pf=stv["ev_pfree"], _ey=ev_y: (lambda tensor: (
                    tensor.wait_ge(b.sem("ev"), _ey), b.w(tensor, "ev", _pf))))()
                ps = (lambda _ea=stv["ev_apply"], _ey=ev_y, _sx=stv["st_xnn"]:
                      (lambda scalar: (scalar.wait_ge(b.sem("ev"), _ey),
                                       b.w(scalar, "ev", _ea),
                                       b.w(scalar, "st", _sx))))()
                mm_bc, ev_pfree = ln_block(b, ysb3, H, sq3, pg, ph, pg, ph,
                                           ones_col, ones_row, sm, float(D),
                                           sq_on_vector=False,
                                           pre_stats_wait=pq, pre_sq_wait=ps,
                                           ones_col_b=b.ones_colb)
                stv["ev_pfree"] = ev_pfree

                ev_ap_base = b.v("ev")
                ac_aff_base = b.v("ac")

                @b.block.vector
                def _(vector, _mb=mm_bc, _ab=ac_aff_base):
                    vector.wait_ge(b.sem("mm"), _mb)
                    for c in range(H):
                        if c >= 2:
                            vector.wait_ge(b.sem("ac"), _ab + c - 1)
                        vector.tensor_mul(ap3[:, c % 2, :], ysb3[:, c, :], pg[:])
                        i = vector.tensor_sub(ap3[:, c % 2, :], ap3[:, c % 2, :],
                                              ph[:])
                        b.inc(i, "ev")
                stv["ev_apply"] = b.v("ev")

                @b.block.scalar
                def _(scalar, _ab=ev_ap_base, _sx=stv["st_xnn"]):
                    b.w(scalar, "st", _sx)
                    for c in range(H):
                        scalar.wait_ge(b.sem("ev"), _ab + c + 1)
                        i = scalar.activation(ynt[:, c, :], ap3[:, c % 2, :],
                                              AF.Identity,
                                              bias=W["vln2b"][:, c:c + 1],
                                              scale=W["vln2w"][:, c:c + 1])
                        b.inc(i, "ac")
                ac_yn = b.v("ac")

                mm_ffn1_base = b.v("mm")
                gl_f1 = {}
                for piece in range(8):
                    @b.block.sync
                    def _(sync, _p=piece, _g=g):
                        if _g == 0 and _p == 0:
                            sync.wait_ge(b.sem("cv"), b.cv_marks[W["cv_fw1"]])
                        key = _p % 2
                        b.w(sync, "mm", mm_f1_piece[key])
                        i = sync.dma_start(out=f1b[:, key], in_=W["fw1"][_p])
                        b.inc(i, "gl", 16)
                        mm_f1_piece[key] = mm_ffn1_base + 4 * (_p + 1)
                    gl_f1[piece] = b.v("gl")

                def f1grp(tensor, hc, _ac=ac_yn):
                    piece, wi = hc // 4, hc % 4
                    if hc == 0:
                        tensor.wait_ge(b.sem("ac"), _ac)
                    if wi == 0:
                        tensor.wait_ge(b.sem("gl"), gl_f1[piece])
                    for dc in range(8):
                        i = tensor.matmul((pa if hc % 2 == 0 else pb)[:],
                                          f1b[:, piece % 2, wi, dc, :],
                                          ynt[:, dc, :],
                                          start=(dc == 0), stop=(dc == 7))
                    return i

                def f1evac(eng, hc):
                    return eng.activation(hh[:, hc, :],
                                          (pa if hc % 2 == 0 else pb)[:], AF.Silu,
                                          bias=W["vfb1"][:, hc:hc + 1])

                _, ac_h = pipe(b, 32, 2, f1grp, f1evac, "scalar")

                mm_ffn2_base = b.v("mm")
                gl_f2 = {}
                for oc in range(8):
                    @b.block.sync
                    def _(sync, _oc=oc, _g=g):
                        if _g == 0 and _oc == 0:
                            sync.wait_ge(b.sem("cv"), b.cv_marks[W["cv_fw2"]])
                        key = _oc % 2
                        b.w(sync, "mm", mm_f2_piece[key])
                        i = sync.dma_start(out=f2b[:, key], in_=W["fw2"][_oc])
                        b.inc(i, "gl", 16)
                        mm_f2_piece[key] = mm_ffn2_base + _oc + 1
                    gl_f2[oc] = b.v("gl")

                def f2grp(tensor, oc, _ac=ac_h):
                    if oc == 0:
                        tensor.wait_ge(b.sem("ac"), _ac)
                    tensor.wait_ge(b.sem("gl"), gl_f2[oc])
                    for hc in range(32):
                        i = tensor.matmul((pc if oc % 2 == 0 else pd)[:],
                                          f2b[:, oc % 2, hc, :], hh[:, hc, :],
                                          start=(hc == 0), stop=(hc == 31))
                    return i

                def f2evac(eng, oc):
                    return eng.scalar_tensor_tensor(
                        ysb3[:, oc, :], (pc if oc % 2 == 0 else pd)[:],
                        W["vfb2"][:, oc:oc + 1], ysb3[:, oc, :],
                        op0=ALU.add, op1=ALU.add)

                _, ev_xp = pipe(b, H, 2, f2grp, f2evac, "vector")

                @b.block.scalar
                def _(scalar, _t0=t0, _ev=ev_xp):
                    scalar.wait_ge(b.sem("ev"), _ev)
                    i = scalar.dma_start(out=resid[:, :, _t0:_t0 + TG]
                                         .rearrange("c p t -> p c t"), in_=ysb3[:])
                    b.inc(i, "st", 16)
                stv["st_resid"] = b.v("st")

                if not last:
                    pq2 = (lambda _pf=stv["ev_pfree"], _ex=ev_xp: (lambda tensor: (
                        tensor.wait_ge(b.sem("ev"), _ex),
                        b.w(tensor, "ev", _pf))))()
                    ps2 = (lambda _ea=stv["ev_apply"], _ex=ev_xp:
                           (lambda scalar: (scalar.wait_ge(b.sem("ev"), _ex),
                                            b.w(scalar, "ev", _ea))))()
                    mm_bc2, ev_pfree2 = ln_block(b, ysb3, H, sq3, pg, ph, pg, ph,
                                                 ones_col, ones_row, sm, float(D),
                                                 sq_on_vector=False,
                                                 pre_stats_wait=pq2,
                                                 pre_sq_wait=ps2,
                                                 ones_col_b=b.ones_colb)
                    stv["ev_pfree"] = ev_pfree2

                    ev_ap2 = b.v("ev")
                    ac_af2 = b.v("ac")

                    @b.block.vector
                    def _(vector, _mb=mm_bc2, _ab=ac_af2):
                        vector.wait_ge(b.sem("mm"), _mb)
                        for c in range(H):
                            if c >= 2:
                                vector.wait_ge(b.sem("ac"), _ab + c - 1)
                            vector.tensor_mul(ap3[:, c % 2, :], ysb3[:, c, :],
                                              pg[:])
                            i = vector.tensor_sub(ap3[:, c % 2, :],
                                                  ap3[:, c % 2, :], ph[:])
                            b.inc(i, "ev")
                    stv["ev_apply"] = b.v("ev")

                    @b.block.scalar
                    def _(scalar, _ab=ev_ap2):
                        for c in range(H):
                            scalar.wait_ge(b.sem("ev"), _ab + c + 1)
                            i = scalar.activation(ynt[:, c, :], ap3[:, c % 2, :],
                                                  AF.Identity,
                                                  bias=W["vln1b_next"][:, c:c + 1],
                                                  scale=W["vln1w_next"][:, c:c + 1])
                            b.inc(i, "ac")
                    ac_xnn = b.v("ac")

                    @b.block.scalar
                    def _(scalar, _t0=t0, _a=ac_xnn):
                        scalar.wait_ge(b.sem("ac"), _a)
                        i = scalar.dma_start(out=xnb[:, :, _t0:_t0 + TG]
                                             .rearrange("c p t -> p c t"),
                                             in_=ynt[:])
                        b.inc(i, "st", 16)
                    stv["st_xnn"] = b.v("st")

            @b.block.scalar
            def _(scalar):
                scalar.wait_ge(b.sem("st"), b.v("st"))
            b.barrier()



def _p4_mid(b, nc, xin, resid, xnb, ones_col, ones_row, sm):
    with ExitStack() as ps_:
        e_ = ps_.enter_context
        ag2 = e_(nc.sbuf_tensor("ag2", [128, 2, H, TG], F32))
        x0g2 = e_(nc.sbuf_tensor("x0g2", [128, 2, H, TG], F32))
        sqg4 = e_(nc.sbuf_tensor("sqg4", [128, H, TG], F32))
        x1t = e_(nc.sbuf_tensor("x1t", [128, H, TG], F32))
        xmt = e_(nc.sbuf_tensor("xmt", [128, H, TG], BF16))
        qS1 = e_(nc.psum_tensor("qS1", [128, 512], F32))
        qS2 = e_(nc.psum_tensor("qS2", [128, 512], F32))
        qR = e_(nc.psum_tensor("qR", [128, 512], F32))
        qMR = e_(nc.psum_tensor("qMR", [128, 512], F32))

        ev_slot_free = {}
        ev_pfree_prev = 0
        st_x1_prev = 0
        st_xm_prev = 0
        for g in range(NG):
            t0 = g * TG

            @b.block.sync
            def _(sync, _g=g, _t0=t0):
                if _g == 0:
                    sync.wait_ge(b.sem("st"), b.v("st"))
                b.w(sync, "ev", ev_slot_free.get(_g - 2))
                i = sync.dma_start(out=ag2[:, _g % 2],
                                   in_=resid[:, :, _t0:_t0 + TG]
                                   .rearrange("c p t -> p c t"))
                b.inc(i, "xl", 16)
                i = sync.dma_start(out=x0g2[:, _g % 2],
                                   in_=xin[:, :, _t0:_t0 + TG]
                                   .rearrange("c p t -> p c t"))
                b.inc(i, "xl", 16)
            xl_g = b.v("xl")
            ag = ag2[:, g % 2]
            x0g = x0g2[:, g % 2]

            pq = (lambda _pf=ev_pfree_prev, _xl=xl_g: (lambda tensor: (
                tensor.wait_ge(b.sem("xl"), _xl - 16), b.w(tensor, "ev", _pf))))()
            ps = (lambda _xl=xl_g: (lambda vector:
                  vector.wait_ge(b.sem("xl"), _xl - 16)))()
            mm_bc, ev_pfree = ln_block(b, ag, H, sqg4, qS1, qS2, qR, qMR,
                                       ones_col, ones_row, sm, float(D),
                                       pre_stats_wait=pq, pre_sq_wait=ps)

            @b.block.vector
            def _(vector, _mb=mm_bc, _st=st_x1_prev, _xl=xl_g):
                vector.wait_ge(b.sem("mm"), _mb)
                vector.wait_ge(b.sem("xl"), _xl)
                b.w(vector, "st", _st)
                for c in range(H):
                    vector.tensor_mul(x1t[:, c, :], ag[:, c, :], qR[:])
                    vector.tensor_sub(x1t[:, c, :], x1t[:, c, :], qMR[:])
                    i = vector.tensor_add(x1t[:, c, :], x1t[:, c, :], x0g[:, c, :])
                b.inc(i, "ev")
            ev_slot_free[g] = b.v("ev")
            ev_x1 = b.v("ev")

            pq2 = (lambda _pf=ev_pfree, _e1=ev_x1: (lambda tensor: (
                tensor.wait_ge(b.sem("ev"), _e1), b.w(tensor, "ev", _pf))))()
            mm_bc2, ev_pfree2 = ln_block(b, x1t, H, sqg4, qS1, qS2, qR, qMR,
                                         ones_col, ones_row, sm, float(D),
                                         pre_stats_wait=pq2)
            ev_pfree_prev = ev_pfree2

            @b.block.vector
            def _(vector, _mb=mm_bc2, _st=st_xm_prev):
                vector.wait_ge(b.sem("mm"), _mb)
                b.w(vector, "st", _st)
                for c in range(H):
                    vector.tensor_mul(sqg4[:, c, :], x1t[:, c, :], qR[:])
                    i = vector.tensor_sub(xmt[:, c, :], sqg4[:, c, :], qMR[:])
                b.inc(i, "ev")
            ev_xm = b.v("ev")

            @b.block.scalar
            def _(scalar, _t0=t0, _e=ev_xm):
                scalar.wait_ge(b.sem("ev"), _e)
                i = scalar.dma_start(out=resid[:, :, _t0:_t0 + TG]
                                     .rearrange("c p t -> p c t"), in_=x1t[:])
                b.inc(i, "st", 16)
                i = scalar.dma_start(out=xnb[:, :, _t0:_t0 + TG]
                                     .rearrange("c p t -> p c t"), in_=xmt[:])
                b.inc(i, "st", 16)
            st_x1_prev = b.v("st") - 16
            st_xm_prev = b.v("st")

        @b.block.scalar
        def _(scalar):
            scalar.wait_ge(b.sem("st"), b.v("st"))
        b.barrier()



def _p5_mlp(b, nc, W, ones_col, ones_row, sm):
    resid, xnb, xout = W["resid"], W["xnb"], W["xout"]
    with ExitStack() as ps_:
        e_ = ps_.enter_context
        xmg2 = e_(nc.sbuf_tensor("xmg2", [128, 2, H, TG], BF16))
        x1g = e_(nc.sbuf_tensor("x1g", [128, H, TG], F32))
        ham = e_(nc.sbuf_tensor("ham", [128, 32, TG], BF16))
        hbm = e_(nc.sbuf_tensor("hbm", [128, 32, TG], BF16))
        tmp2k = e_(nc.sbuf_tensor("tmp2k", [128, 2, TG], BF16))
        mot = e_(nc.sbuf_tensor("mot", [128, H, TG], F32))
        sqf5 = e_(nc.sbuf_tensor("sqf5", [128, H, TG], BF16))
        ap5 = e_(nc.sbuf_tensor("ap5", [128, 2, TG], F32))
        w12b = e_(nc.sbuf_tensor("w12b", [128, 2, 2, 2, 8, 128], BF16))
        whb = e_(nc.sbuf_tensor("whb", [128, 3, 32, 128], BF16))
        w3b = e_(nc.sbuf_tensor("w3b", [128, 2, 32, 128], BF16))
        ma = e_(nc.psum_tensor("ma", [128, 512], F32))
        mb = e_(nc.psum_tensor("mb", [128, 512], F32))
        mc = e_(nc.psum_tensor("mc", [128, 512], F32))
        md = e_(nc.psum_tensor("md", [128, 512], F32))
        me = e_(nc.psum_tensor("me", [128, 512], F32))
        mf = e_(nc.psum_tensor("mf", [128, 512], F32))

        cv_all = b.v("cv")
        mm_w12_piece = {0: 0, 1: 0}
        mm_wh_slab = {0: 0, 1: 0, 2: 0}
        st_ = {"mm_w3_used": 0, "ev_pfree_prev": 0, "ev_apply_prev": 0,
               "ev_fin_prev": 0, "so_prev": 0, "ac_affine_prev": 0, "xl_x1": 0}
        mm_xm_used = 0
        fin_pend = None
        for g in range(NG):
            t0 = g * TG

            @b.block.sync
            def _(sync, _g=g, _t0=t0, _mx=mm_xm_used):
                if _g == 0:
                    sync.wait_ge(b.sem("st"), b.v("st"))
                b.w(sync, "mm", _mx)
                i = sync.dma_start(out=xmg2[:, _g % 2],
                                   in_=xnb[:, :, _t0:_t0 + TG]
                                   .rearrange("c p t -> p c t"))
                b.inc(i, "xl", 16)
            xl_g = b.v("xl")
            if g == 0:
                @b.block.sync
                def _(sync, _t0=t0):
                    i = sync.dma_start(out=x1g[:],
                                       in_=resid[:, :, _t0:_t0 + TG]
                                       .rearrange("c p t -> p c t"))
                    b.inc(i, "x2", 16)
                st_["xl_x1"] = b.v("x2")
            xmg = xmg2[:, g % 2]

            mm_tl_base = b.v("mm")
            gl_12 = {}
            for piece in range(16):
                @b.block.sync
                def _(sync, _p=piece, _g=g):
                    if _g == 0 and _p == 0:
                        sync.wait_ge(b.sem("cv"), cv_all)
                    key = _p % 2
                    b.w(sync, "mm", mm_w12_piece[key])
                    i = sync.dma_start(out=w12b[:, key, 0], in_=W["M1"][_p])
                    b.inc(i, "gl", 16)
                    i = sync.dma_start(out=w12b[:, key, 1], in_=W["M2"][_p])
                    b.inc(i, "gl", 16)
                    mm_w12_piece[key] = mm_tl_base + 4 * (_p + 1)
                gl_12[piece] = b.v("gl")

            ac_tl_base = b.v("ac")
            ev_tl_base = b.v("ev")

            @b.block.tensor
            def _(tensor, _xl=xl_g, _g=g):
                tensor.wait_ge(b.sem("xl"), _xl - (16 if _g == 0 else 0))
                for hc in range(32):
                    piece, wi = hc // 2, hc % 2
                    if wi == 0:
                        tensor.wait_ge(b.sem("gl"), gl_12[piece])
                    if hc >= 2:
                        tensor.wait_ge(b.sem("ev"), ev_tl_base + hc - 1)
                    pA = ma if hc % 2 == 0 else mc
                    pB = mb if hc % 2 == 0 else md
                    for dc in range(8):
                        i = tensor.matmul(pA[:], w12b[:, piece % 2, 0, wi, dc, :],
                                          xmg[:, dc, :], start=(dc == 0),
                                          stop=(dc == 7))
                    b.inc(i, "mm")
                    for dc in range(8):
                        i = tensor.matmul(pB[:], w12b[:, piece % 2, 1, wi, dc, :],
                                          xmg[:, dc, :], start=(dc == 0),
                                          stop=(dc == 7))
                    b.inc(i, "mm")
            mm_xm_used = b.v("mm")

            if fin_pend is not None:
                _fa_ev = fin_pend[0]()
                _fb_mm = fin_pend[1](_fa_ev)

            @b.block.scalar
            def _(scalar):
                for hc in range(32):
                    scalar.wait_ge(b.sem("mm"), mm_tl_base + 2 * hc + 1)
                    i = scalar.activation(tmp2k[:, hc % 2, :],
                                          (ma if hc % 2 == 0 else mc)[:], AF.Silu,
                                          scale=W["vms1"][:, hc:hc + 1])
                    b.inc(i, "ac")

            @b.block.vector
            def _(vector):
                for hc in range(32):
                    vector.wait_ge(b.sem("ac"), ac_tl_base + hc + 1)
                    vector.wait_ge(b.sem("mm"), mm_tl_base + 2 * hc + 2)
                    i = vector.scalar_tensor_tensor(
                        ham[:, hc, :], (mb if hc % 2 == 0 else md)[:],
                        W["vms2"][:, hc:hc + 1], tmp2k[:, hc % 2, :],
                        op0=ALU.mult, op1=ALU.mult)
                    b.inc(i, "ev")
            ev_ha = b.v("ev")

            if fin_pend is not None:
                _fb_ev = fin_pend[2](_fb_mm)

            def hid(wdram, scales, src, dst, ev_src_ready, q, lsem):
                mm_hid_base = b.v("mm")
                gl_h = {}
                for oc in range(32):
                    @getattr(b.block, q)
                    def _(eng, _oc=oc):
                        key = _oc % 3
                        b.w(eng, "mm", mm_wh_slab[key])
                        i = eng.dma_start(out=whb[:, key], in_=wdram[_oc])
                        b.inc(i, lsem, 16)
                        mm_wh_slab[key] = mm_hid_base + _oc + 1
                    gl_h[oc] = b.v(lsem)

                def hgrp(tensor, oc, _esr=ev_src_ready):
                    if oc == 0:
                        tensor.wait_ge(b.sem("ev"), _esr)
                    tensor.wait_ge(b.sem(lsem), gl_h[oc])
                    for hc in range(32):
                        i = tensor.matmul((mc if oc % 2 == 0 else md)[:],
                                          whb[:, oc % 3, hc, :], src[:, hc, :],
                                          start=(hc == 0), stop=(hc == 31))
                    return i

                def hevac(eng, oc):
                    return eng.activation(dst[:, oc, :],
                                          (mc if oc % 2 == 0 else md)[:], AF.Gelu,
                                          scale=scales[:, oc:oc + 1])

                return pipe(b, 32, 2, hgrp, hevac, "scalar")

            _, ac_h1 = hid(W["MH1"], W["vmsh1"], ham, hbm, ev_ha, "sync", "gl")

            if fin_pend is not None:
                fin_pend[3](_fb_ev)
                fin_pend = None

            @b.block.vector
            def _(vector, _a=ac_h1):
                vector.wait_ge(b.sem("ac"), _a)
                i = vector.engine_nop()
                b.inc(i, "ev")
            ev_h1 = b.v("ev")
            _, ac_h2 = hid(W["MH2"], W["vmsh2"], hbm, ham, ev_h1, "gpsimd", "g2")

            @b.block.vector
            def _(vector, _a=ac_h2):
                vector.wait_ge(b.sem("ac"), _a)
                i = vector.engine_nop()
                b.inc(i, "ev")
            ev_h2 = b.v("ev")

            mm_m3_base = b.v("mm")
            gl_3 = {}
            for oc in range(8):
                @b.block.sync
                def _(sync, _oc=oc):
                    b.w(sync, "mm",
                        st_["mm_w3_used"] if _oc < 2 else mm_m3_base + _oc - 1)
                    i = sync.dma_start(out=w3b[:, _oc % 2], in_=W["M3"][_oc])
                    b.inc(i, "gl", 16)
                gl_3[oc] = b.v("gl")

            def m3grp(tensor, oc, _e=ev_h2, _ef=st_["ev_fin_prev"]):
                if oc == 0:
                    tensor.wait_ge(b.sem("ev"), _e)
                    b.w(tensor, "ev", _ef)
                tensor.wait_ge(b.sem("gl"), gl_3[oc])
                for hc in range(32):
                    i = tensor.matmul((me if oc % 2 == 0 else mf)[:],
                                      w3b[:, oc % 2, hc, :], ham[:, hc, :],
                                      start=(hc == 0), stop=(hc == 31))
                return i

            def m3evac(eng, oc):
                return eng.activation(mot[:, oc, :],
                                      (me if oc % 2 == 0 else mf)[:], AF.Identity,
                                      scale=W["vms3"][:, oc:oc + 1])

            mm_m3, ac_mo = pipe(b, H, 2, m3grp, m3evac, "scalar")
            st_["mm_w3_used"] = mm_m3

            def fin_a(_ac_mo=ac_mo, _ea=st_["ev_apply_prev"]):
                @b.block.vector
                def _(vector):
                    vector.wait_ge(b.sem("ac"), _ac_mo)
                    b.w(vector, "ev", _ea)
                    i = vector.tensor_mul(sqf5[:], mot[:], mot[:])
                    b.inc(i, "ev")
                return b.v("ev")

            def fin_b1(ev_sq, _ac_mo=ac_mo, _pf=st_["ev_pfree_prev"]):
                @b.block.tensor
                def _(tensor):
                    tensor.wait_ge(b.sem("ac"), _ac_mo)
                    b.w(tensor, "ev", _pf)
                    for c in range(H):
                        tensor.matmul(me[0:1, :], ones_col[:], mot[:, c, :],
                                      start=(c == 0), stop=(c == H - 1))
                    tensor.wait_ge(b.sem("ev"), ev_sq)
                    for c in range(H):
                        i = tensor.matmul(mf[0:1, :], b.ones_colb[:],
                                          sqf5[:, c, :],
                                          start=(c == 0), stop=(c == H - 1))
                    b.inc(i, "mm")
                return b.v("mm")

            def fin_b2(mm_stats):
                ev_pfree = ln_smalls(b, sm, me, mf, float(D), mm_stats)
                st_["ev_pfree_prev"] = ev_pfree
                return b.v("ev")

            def fin_c(ev_mr, _g=g, _t0=t0, _xl1=st_["xl_x1"]):
                @b.block.tensor
                def _(tensor):
                    tensor.wait_ge(b.sem("ev"), ev_mr)
                    tensor.matmul(me[:], ones_row[:], sm["rstd"][0:1, :],
                                  start=True, stop=True)
                    i = tensor.matmul(mf[:], ones_row[:], sm["mr"][0:1, :],
                                      start=True, stop=True)
                    b.inc(i, "mm")
                mm_bco = b.v("mm")

                @b.block.vector
                def _(vector, _mb=mm_bco, _st=st_["so_prev"]):
                    vector.wait_ge(b.sem("mm"), _mb)
                    vector.wait_ge(b.sem("x2"), _xl1)
                    b.w(vector, "so", _st)
                    for c in range(H):
                        vector.tensor_mul(ap5[:, 0, :], mot[:, c, :], me[:])
                        vector.tensor_sub(ap5[:, 1, :], ap5[:, 0, :], mf[:])
                        i = vector.tensor_add(mot[:, c, :], ap5[:, 1, :],
                                              x1g[:, c, :])
                        b.inc(i, "ev")
                st_["ev_apply_prev"] = b.v("ev")
                st_["ev_fin_prev"] = b.v("ev")
                ev_o = b.v("ev")
                if _g + 1 < NG:
                    @b.block.sync
                    def _(sync, _e=ev_o, _t02=_t0):
                        sync.wait_ge(b.sem("ev"), _e)
                        i = sync.dma_start(
                            out=x1g[:],
                            in_=resid[:, :, _t02 + TG:_t02 + 2 * TG]
                            .rearrange("c p t -> p c t"))
                        b.inc(i, "x2", 16)
                    st_["xl_x1"] = b.v("x2")

                @b.block.scalar
                def _(scalar, _e=ev_o):
                    scalar.wait_ge(b.sem("ev"), _e)
                    i = scalar.dma_start(out=xout[:, :, _t0:_t0 + TG]
                                         .rearrange("c p t -> p c t"), in_=mot[:])
                    b.inc(i, "so", 16)
                st_["so_prev"] = b.v("so")

            fin_pend = (fin_a, fin_b1, fin_b2, fin_c)

        _fa_ev = fin_pend[0]()
        _fb_mm = fin_pend[1](_fa_ev)
        _fb_ev = fin_pend[2](_fb_mm)
        fin_pend[3](_fb_ev)

        @b.block.scalar
        def _(scalar):
            scalar.wait_ge(b.sem("so"), b.v("so"))
        b.barrier()



_CACHED = {}


def get_nc(tap=None):
    if tap not in _CACHED:
        _CACHED[tap] = build_program(tap)
    return _CACHED[tap]


def kernel(**inputs):
    nc = get_nc()
    in_maps = [prep_core_inputs(inputs, k) for k in range(NCORES)]
    res = run_bass_kernel_spmd(nc, in_maps, core_ids=list(range(NCORES)))
    outs = []
    for k in range(NCORES):
        o = np.asarray(res.results[k]["xout"])
        o = o.reshape(H, 128, BLOC, S).transpose(2, 3, 0, 1).reshape(BLOC, S, D)
        outs.append(o)
    return np.concatenate(outs, axis=0).astype(np.float32)


# version 08
# speedup vs baseline: 1.1071x; 1.0052x over previous
import numpy as np
import ml_dtypes
from contextlib import ExitStack

import concourse.bass as bass
import concourse.mybir as mybir
from concourse.bass_utils import run_bass_kernel_spmd

F32 = mybir.dt.float32
BF16 = mybir.dt.bfloat16
AF = mybir.ActivationFunctionType
ALU = mybir.AluOpType

H = 8
HD = 128
D = 1024
FFND = 4096
S = 2048
BLOC = 2
T = BLOC * S
TG = 512
NG = T // TG
EPS = 1e-5
NCORES = 8

BF16NP = ml_dtypes.bfloat16
LAG_GN = True
DBG = {}



def _perm_deint():
    return np.concatenate([np.arange(0, HD, 2), np.arange(1, HD, 2)])


def _xpos_tables():
    SCALE_BASE = 512
    scale_vec = (np.arange(0, HD, 2) + 0.4 * HD) / (1.4 * HD)
    power = (np.arange(S, dtype=np.float64) / SCALE_BASE)[:, None]
    scale = scale_vec[None, :] ** power
    half = HD // 2
    inv_freq = 1.0 / 10000.0 ** (np.arange(half, dtype=np.float64) / half)
    ang = np.arange(S, dtype=np.float64)[:, None] * inv_freq[None, :]
    sin, cos = np.sin(ang), np.cos(ang)

    def dupT(m):
        return np.concatenate([m.T, m.T], axis=0)

    cosq = dupT(cos * scale)
    sinq = dupT(sin * scale)
    cosk = dupT(cos / scale)
    sink = dupT(sin / scale)
    sinq[0:64] *= -1.0
    sink[0:64] *= -1.0
    return (cosq.astype(np.float32), sinq.astype(np.float32),
            cosk.astype(np.float32), sink.astype(np.float32))


def _decay_consts():
    gam = 1.0 - np.exp(np.linspace(np.log(1.0 / 32), np.log(1.0 / 512), H))
    j = np.arange(TG, dtype=np.float64)
    gdec = np.broadcast_to(gam[:, None, None] ** j[None, None, :],
                           (H, 128, TG)).astype(np.float32)
    p = np.arange(128, dtype=np.float64)
    mi = np.arange(16, dtype=np.float64)
    expo = 128.0 * (mi[None, None, :] - 3.0) - p[:, None, None]
    dvec = (gam[None, :, None] ** expo).astype(np.float32)
    dd = np.arange(4)[:, None, None]
    pp = np.arange(128)[None, :, None]
    jj = np.arange(TG)[None, None, :]
    masks = (jj >= 128 * dd + pp).astype(BF16NP)
    return gdec, dvec, masks


def _lhsT_slabs(w):
    I, O = w.shape
    return np.ascontiguousarray(
        w.reshape(I // 128, 128, O // 128, 128).transpose(2, 1, 0, 3))


def _pieces(slabs, k):
    NO, _, NI, _ = slabs.shape
    return np.ascontiguousarray(
        slabs.reshape(NO // k, k, 128, NI, 128).transpose(0, 2, 1, 3, 4))


def _vec_pc(v):
    n = v.shape[0] // 128
    return np.ascontiguousarray(v.reshape(n, 128).T)


def prep_core_inputs(inputs, core):
    f = lambda k: np.asarray(inputs[k], dtype=np.float32)
    X = f("L0_attn_X")
    xc = X[2 * core:2 * core + 2]
    xin = np.ascontiguousarray(
        xc.reshape(BLOC, S, H, 128).transpose(2, 3, 0, 1).reshape(H, 128, T))

    perm = _perm_deint()
    m = {"xin": xin}
    for i in range(2):
        WQ = f("L0_attn_WQ")[i][:, :, perm]
        WK = f("L0_attn_WK")[i][:, :, perm]
        WV = f("L0_attn_WV")[i]
        m[f"wq{i}"] = _lhsT_slabs(WQ.transpose(1, 0, 2).reshape(D, H * HD))
        m[f"wk{i}"] = _lhsT_slabs(WK.transpose(1, 0, 2).reshape(D, H * HD))
        m[f"wv{i}"] = np.ascontiguousarray(
            WV.transpose(1, 0, 2).reshape(D, H * HD).reshape(8, 128, 1024))
        m[f"wg{i}"] = _pieces(_lhsT_slabs(f("L0_attn_WG")[i]), 4)
        m[f"wo{i}"] = _pieces(_lhsT_slabs(f("L0_attn_WO")[i]), 4)
        m[f"fw1{i}"] = _pieces(_lhsT_slabs(f("L0_attn_ffn_w1")[i]), 4)
        m[f"fw2{i}"] = _lhsT_slabs(f("L0_attn_ffn_w2")[i])
        m[f"ln1w{i}"] = _vec_pc(f("L0_attn_ln1_w")[i])
        m[f"ln1b{i}"] = _vec_pc(f("L0_attn_ln1_b")[i])
        m[f"ln2w{i}"] = _vec_pc(f("L0_attn_ln2_w")[i])
        m[f"ln2b{i}"] = _vec_pc(f("L0_attn_ln2_b")[i])
        m[f"gnw{i}"] = _vec_pc(f("L0_attn_gn_w")[i])
        m[f"gnb{i}"] = _vec_pc(f("L0_attn_gn_b")[i])
        m[f"fb1{i}"] = _vec_pc(f("L0_attn_ffn_b1")[i])
        m[f"fb2{i}"] = _vec_pc(f("L0_attn_ffn_b2")[i])

    m["m1"] = _pieces(_lhsT_slabs(np.ascontiguousarray(f("L0_ffn_w1_w").T)), 2)
    m["m2"] = _pieces(_lhsT_slabs(np.ascontiguousarray(f("L0_ffn_w2_w").T)), 2)
    m["mh1"] = _lhsT_slabs(np.ascontiguousarray(f("L0_ffn_h1_w").T))
    m["mh2"] = _lhsT_slabs(np.ascontiguousarray(f("L0_ffn_h2_w").T))
    m["m3"] = _lhsT_slabs(np.ascontiguousarray(f("L0_ffn_w3_w").T))
    m["ms1"] = _vec_pc(f("L0_ffn_w1_s"))
    m["ms2"] = _vec_pc(f("L0_ffn_w2_s"))
    m["msh1"] = _vec_pc(f("L0_ffn_h1_s"))
    m["msh2"] = _vec_pc(f("L0_ffn_h2_s"))
    m["ms3"] = _vec_pc(f("L0_ffn_w3_s"))

    m["cq"], m["sq_"], m["ck"], m["sk_"] = _xpos_tables()
    m["gdec"], m["dvec"], m["masks"] = _decay_consts()
    return m



class B:
    def __init__(self, nc, block, stack):
        self.nc = nc
        self.block = block
        self.stack = stack
        self.sems = {}
        self.cnt = {}

    def mksem(self, *names):
        for n in names:
            self.sems[n] = self.stack.enter_context(self.nc.semaphore(n))
            self.cnt[n] = 0

    def inc(self, inst, name, by=1):
        inst.then_inc(self.sems[name], by)
        self.cnt[name] += by
        return self.cnt[name]

    def v(self, name):
        return self.cnt[name]

    def sem(self, name):
        return self.sems[name]

    def w(self, eng, name, val):
        if val and val > 0:
            eng.wait_ge(self.sems[name], val)

    def barrier(self, engines=("sync", "tensor", "vector", "scalar")):
        tgt = self.cnt["bar"] + len(engines)
        for e in engines:
            @getattr(self.block, e)
            def _(eng):
                eng.drain()
                eng.sem_inc(self.sems["bar"], 1)
        self.cnt["bar"] = tgt
        for e in engines:
            @getattr(self.block, e)
            def _(eng):
                eng.wait_ge(self.sems["bar"], tgt)


def pipe(b, n, nbanks, emit_group, emit_evac, evac_eng="scalar",
         evac_extra_wait=None):
    cn = {"scalar": "ac", "vector": "ev"}[evac_eng]
    mm_base = b.v("mm")
    ev_base = b.v(cn)

    @b.block.tensor
    def _(tensor):
        for gi in range(n):
            if gi >= nbanks:
                tensor.wait_ge(b.sem(cn), ev_base + gi - nbanks + 1)
            i = emit_group(tensor, gi)
            b.inc(i, "mm")

    @getattr(b.block, evac_eng)
    def _(eng):
        for gi in range(n):
            eng.wait_ge(b.sem("mm"), mm_base + gi + 1)
            if evac_extra_wait:
                evac_extra_wait(eng, gi)
            i = emit_evac(eng, gi)
            b.inc(i, cn)

    return b.v("mm"), b.v(cn)


def ln_smalls(b, sm, pS1, pS2, denom, mm_stats):
    @b.block.vector
    def _(vector):
        vector.wait_ge(b.sem("mm"), mm_stats)
        vector.tensor_scalar_mul(sm["mu"][:], pS1[0:1, :], 1.0 / denom)
        vector.tensor_scalar_mul(sm["aux"][:], pS2[0:1, :], 1.0 / denom)
        vector.tensor_mul(sm["m2"][:], sm["mu"][:], sm["mu"][:])
        i = vector.tensor_sub(sm["var"][:], sm["aux"][:], sm["m2"][:])
        b.inc(i, "ev")
    ev_small = b.v("ev")

    @b.block.scalar
    def _(scalar):
        scalar.wait_ge(b.sem("ev"), ev_small)
        scalar.activation(sm["aux"][:], sm["var"][:], AF.Ln, bias=sm["eps"][0:1, 0:1])
        i = scalar.activation(sm["rstd"][:], sm["aux"][:], AF.Exp,
                              scale=sm["nh"][0:1, 0:1])
        b.inc(i, "ac")
    ac_rstd = b.v("ac")

    @b.block.vector
    def _(vector):
        vector.wait_ge(b.sem("ac"), ac_rstd)
        i = vector.tensor_mul(sm["mr"][:], sm["mu"][:], sm["rstd"][:])
        b.inc(i, "ev")
    return ev_small


def ln_block(b, src_view, nchunk, sqbuf, pS1, pS2, pR, pMR, ones_col, ones_row,
             sm, denom, sq_on_vector=True, pre_stats_wait=None, pre_sq_wait=None,
             ones_col_b=None):
    if sq_on_vector:
        @b.block.vector
        def _(vector):
            if pre_sq_wait:
                pre_sq_wait(vector)
            i = vector.tensor_mul(sqbuf[:], src_view[:], src_view[:])
            b.inc(i, "ev")
        ev_sq = b.v("ev")

        @b.block.tensor
        def _(tensor):
            if pre_stats_wait:
                pre_stats_wait(tensor)
            for c in range(nchunk):
                tensor.matmul(pS1[0:1, :], ones_col[:], src_view[:, c, :],
                              start=(c == 0), stop=(c == nchunk - 1))
            tensor.wait_ge(b.sem("ev"), ev_sq)
            for c in range(nchunk):
                i = tensor.matmul(pS2[0:1, :], ones_col[:], sqbuf[:, c, :],
                                  start=(c == 0), stop=(c == nchunk - 1))
            b.inc(i, "mm")
    else:
        ac_base = b.v("ac")
        mm_base = b.v("mm")

        @b.block.scalar
        def _(scalar):
            if pre_sq_wait:
                pre_sq_wait(scalar)
            for c in range(nchunk):
                if c >= 2:
                    scalar.wait_ge(b.sem("mm"), mm_base + c - 1)
                i = scalar.activation(sqbuf[:, c % 2, :], src_view[:, c, :],
                                      AF.Square)
                b.inc(i, "ac")

        @b.block.tensor
        def _(tensor):
            if pre_stats_wait:
                pre_stats_wait(tensor)
            for c in range(nchunk):
                tensor.matmul(pS1[0:1, :], ones_col[:], src_view[:, c, :],
                              start=(c == 0), stop=(c == nchunk - 1))
            oc2 = ones_col_b if ones_col_b is not None else ones_col
            for c in range(nchunk):
                tensor.wait_ge(b.sem("ac"), ac_base + c + 1)
                i = tensor.matmul(pS2[0:1, :], oc2[:], sqbuf[:, c % 2, :],
                                  start=(c == 0), stop=(c == nchunk - 1))
                b.inc(i, "mm")
    mm_stats = b.v("mm")

    ev_pfree = ln_smalls(b, sm, pS1, pS2, denom, mm_stats)
    ev_mr = b.v("ev")

    @b.block.tensor
    def _(tensor):
        tensor.wait_ge(b.sem("ev"), ev_mr)
        tensor.matmul(pR[:], ones_row[:], sm["rstd"][0:1, :], start=True, stop=True)
        i = tensor.matmul(pMR[:], ones_row[:], sm["mr"][0:1, :], start=True, stop=True)
        b.inc(i, "mm")
    return b.v("mm"), ev_pfree


def emit_conversions(b, jobs, cvb):
    nq = [0]

    @b.block.gpsimd
    def _(gpsimd):
        for src, dst, key, N, M in jobs:
            nsub = (M + 2047) // 2048
            for n in range(N):
                for s_ in range(nsub):
                    w = min(2048, M - 2048 * s_)
                    i = nq[0]
                    par = i % 2
                    if i >= 2:
                        gpsimd.wait_ge(b.sem("cv"), 16 * (i - 1))
                    ii = gpsimd.dma_start(out=cvb[:, par, 0:w],
                                          in_=src[n][:, 2048 * s_:2048 * s_ + w])
                    b.inc(ii, "cvl", 16)
                    gpsimd.wait_ge(b.sem("cvl"), 16 * (i + 1))
                    ii = gpsimd.dma_start(out=dst[n][:, 2048 * s_:2048 * s_ + w],
                                          in_=cvb[:, par, 0:w])
                    b.inc(ii, "cv", 16)
                    nq[0] += 1
            b.cv_marks[key] = b.v("cv")



def build_program(tap=None):
    nc = bass.Bass("TRN2", target_bir_lowering=False)
    stack = ExitStack()

    def din(name, shape, dt=F32):
        return nc.dram_tensor(name, list(shape), dt, kind="ExternalInput").ap()

    xin = din("xin", [H, 128, T])
    WQd = [din(f"wq{i}", [8, 128, 8, 128]) for i in range(2)]
    WKd = [din(f"wk{i}", [8, 128, 8, 128]) for i in range(2)]
    WVd = [din(f"wv{i}", [8, 128, 1024]) for i in range(2)]
    WGd = [din(f"wg{i}", [2, 128, 4, 8, 128]) for i in range(2)]
    WOd = [din(f"wo{i}", [2, 128, 4, 8, 128]) for i in range(2)]
    FW1 = [din(f"fw1{i}", [8, 128, 4, 8, 128]) for i in range(2)]
    FW2 = [din(f"fw2{i}", [8, 128, 32, 128]) for i in range(2)]
    V8 = lambda nm: [din(f"{nm}{i}", [128, 8]) for i in range(2)]
    LN1W, LN1B, LN2W, LN2B = V8("ln1w"), V8("ln1b"), V8("ln2w"), V8("ln2b")
    GNW, GNB, FB2 = V8("gnw"), V8("gnb"), V8("fb2")
    FB1 = [din(f"fb1{i}", [128, 32]) for i in range(2)]
    M1 = din("m1", [16, 128, 2, 8, 128])
    M2 = din("m2", [16, 128, 2, 8, 128])
    MH1 = din("mh1", [32, 128, 32, 128])
    MH2 = din("mh2", [32, 128, 32, 128])
    M3 = din("m3", [8, 128, 32, 128])
    MS1, MS2 = din("ms1", [128, 32]), din("ms2", [128, 32])
    MSH1, MSH2 = din("msh1", [128, 32]), din("msh2", [128, 32])
    MS3 = din("ms3", [128, 8])
    CQ, SQ_ = din("cq", [128, S]), din("sq_", [128, S])
    CK, SK_ = din("ck", [128, S]), din("sk_", [128, S])
    GDEC = din("gdec", [H, 128, TG])
    DVEC = din("dvec", [128, 8, 16])
    MASKS = din("masks", [4, 128, TG], BF16)

    xout = nc.dram_tensor("xout", [H, 128, T], F32, kind="ExternalOutput").ap()
    skind = "ExternalOutput" if tap else "Internal"
    CVshapes = {
        "fw1_0": [8, 128, 4, 8, 128], "fw2_0": [8, 128, 32, 128],
        "fw1_1": [8, 128, 4, 8, 128], "fw2_1": [8, 128, 32, 128],
        "m1": [16, 128, 2, 8, 128], "m2": [16, 128, 2, 8, 128],
        "mh1": [32, 128, 32, 128], "mh2": [32, 128, 32, 128],
        "m3": [8, 128, 32, 128],
    }
    CV = {k: nc.dram_tensor("cv_" + k, s, BF16, kind="Internal").ap()
          for k, s in CVshapes.items()}
    global HTAP
    HTAP = nc.dram_tensor("htap", [NG, 128, 32, TG], BF16, kind=skind).ap() \
        if tap else None
    if DBG:
        DBG["dq"] = nc.dram_tensor("dbg_q", [128, S], BF16, kind=skind).ap()
        DBG["dk"] = nc.dram_tensor("dbg_k", [128, S], BF16, kind=skind).ap()
        DBG["dv"] = nc.dram_tensor("dbg_v", [128, 16, 256], BF16, kind=skind).ap()
        DBG["dyt"] = nc.dram_tensor("dbg_yt", [128, TG], F32, kind=skind).ap()
        DBG["dsc"] = nc.dram_tensor("dbg_sc", [128, 16, TG], BF16, kind=skind).ap()
    resid = nc.dram_tensor("resid", [H, 128, T], F32, kind=skind).ap()
    xnb = nc.dram_tensor("xnb", [H, 128, T], BF16, kind=skind).ap()
    ybuf = nc.dram_tensor("ybuf", [H, 128, T], BF16, kind=skind).ap()

    sb = lambda name, shape, dt=F32: stack.enter_context(
        nc.sbuf_tensor(name, list(shape), dt))
    ones_col = sb("ones_col", [128, 1])
    ones_colb = sb("ones_colb", [128, 1], BF16)
    ones_row = sb("ones_row", [1, 128])
    vln1w = [sb(f"vln1w{i}", [128, 8]) for i in range(2)]
    vln1b = [sb(f"vln1b{i}", [128, 8]) for i in range(2)]
    vln2w = [sb(f"vln2w{i}", [128, 8]) for i in range(2)]
    vln2b = [sb(f"vln2b{i}", [128, 8]) for i in range(2)]
    vgnw = [sb(f"vgnw{i}", [128, 8]) for i in range(2)]
    vgnb = [sb(f"vgnb{i}", [128, 8]) for i in range(2)]
    vfb1 = [sb(f"vfb1{i}", [128, 32]) for i in range(2)]
    vfb2 = [sb(f"vfb2{i}", [128, 8]) for i in range(2)]
    vms1, vms2 = sb("vms1", [128, 32]), sb("vms2", [128, 32])
    vmsh1, vmsh2 = sb("vmsh1", [128, 32]), sb("vmsh2", [128, 32])
    vms3 = sb("vms3", [128, 8])
    vdvec = sb("vdvec", [128, 8, 16])
    sm = {k: sb("sm_" + k, [1, TG]) for k in ("mu", "aux", "m2", "var", "rstd", "mr")}
    sm["eps"] = sb("sm_eps", [1, 1])
    sm["nh"] = sb("sm_nh", [1, 1])
    cvb = sb("cvb", [128, 2, 2048], BF16)

    with nc.Block() as block:
        b = B(nc, block, stack)
        b.ones_colb = ones_colb
        b.mksem("cl", "xl", "wl", "gl", "mm", "ev", "ac", "st", "so", "bar",
                "cvl", "cv", "g2", "x2")

        @block.sync
        def _(sync):
            pairs = [(MS1, vms1), (MS2, vms2), (MSH1, vmsh1), (MSH2, vmsh2),
                     (MS3, vms3), (DVEC, vdvec)]
            for i in range(2):
                pairs += [(LN1W[i], vln1w[i]), (LN1B[i], vln1b[i]),
                          (LN2W[i], vln2w[i]), (LN2B[i], vln2b[i]),
                          (GNW[i], vgnw[i]), (GNB[i], vgnb[i]),
                          (FB1[i], vfb1[i]), (FB2[i], vfb2[i])]
            for src, dst in pairs:
                b.inc(sync.dma_start(out=dst[:], in_=src), "cl", 16)
        cl_consts = b.v("cl")

        @block.vector
        def _(vector):
            vector.memset(ones_col[:], 1.0)
            vector.memset(ones_colb[:], 1.0)
            vector.memset(ones_row[:], 1.0)
            vector.memset(sm["eps"][:], EPS)
            i = vector.memset(sm["nh"][:], -0.5)
            b.inc(i, "ev")
        ev_ones = b.v("ev")

        for e in ("tensor", "scalar"):
            @getattr(block, e)
            def _(eng):
                eng.wait_ge(b.sem("ev"), ev_ones)
                eng.wait_ge(b.sem("cl"), cl_consts)

        b.cv_marks = {}
        cv_jobs = []

        def flat(ap, N, M):
            return ap.rearrange("a p " + " ".join(
                f"d{i}" for i in range(len(ap.shape) - 2)) + " -> a p (" + " ".join(
                f"d{i}" for i in range(len(ap.shape) - 2)) + ")")

        for key, fsrc in (("fw1_0", FW1[0]), ("fw2_0", FW2[0]),
                          ("fw1_1", FW1[1]), ("fw2_1", FW2[1]),
                          ("m1", M1), ("m2", M2), ("mh1", MH1), ("mh2", MH2),
                          ("m3", M3)):
            shp = CVshapes[key]
            N = shp[0]
            M = int(np.prod(shp[2:]))
            cv_jobs.append((flat(fsrc, N, M), flat(CV[key], N, M), key, N, M))

        with nc.named_scope("p0"):
            _p0(b, nc, xin, resid, xnb, vln1w[0], vln1b[0], ones_col, ones_row, sm)
            emit_conversions(b, cv_jobs, cvb)
        if tap == "p0":
            _finish(b)
            stack.close()
            return nc

        consts = dict(CQ=CQ, SQ_=SQ_, CK=CK, SK_=SK_, GDEC=GDEC, MASKS=MASKS,
                      vdvec=vdvec, resid=resid, xnb=xnb, ybuf=ybuf)
        for li in range(2):
            Wl = dict(wq=WQd[li], wk=WKd[li], wv=WVd[li], wg=WGd[li], wo=WOd[li],
                      fw1=CV[f"fw1_{li}"], fw2=CV[f"fw2_{li}"],
                      cv_fw1=f"fw1_{li}", cv_fw2=f"fw2_{li}",
                      vln2w=vln2w[li], vln2b=vln2b[li],
                      vgnw=vgnw[li], vgnb=vgnb[li],
                      vfb1=vfb1[li], vfb2=vfb2[li],
                      vln1w_next=vln1w[1], vln1b_next=vln1b[1])
            with nc.named_scope(f"p1_{li}"):
                _p1_retention(b, li, nc, Wl, consts, ones_col, ones_row, sm)
            if tap == f"r{li}":
                _finish(b)
                stack.close()
                return nc
            with nc.named_scope(f"p3_{li}"):
                _p3_woffn(b, li, nc, Wl, consts, ones_col, ones_row, sm)
            if tap == f"l{li}":
                _finish(b)
                stack.close()
                return nc

        with nc.named_scope("p4"):
            _p4_mid(b, nc, xin, resid, xnb, ones_col, ones_row, sm)
        if tap == "p4":
            _finish(b)
            stack.close()
            return nc

        Wm = dict(M1=CV["m1"], M2=CV["m2"], MH1=CV["mh1"], MH2=CV["mh2"],
                  M3=CV["m3"],
                  vms1=vms1, vms2=vms2, vmsh1=vmsh1, vmsh2=vmsh2, vms3=vms3,
                  resid=resid, xnb=xnb, xout=xout)
        with nc.named_scope("p5"):
            _p5_mlp(b, nc, Wm, ones_col, ones_row, sm)
        _finish(b)

    stack.close()
    return nc


def _finish(b):
    for e in ("scalar", "sync"):
        @getattr(b.block, e)
        def _(eng):
            eng.wait_ge(b.sem("st"), b.v("st"))
            eng.wait_ge(b.sem("so"), b.v("so"))



def _p0(b, nc, xin, resid, xnb, vw, vb, ones_col, ones_row, sm):
    with ExitStack() as ps_:
        e_ = ps_.enter_context
        xg2 = e_(nc.sbuf_tensor("xg2", [128, 2, H, TG], F32))
        h0t = e_(nc.sbuf_tensor("h0t", [128, H, TG], F32))
        sqg = e_(nc.sbuf_tensor("sqg", [128, H, TG], F32))
        xnt = e_(nc.sbuf_tensor("xnt", [128, H, TG], BF16))
        pS1 = e_(nc.psum_tensor("pS1", [128, 512], F32))
        pS2 = e_(nc.psum_tensor("pS2", [128, 512], F32))
        pR = e_(nc.psum_tensor("pR", [128, 512], F32))
        pMR = e_(nc.psum_tensor("pMR", [128, 512], F32))

        ev_xg_free = {}
        ev_pfree_prev = 0
        st_h0_prev = 0
        st_xn_prev = 0
        ac_xn_prev = 0
        for g in range(NG):
            t0 = g * TG

            @b.block.sync
            def _(sync, _g=g, _t0=t0):
                b.w(sync, "ev", ev_xg_free.get(_g - 2))
                i = sync.dma_start(out=xg2[:, _g % 2],
                                   in_=xin[:, :, _t0:_t0 + TG].rearrange("c p t -> p c t"))
                b.inc(i, "xl", 16)
            xl_g = b.v("xl")
            xg = xg2[:, g % 2]

            pq1 = (lambda _pf=ev_pfree_prev, _xl=xl_g: (lambda tensor: (
                tensor.wait_ge(b.sem("xl"), _xl), b.w(tensor, "ev", _pf))))()
            ps1 = (lambda _xl=xl_g, _ac=ac_xn_prev: (lambda vector: (
                vector.wait_ge(b.sem("xl"), _xl), b.w(vector, "ac", _ac))))()
            mm_bc1, ev_pfree = ln_block(b, xg, H, sqg, pS1, pS2, pR, pMR,
                                        ones_col, ones_row, sm, float(D),
                                        pre_stats_wait=pq1, pre_sq_wait=ps1)

            @b.block.vector
            def _(vector, _mb=mm_bc1, _st=st_h0_prev):
                vector.wait_ge(b.sem("mm"), _mb)
                b.w(vector, "st", _st)
                for c in range(H):
                    vector.tensor_mul(h0t[:, c, :], xg[:, c, :], pR[:])
                    i = vector.tensor_sub(h0t[:, c, :], h0t[:, c, :], pMR[:])
                b.inc(i, "ev")
            ev_xg_free[g] = b.v("ev")
            ev_apply1 = b.v("ev")

            pq2 = (lambda _pf=ev_pfree, _e1=ev_apply1: (lambda tensor: (
                tensor.wait_ge(b.sem("ev"), _e1), b.w(tensor, "ev", _pf))))()
            mm_bc2, ev_pfree2 = ln_block(b, h0t, H, sqg, pS1, pS2, pR, pMR,
                                         ones_col, ones_row, sm, float(D),
                                         pre_stats_wait=pq2)
            ev_pfree_prev = ev_pfree2

            @b.block.vector
            def _(vector, _mb=mm_bc2):
                vector.wait_ge(b.sem("mm"), _mb)
                for c in range(H):
                    vector.tensor_mul(sqg[:, c, :], h0t[:, c, :], pR[:])
                    i = vector.tensor_sub(sqg[:, c, :], sqg[:, c, :], pMR[:])
                b.inc(i, "ev")
            ev_u = b.v("ev")

            @b.block.scalar
            def _(scalar, _e=ev_u, _st=st_xn_prev):
                scalar.wait_ge(b.sem("ev"), _e)
                b.w(scalar, "st", _st)
                for c in range(H):
                    i = scalar.activation(xnt[:, c, :], sqg[:, c, :], AF.Identity,
                                          bias=vb[:, c:c + 1], scale=vw[:, c:c + 1])
                b.inc(i, "ac")
            ac_xn_prev = b.v("ac")
            ac_xn = b.v("ac")

            @b.block.scalar
            def _(scalar, _t0=t0, _ac=ac_xn):
                i = scalar.dma_start(out=resid[:, :, _t0:_t0 + TG]
                                     .rearrange("c p t -> p c t"), in_=h0t[:])
                b.inc(i, "st", 16)
                scalar.wait_ge(b.sem("ac"), _ac)
                i = scalar.dma_start(out=xnb[:, :, _t0:_t0 + TG]
                                     .rearrange("c p t -> p c t"), in_=xnt[:])
                b.inc(i, "st", 16)
            st_h0_prev = b.v("st") - 16
            st_xn_prev = b.v("st")

        @b.block.scalar
        def _(scalar):
            scalar.wait_ge(b.sem("st"), b.v("st"))
        b.barrier()



def _p1_retention(b, li, nc, W, C, ones_col, ones_row, sm):
    with ExitStack() as ps_:
        e_ = ps_.enter_context
        tcq = e_(nc.sbuf_tensor(f"tcq{li}", [128, S], F32))
        tsq = e_(nc.sbuf_tensor(f"tsq{li}", [128, S], F32))
        tck = e_(nc.sbuf_tensor(f"tck{li}", [128, S], F32))
        tsk = e_(nc.sbuf_tensor(f"tsk{li}", [128, S], F32))
        tmsk = e_(nc.sbuf_tensor(f"tmsk{li}", [128, 4, TG], BF16))
        tgd = e_(nc.sbuf_tensor(f"tgd{li}", [128, 2, TG], F32))
        xnh = e_(nc.sbuf_tensor(f"xnh{li}", [128, H, S], BF16))
        wvf = e_(nc.sbuf_tensor(f"wvf{li}", [128, 1024], F32))
        wvb = e_(nc.sbuf_tensor(f"wvb{li}", [128, 8, 1024], BF16))
        wqkf = e_(nc.sbuf_tensor(f"wqkf{li}", [128, 2, 8, 128], F32))
        wqkb = e_(nc.sbuf_tensor(f"wqkb{li}", [128, 2, 2, 8, 128], BF16))
        vsb = e_(nc.sbuf_tensor(f"vsb{li}", [128, 16, 256], BF16))
        qsb = e_(nc.sbuf_tensor(f"qsb{li}", [128, S], BF16))
        ksb = e_(nc.sbuf_tensor(f"ksb{li}", [128, S], BF16))
        scsb = e_(nc.sbuf_tensor(f"scsb{li}", [128, 16, TG], BF16))
        xt1 = e_(nc.sbuf_tensor(f"xt1{li}", [128, TG], F32))
        xt2 = e_(nc.sbuf_tensor(f"xt2{li}", [128, TG], F32))
        ytl2 = e_(nc.sbuf_tensor(f"ytl2{li}", [128, 2, TG], F32))
        sq12 = e_(nc.sbuf_tensor(f"sq12{li}", [128, 2, TG], F32))
        gxt = e_(nc.sbuf_tensor(f"gxt{li}", [128, TG], F32))
        ygn2 = e_(nc.sbuf_tensor(f"ygn2{li}", [128, 2, S], BF16))
        pq0 = e_(nc.psum_tensor(f"pq0{li}", [128, 512], F32))
        pq1 = e_(nc.psum_tensor(f"pq1{li}", [128, 512], F32))
        pq2 = e_(nc.psum_tensor(f"pq2{li}", [128, 512], F32))
        psc0 = e_(nc.psum_tensor(f"psc0{li}", [128, 512], F32))
        psc1 = e_(nc.psum_tensor(f"psc1{li}", [128, 512], F32))
        pyh = e_(nc.psum_tensor(f"pyh{li}", [128, 512], F32))
        pgA = e_(nc.psum_tensor(f"pgA{li}", [128, 512], F32))
        pgB = e_(nc.psum_tensor(f"pgB{li}", [128, 512], F32))
        qk_banks = [pq0, pq1, pq2]
        sc_banks = [psc0, psc1, pq0]

        @b.block.sync
        def _(sync):
            for src, dst in ((C["CQ"], tcq), (C["SQ_"], tsq),
                             (C["CK"], tck), (C["SK_"], tsk)):
                b.inc(sync.dma_start(out=dst[:], in_=src), "cl", 16)
            i = sync.dma_start(out=tmsk[:], in_=C["MASKS"].rearrange("d p t -> p d t"))
            b.inc(i, "cl", 16)
        cl_tab = b.v("cl")

        ev_wv_prev = 0
        for dc in range(8):
            @b.block.sync
            def _(sync, _dc=dc, _ev=ev_wv_prev):
                b.w(sync, "ev", _ev)
                i = sync.dma_start(out=wvf[:], in_=W["wv"][_dc])
                b.inc(i, "wl", 16)
            wl_v = b.v("wl")

            @b.block.vector
            def _(vector, _dc=dc, _wl=wl_v):
                vector.wait_ge(b.sem("wl"), _wl)
                i = vector.tensor_copy(wvb[:, _dc, :], wvf[:])
                b.inc(i, "ev")
            ev_wv_prev = b.v("ev")
        ev_wv = b.v("ev")

        st = {"mm_last_v_use": 0,
              "ev_wqkf_prev": 0,
              "ev_ytl_prev": 0,
              "ev_apply_prev": 0,
              "ac_affine_prev": 0,
              "ev_mr": 0,
              "st_ygn": {0: 0, 1: 0},
              "ev_tgd_free": {0: 0, 1: 0},
              "psc_free": {0: None, 1: None, 2: None},
              "pend": None}

        def make_pend(_h, _sc, _slot, _acsq, _s0, _tb0):
            def stats():
                @b.block.tensor
                def _(tensor):
                    tensor.wait_ge(b.sem("ac"), _acsq)
                    b.w(tensor, "ev", st["ev_apply_prev"])
                    tensor.matmul(pgA[0:1, :], ones_col[:], ytl2[:, _slot, :],
                                  start=True, stop=True)
                    i = tensor.matmul(pgB[0:1, :], ones_col[:], sq12[:, _slot, :],
                                      start=True, stop=True)
                    b.inc(i, "mm")
                ln_smalls(b, sm, pgA, pgB, float(HD), b.v("mm"))
                st["ev_mr"] = b.v("ev")

            def apply():
                @b.block.tensor
                def _(tensor):
                    tensor.wait_ge(b.sem("ev"), st["ev_mr"])
                    tensor.matmul(pgA[:], ones_row[:], sm["rstd"][0:1, :],
                                  start=True, stop=True)
                    i = tensor.matmul(pgB[:], ones_row[:], sm["mr"][0:1, :],
                                      start=True, stop=True)
                    b.inc(i, "mm")
                mm_bc = b.v("mm")

                @b.block.vector
                def _(vector):
                    vector.wait_ge(b.sem("mm"), mm_bc)
                    b.w(vector, "ac", st["ac_affine_prev"])
                    vector.tensor_mul(gxt[:], ytl2[:, _slot, :], pgA[:])
                    i = vector.tensor_sub(gxt[:], gxt[:], pgB[:])
                    b.inc(i, "ev")
                st["ev_apply_prev"] = b.v("ev")
                ev_ap = b.v("ev")
                ybs = _h % 2

                @b.block.scalar
                def _(scalar):
                    scalar.wait_ge(b.sem("ev"), ev_ap)
                    if _sc == 0:
                        b.w(scalar, "st", st["st_ygn"][ybs])
                    i = scalar.activation(ygn2[:, ybs, _s0:_s0 + TG], gxt[:],
                                          AF.Identity,
                                          bias=W["vgnb"][:, _h:_h + 1],
                                          scale=W["vgnw"][:, _h:_h + 1])
                    b.inc(i, "ac")
                    if _sc == 3:
                        scalar.wait_ge(b.sem("ac"), b.v("ac"))
                        i = scalar.dma_start(out=C["ybuf"][_h][:, _tb0:_tb0 + S],
                                             in_=ygn2[:, ybs, :])
                        b.inc(i, "st", 16)
                st["ac_affine_prev"] = b.v("ac")
                if _sc == 3:
                    st["st_ygn"][ybs] = b.v("st")

            return {"stats": stats, "apply": apply}

        def flush_stats():
            if st["pend"] is not None:
                st["pend"]["stats"]()

        def flush_apply():
            if st["pend"] is not None:
                st["pend"]["apply"]()
                st["pend"] = None

        for bb in range(BLOC):
            tb0 = bb * S

            @b.block.sync
            def _(sync, _tb0=tb0):
                sync.wait_ge(b.sem("st"), b.v("st"))
                b.w(sync, "mm", b.v("mm"))
                i = sync.dma_start(out=xnh[:], in_=C["xnb"][:, :, _tb0:_tb0 + S]
                                   .rearrange("c p t -> p c t"))
                b.inc(i, "xl", 16)
            xl_xn = b.v("xl")

            for hg in range(4):
                ev_pre, ac_pre = b.v("ev"), b.v("ac")
                flush_stats()

                def vgrp(tensor, tt, _hg=hg, _xl=xl_xn, _ep=ev_pre, _ap=ac_pre):
                    if tt == 0:
                        tensor.wait_ge(b.sem("xl"), _xl)
                        b.w(tensor, "ev", max(_ep, ev_wv))
                        b.w(tensor, "ac", _ap)
                    for dc in range(8):
                        i = tensor.matmul(qk_banks[tt % 3][:, 0:256],
                                          xnh[:, dc, tt * 128:(tt + 1) * 128],
                                          wvb[:, dc, 256 * _hg:256 * (_hg + 1)],
                                          start=(dc == 0), stop=(dc == 7))
                    return i

                def vevac(eng, tt, _mv=st["mm_last_v_use"]):
                    if tt == 0:
                        b.w(eng, "mm", _mv)
                    return eng.activation(vsb[:, tt, :], qk_banks[tt % 3][:, 0:256],
                                          AF.Copy)

                _, ac_v = pipe(b, 16, 3, vgrp, vevac, "scalar")
                if DBG and li == 0 and bb == 0 and hg == 0:
                    @b.block.sync
                    def _(sync, _a=ac_v):
                        sync.wait_ge(b.sem("ac"), _a)
                        b.inc(sync.dma_start(out=DBG["dv"], in_=vsb[:]), "st", 16)
                flush_apply()

                for h in (2 * hg, 2 * hg + 1):
                    hcol = h - 2 * hg
                    par = h % 2

                    @b.block.sync
                    def _(sync, _par=par, _h=h, _ewf=st["ev_wqkf_prev"],
                          _egs=st["ev_tgd_free"][par]):
                        b.w(sync, "ev", _ewf)
                        b.inc(sync.dma_start(out=wqkf[:, 0], in_=W["wq"][_h]), "wl", 16)
                        b.inc(sync.dma_start(out=wqkf[:, 1], in_=W["wk"][_h]), "wl", 16)
                        b.w(sync, "ev", _egs)
                        b.inc(sync.dma_start(out=tgd[:, _par], in_=C["GDEC"][_h]),
                              "cl", 16)
                    wl_qk = b.v("wl")
                    cl_gd = b.v("cl")

                    @b.block.vector
                    def _(vector, _par=par, _wl=wl_qk):
                        vector.wait_ge(b.sem("wl"), _wl)
                        i = vector.tensor_copy(wqkb[:, _par], wqkf[:])
                        b.inc(i, "ev")
                    ev_qk_w = b.v("ev")
                    st["ev_wqkf_prev"] = ev_qk_w

                    ev_pre2, ac_pre2 = b.v("ev"), b.v("ac")

                    def qkgrp(tensor, tt, _par=par, _evw=ev_qk_w,
                              _ep=ev_pre2, _ap=ac_pre2):
                        qk, gc = tt // 4, tt % 4
                        if tt == 0:
                            tensor.wait_ge(b.sem("ev"), max(_evw, _ep))
                            tensor.wait_ge(b.sem("cl"), cl_tab)
                            b.w(tensor, "ac", _ap)
                        for dc in range(8):
                            i = tensor.matmul(qk_banks[tt % 3][:],
                                              wqkb[:, _par, qk, dc, :],
                                              xnh[:, dc, gc * TG:(gc + 1) * TG],
                                              start=(dc == 0), stop=(dc == 7))
                        return i

                    def qkevac(eng, tt):
                        qk, gc = tt // 4, tt % 4
                        dst = qsb if qk == 0 else ksb
                        ct = tcq if qk == 0 else tck
                        stb = tsq if qk == 0 else tsk
                        sl = slice(gc * TG, (gc + 1) * TG)
                        ps = qk_banks[tt % 3]
                        eng.tensor_mul(xt1[:], ps[:], ct[:, sl])
                        eng.tensor_mul(xt2[0:64, :], ps[64:128, :], stb[0:64, sl])
                        eng.tensor_mul(xt2[64:128, :], ps[0:64, :], stb[64:128, sl])
                        return eng.tensor_add(dst[:, sl], xt1[:], xt2[:])

                    ev_qk_base = b.v("ev")
                    pipe(b, 8, 3, qkgrp, qkevac, "vector")
                    ev_qk_done = b.v("ev")
                    if DBG and li == 0 and bb == 0 and h == 0:
                        @b.block.sync
                        def _(sync, _e=ev_qk_done):
                            sync.wait_ge(b.sem("ev"), _e)
                            b.inc(sync.dma_start(out=DBG["dq"], in_=qsb[:]), "st", 16)
                            b.inc(sync.dma_start(out=DBG["dk"], in_=ksb[:]), "st", 16)
                    st["psc_free"][2] = ("ev", ev_qk_base + 7)
                    flush_stats()
                    flush_apply()

                    for sc in range(4):
                        s0 = sc * TG
                        ntj = 4 * sc + 4
                        slot = sc % 2

                        for tj in range(ntj):
                            bank = tj % 3

                            @b.block.tensor
                            def _(tensor, _tj=tj, _bank=bank, _s0=s0,
                                  _eq=ev_qk_done, _fr=st["psc_free"][bank]):
                                if _tj == 0 and _s0 == 0:
                                    tensor.wait_ge(b.sem("ev"), _eq)
                                if _fr is not None:
                                    tensor.wait_ge(b.sem(_fr[0]), _fr[1])
                                i = tensor.matmul(sc_banks[_bank][:],
                                                  ksb[:, _tj * 128:(_tj + 1) * 128],
                                                  qsb[:, _s0:_s0 + TG],
                                                  start=True, stop=True)
                                b.inc(i, "mm")
                            mmv = b.v("mm")

                            if tj >= 4 * sc:
                                d = tj - 4 * sc

                                @b.block.vector
                                def _(vector, _tj=tj, _bank=bank, _d=d, _h=h,
                                      _mm=mmv):
                                    vector.wait_ge(b.sem("mm"), _mm)
                                    i = vector.scalar_tensor_tensor(
                                        scsb[:, _tj, :], sc_banks[_bank][:],
                                        C["vdvec"][:, _h, 3 - _d:4 - _d],
                                        tmsk[:, _d, :],
                                        op0=ALU.mult, op1=ALU.mult)
                                    b.inc(i, "ev")
                                st["psc_free"][bank] = ("ev", b.v("ev"))
                            else:
                                mi = 4 * sc - tj + 3

                                @b.block.scalar
                                def _(scalar, _tj=tj, _bank=bank, _h=h,
                                      _mi=mi, _mm=mmv):
                                    scalar.wait_ge(b.sem("mm"), _mm)
                                    i = scalar.activation(
                                        scsb[:, _tj, :], sc_banks[_bank][:],
                                        AF.Identity,
                                        scale=C["vdvec"][:, _h, _mi:_mi + 1])
                                    b.inc(i, "ac")
                                st["psc_free"][bank] = ("ac", b.v("ac"))

                        sc_ac, sc_ev = b.v("ac"), b.v("ev")
                        if sc >= 1:
                            flush_stats()

                        @b.block.tensor
                        def _(tensor, _hcol=hcol, _ntj=ntj, _a=sc_ac, _e=sc_ev,
                              _ey=st["ev_ytl_prev"]):
                            tensor.wait_ge(b.sem("ac"), _a)
                            tensor.wait_ge(b.sem("ev"), _e)
                            b.w(tensor, "ev", _ey)
                            for tj in range(_ntj):
                                i = tensor.matmul(
                                    pyh[:], vsb[:, tj, _hcol * 128:(_hcol + 1) * 128],
                                    scsb[:, tj, :],
                                    start=(tj == 0), stop=(tj == _ntj - 1))
                            b.inc(i, "mm")
                        mm_yh = b.v("mm")
                        st["mm_last_v_use"] = mm_yh

                        @b.block.vector
                        def _(vector, _par=par, _slot=slot, _my=mm_yh, _cg=cl_gd):
                            vector.wait_ge(b.sem("mm"), _my)
                            vector.wait_ge(b.sem("cl"), _cg)
                            i = vector.tensor_mul(ytl2[:, _slot, :], pyh[:],
                                                  tgd[:, _par])
                            b.inc(i, "ev")
                        st["ev_ytl_prev"] = b.v("ev")
                        if sc == 3:
                            st["ev_tgd_free"][par] = b.v("ev")
                        ev_yt = b.v("ev")
                        if DBG and li == 0 and bb == 0 and h == 0 and sc == 3:
                            @b.block.sync
                            def _(sync, _e=ev_yt, _a=b.v("ac"), _slot=slot):
                                sync.wait_ge(b.sem("ev"), _e)
                                sync.wait_ge(b.sem("ac"), _a)
                                b.inc(sync.dma_start(out=DBG["dyt"],
                                                     in_=ytl2[:, _slot, :]), "st", 16)
                                b.inc(sync.dma_start(out=DBG["dsc"],
                                                     in_=scsb[:]), "st", 16)

                        @b.block.scalar
                        def _(scalar, _slot=slot, _e=ev_yt):
                            scalar.wait_ge(b.sem("ev"), _e)
                            i = scalar.activation(sq12[:, _slot, :],
                                                  ytl2[:, _slot, :], AF.Square)
                            b.inc(i, "ac")
                        ac_sq = b.v("ac")

                        if sc >= 1:
                            flush_apply()
                        st["pend"] = make_pend(h, sc, slot, ac_sq, s0, tb0)
                        if not LAG_GN:
                            flush_stats()
                            flush_apply()

        flush_stats()
        flush_apply()

        @b.block.scalar
        def _(scalar):
            scalar.wait_ge(b.sem("st"), b.v("st"))
        b.barrier()



def _p3_woffn(b, li, nc, W, C, ones_col, ones_row, sm):
    resid, xnb, ybuf = C["resid"], C["xnb"], C["ybuf"]
    last = (li == 1)
    with nc.sbuf_tensor(f"wgo{li}", [128, 2, 8, 8, 128], BF16) as wgo:
        with nc.sbuf_tensor(f"wgof{li}", [128, 4, 8, 128], F32) as wgof:
            ev_cv_prev = 0
            for k in range(2):
                wsrc = W["wg"] if k == 0 else W["wo"]
                for hf in range(2):
                    @b.block.sync
                    def _(sync, _hf=hf, _w=wsrc, _ev=ev_cv_prev):
                        b.w(sync, "ev", _ev)
                        i = sync.dma_start(out=wgof[:], in_=_w[_hf])
                        b.inc(i, "wl", 16)
                    wl_w = b.v("wl")

                    @b.block.vector
                    def _(vector, _k=k, _hf=hf, _wl=wl_w):
                        vector.wait_ge(b.sem("wl"), _wl)
                        i = vector.tensor_copy(wgo[:, _k, 4 * _hf:4 * _hf + 4],
                                               wgof[:])
                        b.inc(i, "ev")
                    ev_cv_prev = b.v("ev")
            ev_wgo = b.v("ev")

        with ExitStack() as ps_:
            e_ = ps_.enter_context
            xng2 = e_(nc.sbuf_tensor(f"xng2{li}", [128, 2, H, TG], BF16))
            ybg2 = e_(nc.sbuf_tensor(f"ybg2{li}", [128, 2, H, TG], BF16))
            xrg = e_(nc.sbuf_tensor(f"xrg{li}", [128, H, TG], F32))
            gg = e_(nc.sbuf_tensor(f"gg{li}", [128, H, TG], BF16))
            ysb3 = e_(nc.sbuf_tensor(f"ysb3{li}", [128, H, TG], F32))
            ynt = e_(nc.sbuf_tensor(f"ynt{li}", [128, H, TG], BF16))
            hh = e_(nc.sbuf_tensor(f"hh{li}", [128, 32, TG], BF16))
            sq3 = e_(nc.sbuf_tensor(f"sq3{li}", [128, 2, TG], BF16))
            ap3 = e_(nc.sbuf_tensor(f"ap3{li}", [128, 2, TG], F32))
            f1b = e_(nc.sbuf_tensor(f"f1b{li}", [128, 2, 4, 8, 128], BF16))
            f2b = e_(nc.sbuf_tensor(f"f2b{li}", [128, 2, 32, 128], BF16))
            pa = e_(nc.psum_tensor(f"pa{li}", [128, 512], F32))
            pb = e_(nc.psum_tensor(f"pb{li}", [128, 512], F32))
            pc = e_(nc.psum_tensor(f"pc{li}", [128, 512], F32))
            pd = e_(nc.psum_tensor(f"pd{li}", [128, 512], F32))
            pe = e_(nc.psum_tensor(f"pe{li}", [128, 512], F32))
            pf = e_(nc.psum_tensor(f"pf{li}", [128, 512], F32))
            pg = e_(nc.psum_tensor(f"pg{li}", [128, 512], F32))
            ph = e_(nc.psum_tensor(f"ph{li}", [128, 512], F32))

            stv = {"ev_pfree": 0, "ev_apply": 0, "st_resid": 0, "st_xnn": 0,
                   "ev_pg": {0: 0, 1: 0}, "mm_xn": {0: 0, 1: 0}, "ev_y": 0,
                   "ac_g": 0}
            mm_f1_piece = {0: 0, 1: 0}
            mm_f2_piece = {0: 0, 1: 0}

            def emit_loads(g):
                t0 = g * TG

                @b.block.sync
                def _(sync, _g=g, _t0=t0):
                    if _g == 0:
                        sync.wait_ge(b.sem("st"), b.v("st"))
                        sync.wait_ge(b.sem("ev"), ev_wgo)
                    b.w(sync, "mm", stv["mm_xn"][_g % 2])
                    i = sync.dma_start(out=xng2[:, _g % 2],
                                       in_=xnb[:, :, _t0:_t0 + TG]
                                       .rearrange("c p t -> p c t"))
                    b.inc(i, "xl", 16)
                    b.w(sync, "ev", stv["ev_pg"][_g % 2])
                    i = sync.dma_start(out=ybg2[:, _g % 2],
                                       in_=ybuf[:, :, _t0:_t0 + TG]
                                       .rearrange("c p t -> p c t"))
                    b.inc(i, "xl", 16)
                    b.w(sync, "ev", stv["ev_y"])
                    i = sync.dma_start(out=xrg[:],
                                       in_=resid[:, :, _t0:_t0 + TG]
                                       .rearrange("c p t -> p c t"))
                    b.inc(i, "xl", 16)
                return b.v("xl")

            def emit_G(g, xl_g):
                xng = xng2[:, g % 2]

                def ggrp(tensor, oc, _xl=xl_g, _ew=ev_wgo):
                    if oc == 0:
                        tensor.wait_ge(b.sem("xl"), _xl - 32)
                        tensor.wait_ge(b.sem("ev"), _ew)
                    for dc in range(8):
                        i = tensor.matmul((pe if oc % 2 == 0 else pf)[:],
                                          wgo[:, 0, oc, dc, :], xng[:, dc, :],
                                          start=(dc == 0), stop=(dc == 7))
                    return i

                def gevac(eng, oc):
                    return eng.activation(gg[:, oc, :],
                                          (pe if oc % 2 == 0 else pf)[:], AF.Silu)

                mm_g, ac_g = pipe(b, H, 2, ggrp, gevac, "scalar")
                stv["mm_xn"][g % 2] = mm_g
                stv["ac_g"] = ac_g

            xl_cur = emit_loads(0)
            emit_G(0, xl_cur)
            for g in range(NG):
                t0 = g * TG
                xl_g = xl_cur
                ybg = ybg2[:, g % 2]

                @b.block.vector
                def _(vector, _a=stv["ac_g"], _xl=xl_g):
                    vector.wait_ge(b.sem("ac"), _a)
                    vector.wait_ge(b.sem("xl"), _xl - 16)
                    i = vector.tensor_mul(gg[:], gg[:], ybg[:])
                    b.inc(i, "ev")
                stv["ev_pg"][g % 2] = b.v("ev")
                ev_pg = b.v("ev")

                def wgrp(tensor, oc, _ev=ev_pg, _xl=xl_g):
                    if oc == 0:
                        tensor.wait_ge(b.sem("ev"), _ev)
                        tensor.wait_ge(b.sem("xl"), _xl)
                    for dc in range(8):
                        i = tensor.matmul((pc if oc % 2 == 0 else pd)[:],
                                          wgo[:, 1, oc, dc, :], gg[:, dc, :],
                                          start=(dc == 0), stop=(dc == 7))
                    return i

                def wevac(eng, oc, _st=stv["st_resid"]):
                    if oc == 0:
                        b.w(eng, "st", _st)
                    return eng.tensor_add(ysb3[:, oc, :],
                                          (pc if oc % 2 == 0 else pd)[:],
                                          xrg[:, oc, :])

                _, ev_y = pipe(b, H, 2, wgrp, wevac, "vector")
                stv["ev_y"] = ev_y

                if g + 1 < NG:
                    xl_cur = emit_loads(g + 1)
                    emit_G(g + 1, xl_cur)

                pq = (lambda _pf=stv["ev_pfree"], _ey=ev_y: (lambda tensor: (
                    tensor.wait_ge(b.sem("ev"), _ey), b.w(tensor, "ev", _pf))))()
                ps = (lambda _ea=stv["ev_apply"], _ey=ev_y, _sx=stv["st_xnn"]:
                      (lambda scalar: (scalar.wait_ge(b.sem("ev"), _ey),
                                       b.w(scalar, "ev", _ea),
                                       b.w(scalar, "st", _sx))))()
                mm_bc, ev_pfree = ln_block(b, ysb3, H, sq3, pg, ph, pg, ph,
                                           ones_col, ones_row, sm, float(D),
                                           sq_on_vector=False,
                                           pre_stats_wait=pq, pre_sq_wait=ps,
                                           ones_col_b=b.ones_colb)
                stv["ev_pfree"] = ev_pfree

                ev_ap_base = b.v("ev")
                ac_aff_base = b.v("ac")

                @b.block.vector
                def _(vector, _mb=mm_bc, _ab=ac_aff_base):
                    vector.wait_ge(b.sem("mm"), _mb)
                    for c in range(H):
                        if c >= 2:
                            vector.wait_ge(b.sem("ac"), _ab + c - 1)
                        vector.tensor_mul(ap3[:, c % 2, :], ysb3[:, c, :], pg[:])
                        i = vector.tensor_sub(ap3[:, c % 2, :], ap3[:, c % 2, :],
                                              ph[:])
                        b.inc(i, "ev")
                stv["ev_apply"] = b.v("ev")

                @b.block.scalar
                def _(scalar, _ab=ev_ap_base, _sx=stv["st_xnn"]):
                    b.w(scalar, "st", _sx)
                    for c in range(H):
                        scalar.wait_ge(b.sem("ev"), _ab + c + 1)
                        i = scalar.activation(ynt[:, c, :], ap3[:, c % 2, :],
                                              AF.Identity,
                                              bias=W["vln2b"][:, c:c + 1],
                                              scale=W["vln2w"][:, c:c + 1])
                        b.inc(i, "ac")
                ac_yn = b.v("ac")

                mm_ffn1_base = b.v("mm")
                gl_f1 = {}
                for piece in range(8):
                    @b.block.sync
                    def _(sync, _p=piece, _g=g):
                        if _g == 0 and _p == 0:
                            sync.wait_ge(b.sem("cv"), b.cv_marks[W["cv_fw1"]])
                        key = _p % 2
                        b.w(sync, "mm", mm_f1_piece[key])
                        i = sync.dma_start(out=f1b[:, key], in_=W["fw1"][_p])
                        b.inc(i, "gl", 16)
                        mm_f1_piece[key] = mm_ffn1_base + 4 * (_p + 1)
                    gl_f1[piece] = b.v("gl")

                def f1grp(tensor, hc, _ac=ac_yn):
                    piece, wi = hc // 4, hc % 4
                    if hc == 0:
                        tensor.wait_ge(b.sem("ac"), _ac)
                    if wi == 0:
                        tensor.wait_ge(b.sem("gl"), gl_f1[piece])
                    for dc in range(8):
                        i = tensor.matmul((pa if hc % 2 == 0 else pb)[:],
                                          f1b[:, piece % 2, wi, dc, :],
                                          ynt[:, dc, :],
                                          start=(dc == 0), stop=(dc == 7))
                    return i

                def f1evac(eng, hc):
                    return eng.activation(hh[:, hc, :],
                                          (pa if hc % 2 == 0 else pb)[:], AF.Silu,
                                          bias=W["vfb1"][:, hc:hc + 1])

                _, ac_h = pipe(b, 32, 2, f1grp, f1evac, "scalar")

                mm_ffn2_base = b.v("mm")
                gl_f2 = {}
                for oc in range(8):
                    @b.block.sync
                    def _(sync, _oc=oc, _g=g):
                        if _g == 0 and _oc == 0:
                            sync.wait_ge(b.sem("cv"), b.cv_marks[W["cv_fw2"]])
                        key = _oc % 2
                        b.w(sync, "mm", mm_f2_piece[key])
                        i = sync.dma_start(out=f2b[:, key], in_=W["fw2"][_oc])
                        b.inc(i, "gl", 16)
                        mm_f2_piece[key] = mm_ffn2_base + _oc + 1
                    gl_f2[oc] = b.v("gl")

                def f2grp(tensor, oc, _ac=ac_h):
                    if oc == 0:
                        tensor.wait_ge(b.sem("ac"), _ac)
                    tensor.wait_ge(b.sem("gl"), gl_f2[oc])
                    for hc in range(32):
                        i = tensor.matmul((pc if oc % 2 == 0 else pd)[:],
                                          f2b[:, oc % 2, hc, :], hh[:, hc, :],
                                          start=(hc == 0), stop=(hc == 31))
                    return i

                def f2evac(eng, oc):
                    return eng.scalar_tensor_tensor(
                        ysb3[:, oc, :], (pc if oc % 2 == 0 else pd)[:],
                        W["vfb2"][:, oc:oc + 1], ysb3[:, oc, :],
                        op0=ALU.add, op1=ALU.add)

                _, ev_xp = pipe(b, H, 2, f2grp, f2evac, "vector")

                @b.block.scalar
                def _(scalar, _t0=t0, _ev=ev_xp):
                    scalar.wait_ge(b.sem("ev"), _ev)
                    i = scalar.dma_start(out=resid[:, :, _t0:_t0 + TG]
                                         .rearrange("c p t -> p c t"), in_=ysb3[:])
                    b.inc(i, "st", 16)
                stv["st_resid"] = b.v("st")

                if not last:
                    pq2 = (lambda _pf=stv["ev_pfree"], _ex=ev_xp: (lambda tensor: (
                        tensor.wait_ge(b.sem("ev"), _ex),
                        b.w(tensor, "ev", _pf))))()
                    ps2 = (lambda _ea=stv["ev_apply"], _ex=ev_xp:
                           (lambda scalar: (scalar.wait_ge(b.sem("ev"), _ex),
                                            b.w(scalar, "ev", _ea))))()
                    mm_bc2, ev_pfree2 = ln_block(b, ysb3, H, sq3, pg, ph, pg, ph,
                                                 ones_col, ones_row, sm, float(D),
                                                 sq_on_vector=False,
                                                 pre_stats_wait=pq2,
                                                 pre_sq_wait=ps2,
                                                 ones_col_b=b.ones_colb)
                    stv["ev_pfree"] = ev_pfree2

                    ev_ap2 = b.v("ev")
                    ac_af2 = b.v("ac")

                    @b.block.vector
                    def _(vector, _mb=mm_bc2, _ab=ac_af2):
                        vector.wait_ge(b.sem("mm"), _mb)
                        for c in range(H):
                            if c >= 2:
                                vector.wait_ge(b.sem("ac"), _ab + c - 1)
                            vector.tensor_mul(ap3[:, c % 2, :], ysb3[:, c, :],
                                              pg[:])
                            i = vector.tensor_sub(ap3[:, c % 2, :],
                                                  ap3[:, c % 2, :], ph[:])
                            b.inc(i, "ev")
                    stv["ev_apply"] = b.v("ev")

                    @b.block.scalar
                    def _(scalar, _ab=ev_ap2):
                        for c in range(H):
                            scalar.wait_ge(b.sem("ev"), _ab + c + 1)
                            i = scalar.activation(ynt[:, c, :], ap3[:, c % 2, :],
                                                  AF.Identity,
                                                  bias=W["vln1b_next"][:, c:c + 1],
                                                  scale=W["vln1w_next"][:, c:c + 1])
                            b.inc(i, "ac")
                    ac_xnn = b.v("ac")

                    @b.block.scalar
                    def _(scalar, _t0=t0, _a=ac_xnn):
                        scalar.wait_ge(b.sem("ac"), _a)
                        i = scalar.dma_start(out=xnb[:, :, _t0:_t0 + TG]
                                             .rearrange("c p t -> p c t"),
                                             in_=ynt[:])
                        b.inc(i, "st", 16)
                    stv["st_xnn"] = b.v("st")

            @b.block.scalar
            def _(scalar):
                scalar.wait_ge(b.sem("st"), b.v("st"))
            b.barrier()



def _p4_mid(b, nc, xin, resid, xnb, ones_col, ones_row, sm):
    with ExitStack() as ps_:
        e_ = ps_.enter_context
        ag2 = e_(nc.sbuf_tensor("ag2", [128, 2, H, TG], F32))
        x0g2 = e_(nc.sbuf_tensor("x0g2", [128, 2, H, TG], F32))
        sqg4 = e_(nc.sbuf_tensor("sqg4", [128, H, TG], F32))
        x1t = e_(nc.sbuf_tensor("x1t", [128, H, TG], F32))
        xmt = e_(nc.sbuf_tensor("xmt", [128, H, TG], BF16))
        qS1 = e_(nc.psum_tensor("qS1", [128, 512], F32))
        qS2 = e_(nc.psum_tensor("qS2", [128, 512], F32))
        qR = e_(nc.psum_tensor("qR", [128, 512], F32))
        qMR = e_(nc.psum_tensor("qMR", [128, 512], F32))

        ev_slot_free = {}
        ev_pfree_prev = 0
        st_x1_prev = 0
        st_xm_prev = 0
        for g in range(NG):
            t0 = g * TG

            @b.block.sync
            def _(sync, _g=g, _t0=t0):
                if _g == 0:
                    sync.wait_ge(b.sem("st"), b.v("st"))
                b.w(sync, "ev", ev_slot_free.get(_g - 2))
                i = sync.dma_start(out=ag2[:, _g % 2],
                                   in_=resid[:, :, _t0:_t0 + TG]
                                   .rearrange("c p t -> p c t"))
                b.inc(i, "xl", 16)
                i = sync.dma_start(out=x0g2[:, _g % 2],
                                   in_=xin[:, :, _t0:_t0 + TG]
                                   .rearrange("c p t -> p c t"))
                b.inc(i, "xl", 16)
            xl_g = b.v("xl")
            ag = ag2[:, g % 2]
            x0g = x0g2[:, g % 2]

            pq = (lambda _pf=ev_pfree_prev, _xl=xl_g: (lambda tensor: (
                tensor.wait_ge(b.sem("xl"), _xl - 16), b.w(tensor, "ev", _pf))))()
            ps = (lambda _xl=xl_g: (lambda vector:
                  vector.wait_ge(b.sem("xl"), _xl - 16)))()
            mm_bc, ev_pfree = ln_block(b, ag, H, sqg4, qS1, qS2, qR, qMR,
                                       ones_col, ones_row, sm, float(D),
                                       pre_stats_wait=pq, pre_sq_wait=ps)

            @b.block.vector
            def _(vector, _mb=mm_bc, _st=st_x1_prev, _xl=xl_g):
                vector.wait_ge(b.sem("mm"), _mb)
                vector.wait_ge(b.sem("xl"), _xl)
                b.w(vector, "st", _st)
                for c in range(H):
                    vector.tensor_mul(x1t[:, c, :], ag[:, c, :], qR[:])
                    vector.tensor_sub(x1t[:, c, :], x1t[:, c, :], qMR[:])
                    i = vector.tensor_add(x1t[:, c, :], x1t[:, c, :], x0g[:, c, :])
                b.inc(i, "ev")
            ev_slot_free[g] = b.v("ev")
            ev_x1 = b.v("ev")

            pq2 = (lambda _pf=ev_pfree, _e1=ev_x1: (lambda tensor: (
                tensor.wait_ge(b.sem("ev"), _e1), b.w(tensor, "ev", _pf))))()
            mm_bc2, ev_pfree2 = ln_block(b, x1t, H, sqg4, qS1, qS2, qR, qMR,
                                         ones_col, ones_row, sm, float(D),
                                         pre_stats_wait=pq2)
            ev_pfree_prev = ev_pfree2

            @b.block.vector
            def _(vector, _mb=mm_bc2, _st=st_xm_prev):
                vector.wait_ge(b.sem("mm"), _mb)
                b.w(vector, "st", _st)
                for c in range(H):
                    vector.tensor_mul(sqg4[:, c, :], x1t[:, c, :], qR[:])
                    i = vector.tensor_sub(xmt[:, c, :], sqg4[:, c, :], qMR[:])
                b.inc(i, "ev")
            ev_xm = b.v("ev")

            @b.block.scalar
            def _(scalar, _t0=t0, _e=ev_xm):
                scalar.wait_ge(b.sem("ev"), _e)
                i = scalar.dma_start(out=resid[:, :, _t0:_t0 + TG]
                                     .rearrange("c p t -> p c t"), in_=x1t[:])
                b.inc(i, "st", 16)
                i = scalar.dma_start(out=xnb[:, :, _t0:_t0 + TG]
                                     .rearrange("c p t -> p c t"), in_=xmt[:])
                b.inc(i, "st", 16)
            st_x1_prev = b.v("st") - 16
            st_xm_prev = b.v("st")

        @b.block.scalar
        def _(scalar):
            scalar.wait_ge(b.sem("st"), b.v("st"))
        b.barrier()



def _p5_mlp(b, nc, W, ones_col, ones_row, sm):
    resid, xnb, xout = W["resid"], W["xnb"], W["xout"]
    with ExitStack() as ps_:
        e_ = ps_.enter_context
        xmg2 = e_(nc.sbuf_tensor("xmg2", [128, 2, H, TG], BF16))
        x1g = e_(nc.sbuf_tensor("x1g", [128, H, TG], F32))
        ham = e_(nc.sbuf_tensor("ham", [128, 32, TG], BF16))
        hbm = e_(nc.sbuf_tensor("hbm", [128, 32, TG], BF16))
        tmp2k = e_(nc.sbuf_tensor("tmp2k", [128, 2, TG], BF16))
        mot = e_(nc.sbuf_tensor("mot", [128, H, TG], F32))
        sqf5 = e_(nc.sbuf_tensor("sqf5", [128, H, TG], BF16))
        ap5 = e_(nc.sbuf_tensor("ap5", [128, 2, TG], F32))
        w12b = e_(nc.sbuf_tensor("w12b", [128, 2, 2, 2, 8, 128], BF16))
        whb = e_(nc.sbuf_tensor("whb", [128, 3, 32, 128], BF16))
        w3b = e_(nc.sbuf_tensor("w3b", [128, 2, 32, 128], BF16))
        ma = e_(nc.psum_tensor("ma", [128, 512], F32))
        mb = e_(nc.psum_tensor("mb", [128, 512], F32))
        mc = e_(nc.psum_tensor("mc", [128, 512], F32))
        md = e_(nc.psum_tensor("md", [128, 512], F32))
        me = e_(nc.psum_tensor("me", [128, 512], F32))
        mf = e_(nc.psum_tensor("mf", [128, 512], F32))

        cv_all = b.v("cv")
        mm_w12_piece = {0: 0, 1: 0}
        mm_wh_slab = {0: 0, 1: 0, 2: 0}
        st_ = {"mm_w3_used": 0, "ev_pfree_prev": 0, "ev_apply_prev": 0,
               "ev_fin_prev": 0, "so_prev": 0, "ac_affine_prev": 0, "xl_x1": 0}
        mm_xm_used = 0
        fin_pend = None
        for g in range(NG):
            t0 = g * TG

            @b.block.sync
            def _(sync, _g=g, _t0=t0, _mx=mm_xm_used):
                if _g == 0:
                    sync.wait_ge(b.sem("st"), b.v("st"))
                b.w(sync, "mm", _mx)
                i = sync.dma_start(out=xmg2[:, _g % 2],
                                   in_=xnb[:, :, _t0:_t0 + TG]
                                   .rearrange("c p t -> p c t"))
                b.inc(i, "xl", 16)
            xl_g = b.v("xl")
            if g == 0:
                @b.block.sync
                def _(sync, _t0=t0):
                    i = sync.dma_start(out=x1g[:],
                                       in_=resid[:, :, _t0:_t0 + TG]
                                       .rearrange("c p t -> p c t"))
                    b.inc(i, "x2", 16)
                st_["xl_x1"] = b.v("x2")
            xmg = xmg2[:, g % 2]

            mm_tl_base = b.v("mm")
            gl_12 = {}
            for piece in range(16):
                @b.block.sync
                def _(sync, _p=piece, _g=g):
                    if _g == 0 and _p == 0:
                        sync.wait_ge(b.sem("cv"), cv_all)
                    key = _p % 2
                    b.w(sync, "mm", mm_w12_piece[key])
                    i = sync.dma_start(out=w12b[:, key, 0], in_=W["M1"][_p])
                    b.inc(i, "gl", 16)
                    i = sync.dma_start(out=w12b[:, key, 1], in_=W["M2"][_p])
                    b.inc(i, "gl", 16)
                    mm_w12_piece[key] = mm_tl_base + 4 * (_p + 1)
                gl_12[piece] = b.v("gl")

            ac_tl_base = b.v("ac")
            ev_tl_base = b.v("ev")

            @b.block.tensor
            def _(tensor, _xl=xl_g, _g=g):
                tensor.wait_ge(b.sem("xl"), _xl - (16 if _g == 0 else 0))
                for hc in range(32):
                    piece, wi = hc // 2, hc % 2
                    if wi == 0:
                        tensor.wait_ge(b.sem("gl"), gl_12[piece])
                    if hc >= 2:
                        tensor.wait_ge(b.sem("ev"), ev_tl_base + hc - 1)
                    pA = ma if hc % 2 == 0 else mc
                    pB = mb if hc % 2 == 0 else md
                    for dc in range(8):
                        i = tensor.matmul(pA[:], w12b[:, piece % 2, 0, wi, dc, :],
                                          xmg[:, dc, :], start=(dc == 0),
                                          stop=(dc == 7))
                    b.inc(i, "mm")
                    for dc in range(8):
                        i = tensor.matmul(pB[:], w12b[:, piece % 2, 1, wi, dc, :],
                                          xmg[:, dc, :], start=(dc == 0),
                                          stop=(dc == 7))
                    b.inc(i, "mm")
            mm_xm_used = b.v("mm")

            if fin_pend is not None:
                _fa_ev = fin_pend[0]()
                _fb_mm = fin_pend[1](_fa_ev)

            @b.block.scalar
            def _(scalar):
                for hc in range(32):
                    scalar.wait_ge(b.sem("mm"), mm_tl_base + 2 * hc + 1)
                    i = scalar.activation(tmp2k[:, hc % 2, :],
                                          (ma if hc % 2 == 0 else mc)[:], AF.Silu,
                                          scale=W["vms1"][:, hc:hc + 1])
                    b.inc(i, "ac")

            @b.block.vector
            def _(vector):
                for hc in range(32):
                    vector.wait_ge(b.sem("ac"), ac_tl_base + hc + 1)
                    vector.wait_ge(b.sem("mm"), mm_tl_base + 2 * hc + 2)
                    i = vector.scalar_tensor_tensor(
                        ham[:, hc, :], (mb if hc % 2 == 0 else md)[:],
                        W["vms2"][:, hc:hc + 1], tmp2k[:, hc % 2, :],
                        op0=ALU.mult, op1=ALU.mult)
                    b.inc(i, "ev")
            ev_ha = b.v("ev")

            if fin_pend is not None:
                _fb_ev = fin_pend[2](_fb_mm)

            def hid(wdram, scales, src, dst, ev_src_ready, q, lsem):
                mm_hid_base = b.v("mm")
                gl_h = {}
                for oc in range(32):
                    @getattr(b.block, q)
                    def _(eng, _oc=oc):
                        key = _oc % 3
                        b.w(eng, "mm", mm_wh_slab[key])
                        i = eng.dma_start(out=whb[:, key], in_=wdram[_oc])
                        b.inc(i, lsem, 16)
                        mm_wh_slab[key] = mm_hid_base + _oc + 1
                    gl_h[oc] = b.v(lsem)

                def hgrp(tensor, oc, _esr=ev_src_ready):
                    if oc == 0:
                        tensor.wait_ge(b.sem("ev"), _esr)
                    tensor.wait_ge(b.sem(lsem), gl_h[oc])
                    for hc in range(32):
                        i = tensor.matmul((mc if oc % 2 == 0 else md)[:],
                                          whb[:, oc % 3, hc, :], src[:, hc, :],
                                          start=(hc == 0), stop=(hc == 31))
                    return i

                def hevac(eng, oc):
                    return eng.activation(dst[:, oc, :],
                                          (mc if oc % 2 == 0 else md)[:], AF.Gelu,
                                          scale=scales[:, oc:oc + 1])

                return pipe(b, 32, 2, hgrp, hevac, "scalar")

            _, ac_h1 = hid(W["MH1"], W["vmsh1"], ham, hbm, ev_ha, "sync", "gl")

            if fin_pend is not None:
                fin_pend[3](_fb_ev)
                fin_pend = None

            @b.block.vector
            def _(vector, _a=ac_h1):
                vector.wait_ge(b.sem("ac"), _a)
                i = vector.engine_nop()
                b.inc(i, "ev")
            ev_h1 = b.v("ev")
            _, ac_h2 = hid(W["MH2"], W["vmsh2"], hbm, ham, ev_h1, "gpsimd", "g2")

            @b.block.vector
            def _(vector, _a=ac_h2):
                vector.wait_ge(b.sem("ac"), _a)
                i = vector.engine_nop()
                b.inc(i, "ev")
            ev_h2 = b.v("ev")

            mm_m3_base = b.v("mm")
            gl_3 = {}
            for oc in range(8):
                @b.block.sync
                def _(sync, _oc=oc):
                    b.w(sync, "mm",
                        st_["mm_w3_used"] if _oc < 2 else mm_m3_base + _oc - 1)
                    i = sync.dma_start(out=w3b[:, _oc % 2], in_=W["M3"][_oc])
                    b.inc(i, "gl", 16)
                gl_3[oc] = b.v("gl")

            def m3grp(tensor, oc, _e=ev_h2, _ef=st_["ev_fin_prev"]):
                if oc == 0:
                    tensor.wait_ge(b.sem("ev"), _e)
                    b.w(tensor, "ev", _ef)
                tensor.wait_ge(b.sem("gl"), gl_3[oc])
                for hc in range(32):
                    i = tensor.matmul((me if oc % 2 == 0 else mf)[:],
                                      w3b[:, oc % 2, hc, :], ham[:, hc, :],
                                      start=(hc == 0), stop=(hc == 31))
                return i

            def m3evac(eng, oc):
                return eng.activation(mot[:, oc, :],
                                      (me if oc % 2 == 0 else mf)[:], AF.Identity,
                                      scale=W["vms3"][:, oc:oc + 1])

            mm_m3, ac_mo = pipe(b, H, 2, m3grp, m3evac, "scalar")
            st_["mm_w3_used"] = mm_m3

            def fin_a(_ac_mo=ac_mo, _ea=st_["ev_apply_prev"]):
                @b.block.vector
                def _(vector):
                    vector.wait_ge(b.sem("ac"), _ac_mo)
                    b.w(vector, "ev", _ea)
                    i = vector.tensor_mul(sqf5[:], mot[:], mot[:])
                    b.inc(i, "ev")
                return b.v("ev")

            def fin_b1(ev_sq, _ac_mo=ac_mo, _pf=st_["ev_pfree_prev"]):
                @b.block.tensor
                def _(tensor):
                    tensor.wait_ge(b.sem("ac"), _ac_mo)
                    b.w(tensor, "ev", _pf)
                    for c in range(H):
                        tensor.matmul(me[0:1, :], ones_col[:], mot[:, c, :],
                                      start=(c == 0), stop=(c == H - 1))
                    tensor.wait_ge(b.sem("ev"), ev_sq)
                    for c in range(H):
                        i = tensor.matmul(mf[0:1, :], b.ones_colb[:],
                                          sqf5[:, c, :],
                                          start=(c == 0), stop=(c == H - 1))
                    b.inc(i, "mm")
                return b.v("mm")

            def fin_b2(mm_stats):
                ev_pfree = ln_smalls(b, sm, me, mf, float(D), mm_stats)
                st_["ev_pfree_prev"] = ev_pfree
                return b.v("ev")

            def fin_c(ev_mr, _g=g, _t0=t0, _xl1=st_["xl_x1"]):
                @b.block.tensor
                def _(tensor):
                    tensor.wait_ge(b.sem("ev"), ev_mr)
                    tensor.matmul(me[:], ones_row[:], sm["rstd"][0:1, :],
                                  start=True, stop=True)
                    i = tensor.matmul(mf[:], ones_row[:], sm["mr"][0:1, :],
                                      start=True, stop=True)
                    b.inc(i, "mm")
                mm_bco = b.v("mm")

                @b.block.vector
                def _(vector, _mb=mm_bco, _st=st_["so_prev"]):
                    vector.wait_ge(b.sem("mm"), _mb)
                    vector.wait_ge(b.sem("x2"), _xl1)
                    b.w(vector, "so", _st)
                    for c in range(H):
                        vector.tensor_mul(ap5[:, 0, :], mot[:, c, :], me[:])
                        vector.tensor_sub(ap5[:, 1, :], ap5[:, 0, :], mf[:])
                        i = vector.tensor_add(mot[:, c, :], ap5[:, 1, :],
                                              x1g[:, c, :])
                        b.inc(i, "ev")
                st_["ev_apply_prev"] = b.v("ev")
                st_["ev_fin_prev"] = b.v("ev")
                ev_o = b.v("ev")
                if _g + 1 < NG:
                    @b.block.sync
                    def _(sync, _e=ev_o, _t02=_t0):
                        sync.wait_ge(b.sem("ev"), _e)
                        i = sync.dma_start(
                            out=x1g[:],
                            in_=resid[:, :, _t02 + TG:_t02 + 2 * TG]
                            .rearrange("c p t -> p c t"))
                        b.inc(i, "x2", 16)
                    st_["xl_x1"] = b.v("x2")

                @b.block.scalar
                def _(scalar, _e=ev_o):
                    scalar.wait_ge(b.sem("ev"), _e)
                    i = scalar.dma_start(out=xout[:, :, _t0:_t0 + TG]
                                         .rearrange("c p t -> p c t"), in_=mot[:])
                    b.inc(i, "so", 16)
                st_["so_prev"] = b.v("so")

            fin_pend = (fin_a, fin_b1, fin_b2, fin_c)

        _fa_ev = fin_pend[0]()
        _fb_mm = fin_pend[1](_fa_ev)
        _fb_ev = fin_pend[2](_fb_mm)
        fin_pend[3](_fb_ev)

        @b.block.scalar
        def _(scalar):
            scalar.wait_ge(b.sem("so"), b.v("so"))
        b.barrier()



_CACHED = {}


def get_nc(tap=None):
    if tap not in _CACHED:
        _CACHED[tap] = build_program(tap)
    return _CACHED[tap]


def kernel(**inputs):
    nc = get_nc()
    in_maps = [prep_core_inputs(inputs, k) for k in range(NCORES)]
    res = run_bass_kernel_spmd(nc, in_maps, core_ids=list(range(NCORES)))
    outs = []
    for k in range(NCORES):
        o = np.asarray(res.results[k]["xout"])
        o = o.reshape(H, 128, BLOC, S).transpose(2, 3, 0, 1).reshape(BLOC, S, D)
        outs.append(o)
    return np.concatenate(outs, axis=0).astype(np.float32)


# version 09
# speedup vs baseline: 1.2306x; 1.1116x over previous
import numpy as np
import ml_dtypes
from contextlib import ExitStack

import concourse.bass as bass
import concourse.mybir as mybir
from concourse.bass_utils import run_bass_kernel_spmd

F32 = mybir.dt.float32
BF16 = mybir.dt.bfloat16
AF = mybir.ActivationFunctionType
ALU = mybir.AluOpType

H = 8
HD = 128
D = 1024
FFND = 4096
S = 2048
BLOC = 2
T = BLOC * S
TG = 512
NG = T // TG
EPS = 1e-5
NCORES = 8

BF16NP = ml_dtypes.bfloat16
LAG_GN = True
DBG = {}



def _perm_deint():
    return np.concatenate([np.arange(0, HD, 2), np.arange(1, HD, 2)])


def _xpos_tables():
    SCALE_BASE = 512
    scale_vec = (np.arange(0, HD, 2) + 0.4 * HD) / (1.4 * HD)
    power = (np.arange(S, dtype=np.float64) / SCALE_BASE)[:, None]
    scale = scale_vec[None, :] ** power
    half = HD // 2
    inv_freq = 1.0 / 10000.0 ** (np.arange(half, dtype=np.float64) / half)
    ang = np.arange(S, dtype=np.float64)[:, None] * inv_freq[None, :]
    sin, cos = np.sin(ang), np.cos(ang)

    def dupT(m):
        return np.concatenate([m.T, m.T], axis=0)

    cosq = dupT(cos * scale)
    sinq = dupT(sin * scale)
    cosk = dupT(cos / scale)
    sink = dupT(sin / scale)
    sinq[0:64] *= -1.0
    sink[0:64] *= -1.0
    return (cosq.astype(np.float32), sinq.astype(np.float32),
            cosk.astype(np.float32), sink.astype(np.float32))


def _decay_consts():
    gam = 1.0 - np.exp(np.linspace(np.log(1.0 / 32), np.log(1.0 / 512), H))
    j = np.arange(TG, dtype=np.float64)
    gdec = np.broadcast_to(gam[:, None, None] ** j[None, None, :],
                           (H, 128, TG)).astype(np.float32)
    p = np.arange(128, dtype=np.float64)
    mi = np.arange(16, dtype=np.float64)
    expo = 128.0 * (mi[None, None, :] - 3.0) - p[:, None, None]
    dvec = (gam[None, :, None] ** expo).astype(np.float32)
    dd = np.arange(4)[:, None, None]
    pp = np.arange(128)[None, :, None]
    jj = np.arange(TG)[None, None, :]
    masks = (jj >= 128 * dd + pp).astype(BF16NP)
    return gdec, dvec, masks


def _lhsT_slabs(w):
    I, O = w.shape
    return np.ascontiguousarray(
        w.reshape(I // 128, 128, O // 128, 128).transpose(2, 1, 0, 3))


def _pieces(slabs, k):
    NO, _, NI, _ = slabs.shape
    return np.ascontiguousarray(
        slabs.reshape(NO // k, k, 128, NI, 128).transpose(0, 2, 1, 3, 4))


def _vec_pc(v):
    n = v.shape[0] // 128
    return np.ascontiguousarray(v.reshape(n, 128).T)


def prep_core_inputs(inputs, core):
    f = lambda k: np.asarray(inputs[k], dtype=np.float32)
    X = f("L0_attn_X")
    xc = X[2 * core:2 * core + 2]
    xin = np.ascontiguousarray(
        xc.reshape(BLOC, S, H, 128).transpose(2, 3, 0, 1).reshape(H, 128, T))

    perm = _perm_deint()
    m = {"xin": xin}
    for i in range(2):
        WQ = f("L0_attn_WQ")[i][:, :, perm]
        WK = f("L0_attn_WK")[i][:, :, perm]
        WV = f("L0_attn_WV")[i]
        m[f"wq{i}"] = _lhsT_slabs(WQ.transpose(1, 0, 2).reshape(D, H * HD))
        m[f"wk{i}"] = _lhsT_slabs(WK.transpose(1, 0, 2).reshape(D, H * HD))
        m[f"wv{i}"] = np.ascontiguousarray(
            WV.transpose(1, 0, 2).reshape(D, H * HD).reshape(8, 128, 1024))
        m[f"wg{i}"] = _pieces(_lhsT_slabs(f("L0_attn_WG")[i]), 4)
        m[f"wo{i}"] = _pieces(_lhsT_slabs(f("L0_attn_WO")[i]), 4)
        m[f"fw1{i}"] = _pieces(_lhsT_slabs(f("L0_attn_ffn_w1")[i]), 4)
        m[f"fw2{i}"] = _lhsT_slabs(f("L0_attn_ffn_w2")[i])
        m[f"ln1w{i}"] = _vec_pc(f("L0_attn_ln1_w")[i])
        m[f"ln1b{i}"] = _vec_pc(f("L0_attn_ln1_b")[i])
        m[f"ln2w{i}"] = _vec_pc(f("L0_attn_ln2_w")[i])
        m[f"ln2b{i}"] = _vec_pc(f("L0_attn_ln2_b")[i])
        m[f"gnw{i}"] = _vec_pc(f("L0_attn_gn_w")[i])
        m[f"gnb{i}"] = _vec_pc(f("L0_attn_gn_b")[i])
        m[f"fb1{i}"] = _vec_pc(f("L0_attn_ffn_b1")[i])
        m[f"fb2{i}"] = _vec_pc(f("L0_attn_ffn_b2")[i])

    m["m1"] = _pieces(_lhsT_slabs(np.ascontiguousarray(f("L0_ffn_w1_w").T)), 2)
    m["m2"] = _pieces(_lhsT_slabs(np.ascontiguousarray(f("L0_ffn_w2_w").T)), 2)
    m["mh1"] = _lhsT_slabs(np.ascontiguousarray(f("L0_ffn_h1_w").T))
    m["mh2"] = _lhsT_slabs(np.ascontiguousarray(f("L0_ffn_h2_w").T))
    m["m3"] = _lhsT_slabs(np.ascontiguousarray(f("L0_ffn_w3_w").T))
    m["ms1"] = _vec_pc(f("L0_ffn_w1_s"))
    m["ms2"] = _vec_pc(f("L0_ffn_w2_s"))
    m["msh1"] = _vec_pc(f("L0_ffn_h1_s"))
    m["msh2"] = _vec_pc(f("L0_ffn_h2_s"))
    m["ms3"] = _vec_pc(f("L0_ffn_w3_s"))

    m["cq"], m["sq_"], m["ck"], m["sk_"] = _xpos_tables()
    m["gdec"], m["dvec"], m["masks"] = _decay_consts()
    return m



class B:
    def __init__(self, nc, block, stack):
        self.nc = nc
        self.block = block
        self.stack = stack
        self.sems = {}
        self.cnt = {}

    def mksem(self, *names):
        for n in names:
            self.sems[n] = self.stack.enter_context(self.nc.semaphore(n))
            self.cnt[n] = 0

    def inc(self, inst, name, by=1):
        inst.then_inc(self.sems[name], by)
        self.cnt[name] += by
        return self.cnt[name]

    def v(self, name):
        return self.cnt[name]

    def sem(self, name):
        return self.sems[name]

    def w(self, eng, name, val):
        if val and val > 0:
            eng.wait_ge(self.sems[name], val)

    def barrier(self, engines=("sync", "tensor", "vector", "scalar")):
        tgt = self.cnt["bar"] + len(engines)
        for e in engines:
            @getattr(self.block, e)
            def _(eng):
                eng.drain()
                eng.sem_inc(self.sems["bar"], 1)
        self.cnt["bar"] = tgt
        for e in engines:
            @getattr(self.block, e)
            def _(eng):
                eng.wait_ge(self.sems["bar"], tgt)


def pipe(b, n, nbanks, emit_group, emit_evac, evac_eng="scalar",
         evac_extra_wait=None):
    cn = {"scalar": "ac", "vector": "ev"}[evac_eng]
    mm_base = b.v("mm")
    ev_base = b.v(cn)

    @b.block.tensor
    def _(tensor):
        for gi in range(n):
            if gi >= nbanks:
                tensor.wait_ge(b.sem(cn), ev_base + gi - nbanks + 1)
            i = emit_group(tensor, gi)
            b.inc(i, "mm")

    @getattr(b.block, evac_eng)
    def _(eng):
        for gi in range(n):
            eng.wait_ge(b.sem("mm"), mm_base + gi + 1)
            if evac_extra_wait:
                evac_extra_wait(eng, gi)
            i = emit_evac(eng, gi)
            b.inc(i, cn)

    return b.v("mm"), b.v(cn)


def ln_smalls(b, sm, pS1, pS2, denom, mm_stats):
    @b.block.vector
    def _(vector):
        vector.wait_ge(b.sem("mm"), mm_stats)
        vector.tensor_scalar_mul(sm["mu"][:], pS1[0:1, :], 1.0 / denom)
        vector.tensor_scalar_mul(sm["aux"][:], pS2[0:1, :], 1.0 / denom)
        vector.tensor_mul(sm["m2"][:], sm["mu"][:], sm["mu"][:])
        i = vector.tensor_sub(sm["var"][:], sm["aux"][:], sm["m2"][:])
        b.inc(i, "ev")
    ev_small = b.v("ev")

    @b.block.scalar
    def _(scalar):
        scalar.wait_ge(b.sem("ev"), ev_small)
        scalar.activation(sm["aux"][:], sm["var"][:], AF.Ln, bias=sm["eps"][0:1, 0:1])
        i = scalar.activation(sm["rstd"][:], sm["aux"][:], AF.Exp,
                              scale=sm["nh"][0:1, 0:1])
        b.inc(i, "ac")
    ac_rstd = b.v("ac")

    @b.block.vector
    def _(vector):
        vector.wait_ge(b.sem("ac"), ac_rstd)
        i = vector.tensor_mul(sm["mr"][:], sm["mu"][:], sm["rstd"][:])
        b.inc(i, "ev")
    return ev_small


def ln_block(b, src_view, nchunk, sqbuf, pS1, pS2, pR, pMR, ones_col, ones_row,
             sm, denom, sq_on_vector=True, pre_stats_wait=None, pre_sq_wait=None,
             ones_col_b=None):
    if sq_on_vector:
        @b.block.vector
        def _(vector):
            if pre_sq_wait:
                pre_sq_wait(vector)
            i = vector.tensor_mul(sqbuf[:], src_view[:], src_view[:])
            b.inc(i, "ev")
        ev_sq = b.v("ev")

        @b.block.tensor
        def _(tensor):
            if pre_stats_wait:
                pre_stats_wait(tensor)
            for c in range(nchunk):
                tensor.matmul(pS1[0:1, :], ones_col[:], src_view[:, c, :],
                              start=(c == 0), stop=(c == nchunk - 1))
            tensor.wait_ge(b.sem("ev"), ev_sq)
            for c in range(nchunk):
                i = tensor.matmul(pS2[0:1, :], ones_col[:], sqbuf[:, c, :],
                                  start=(c == 0), stop=(c == nchunk - 1))
            b.inc(i, "mm")
    else:
        ac_base = b.v("ac")
        mm_base = b.v("mm")

        @b.block.scalar
        def _(scalar):
            if pre_sq_wait:
                pre_sq_wait(scalar)
            for c in range(nchunk):
                if c >= 2:
                    scalar.wait_ge(b.sem("mm"), mm_base + c - 1)
                i = scalar.activation(sqbuf[:, c % 2, :], src_view[:, c, :],
                                      AF.Square)
                b.inc(i, "ac")

        @b.block.tensor
        def _(tensor):
            if pre_stats_wait:
                pre_stats_wait(tensor)
            for c in range(nchunk):
                tensor.matmul(pS1[0:1, :], ones_col[:], src_view[:, c, :],
                              start=(c == 0), stop=(c == nchunk - 1))
            oc2 = ones_col_b if ones_col_b is not None else ones_col
            for c in range(nchunk):
                tensor.wait_ge(b.sem("ac"), ac_base + c + 1)
                i = tensor.matmul(pS2[0:1, :], oc2[:], sqbuf[:, c % 2, :],
                                  start=(c == 0), stop=(c == nchunk - 1))
                b.inc(i, "mm")
    mm_stats = b.v("mm")

    ev_pfree = ln_smalls(b, sm, pS1, pS2, denom, mm_stats)
    ev_mr = b.v("ev")

    @b.block.tensor
    def _(tensor):
        tensor.wait_ge(b.sem("ev"), ev_mr)
        tensor.matmul(pR[:], ones_row[:], sm["rstd"][0:1, :], start=True, stop=True)
        i = tensor.matmul(pMR[:], ones_row[:], sm["mr"][0:1, :], start=True, stop=True)
        b.inc(i, "mm")
    return b.v("mm"), ev_pfree


def emit_conversions(b, jobs, cvb):
    nq = [0]

    @b.block.gpsimd
    def _(gpsimd):
        for src, dst, key, N, M in jobs:
            nsub = (M + 2047) // 2048
            for n in range(N):
                for s_ in range(nsub):
                    w = min(2048, M - 2048 * s_)
                    i = nq[0]
                    par = i % 2
                    if i >= 2:
                        gpsimd.wait_ge(b.sem("cv"), 16 * (i - 1))
                    ii = gpsimd.dma_start(out=cvb[:, par, 0:w],
                                          in_=src[n][:, 2048 * s_:2048 * s_ + w])
                    b.inc(ii, "cvl", 16)
                    gpsimd.wait_ge(b.sem("cvl"), 16 * (i + 1))
                    ii = gpsimd.dma_start(out=dst[n][:, 2048 * s_:2048 * s_ + w],
                                          in_=cvb[:, par, 0:w])
                    b.inc(ii, "cv", 16)
                    nq[0] += 1
            b.cv_marks[key] = b.v("cv")



def build_program(tap=None):
    nc = bass.Bass("TRN2", target_bir_lowering=False)
    stack = ExitStack()

    def din(name, shape, dt=F32):
        return nc.dram_tensor(name, list(shape), dt, kind="ExternalInput").ap()

    xin = din("xin", [H, 128, T])
    WQd = [din(f"wq{i}", [8, 128, 8, 128]) for i in range(2)]
    WKd = [din(f"wk{i}", [8, 128, 8, 128]) for i in range(2)]
    WVd = [din(f"wv{i}", [8, 128, 1024]) for i in range(2)]
    WGd = [din(f"wg{i}", [2, 128, 4, 8, 128]) for i in range(2)]
    WOd = [din(f"wo{i}", [2, 128, 4, 8, 128]) for i in range(2)]
    FW1 = [din(f"fw1{i}", [8, 128, 4, 8, 128]) for i in range(2)]
    FW2 = [din(f"fw2{i}", [8, 128, 32, 128]) for i in range(2)]
    V8 = lambda nm: [din(f"{nm}{i}", [128, 8]) for i in range(2)]
    LN1W, LN1B, LN2W, LN2B = V8("ln1w"), V8("ln1b"), V8("ln2w"), V8("ln2b")
    GNW, GNB, FB2 = V8("gnw"), V8("gnb"), V8("fb2")
    FB1 = [din(f"fb1{i}", [128, 32]) for i in range(2)]
    M1 = din("m1", [16, 128, 2, 8, 128])
    M2 = din("m2", [16, 128, 2, 8, 128])
    MH1 = din("mh1", [32, 128, 32, 128])
    MH2 = din("mh2", [32, 128, 32, 128])
    M3 = din("m3", [8, 128, 32, 128])
    MS1, MS2 = din("ms1", [128, 32]), din("ms2", [128, 32])
    MSH1, MSH2 = din("msh1", [128, 32]), din("msh2", [128, 32])
    MS3 = din("ms3", [128, 8])
    CQ, SQ_ = din("cq", [128, S]), din("sq_", [128, S])
    CK, SK_ = din("ck", [128, S]), din("sk_", [128, S])
    GDEC = din("gdec", [H, 128, TG])
    DVEC = din("dvec", [128, 8, 16])
    MASKS = din("masks", [4, 128, TG], BF16)

    xout = nc.dram_tensor("xout", [H, 128, T], F32, kind="ExternalOutput").ap()
    skind = "ExternalOutput" if tap else "Internal"
    CVshapes = {
        "fw1_0": [8, 128, 4, 8, 128], "fw2_0": [8, 128, 32, 128],
        "fw1_1": [8, 128, 4, 8, 128], "fw2_1": [8, 128, 32, 128],
        "m1": [16, 128, 2, 8, 128], "m2": [16, 128, 2, 8, 128],
        "mh1": [32, 128, 32, 128], "mh2": [32, 128, 32, 128],
        "m3": [8, 128, 32, 128],
    }
    CV = {k: nc.dram_tensor("cv_" + k, s, BF16, kind="Internal").ap()
          for k, s in CVshapes.items()}
    global HTAP
    HTAP = nc.dram_tensor("htap", [NG, 128, 32, TG], BF16, kind=skind).ap() \
        if tap else None
    if DBG:
        DBG["dq"] = nc.dram_tensor("dbg_q", [128, S], BF16, kind=skind).ap()
        DBG["dk"] = nc.dram_tensor("dbg_k", [128, S], BF16, kind=skind).ap()
        DBG["dv"] = nc.dram_tensor("dbg_v", [128, 16, 256], BF16, kind=skind).ap()
        DBG["dyt"] = nc.dram_tensor("dbg_yt", [128, TG], F32, kind=skind).ap()
        DBG["dsc"] = nc.dram_tensor("dbg_sc", [128, 16, TG], BF16, kind=skind).ap()
    resid = nc.dram_tensor("resid", [H, 128, T], F32, kind=skind).ap()
    xnb = nc.dram_tensor("xnb", [H, 128, T], BF16, kind=skind).ap()
    ybuf = nc.dram_tensor("ybuf", [H, 128, T], BF16, kind=skind).ap()

    sb = lambda name, shape, dt=F32: stack.enter_context(
        nc.sbuf_tensor(name, list(shape), dt))
    ones_col = sb("ones_col", [128, 1])
    ones_colb = sb("ones_colb", [128, 1], BF16)
    ones_row = sb("ones_row", [1, 128])
    vln1w = [sb(f"vln1w{i}", [128, 8]) for i in range(2)]
    vln1b = [sb(f"vln1b{i}", [128, 8]) for i in range(2)]
    vln2w = [sb(f"vln2w{i}", [128, 8]) for i in range(2)]
    vln2b = [sb(f"vln2b{i}", [128, 8]) for i in range(2)]
    vgnw = [sb(f"vgnw{i}", [128, 8]) for i in range(2)]
    vgnb = [sb(f"vgnb{i}", [128, 8]) for i in range(2)]
    vfb1 = [sb(f"vfb1{i}", [128, 32]) for i in range(2)]
    vfb2 = [sb(f"vfb2{i}", [128, 8]) for i in range(2)]
    vms1, vms2 = sb("vms1", [128, 32]), sb("vms2", [128, 32])
    vmsh1, vmsh2 = sb("vmsh1", [128, 32]), sb("vmsh2", [128, 32])
    vms3 = sb("vms3", [128, 8])
    vdvec = sb("vdvec", [128, 8, 16])
    sm = {k: sb("sm_" + k, [1, TG]) for k in ("mu", "aux", "m2", "var", "rstd", "mr")}
    sm["eps"] = sb("sm_eps", [1, 1])
    sm["nh"] = sb("sm_nh", [1, 1])
    cvb = sb("cvb", [128, 2, 2048], BF16)

    with nc.Block() as block:
        b = B(nc, block, stack)
        b.ones_colb = ones_colb
        b.mksem("cl", "xl", "wl", "gl", "mm", "ev", "ac", "st", "so", "bar",
                "cvl", "cv", "g2", "x2")

        @block.sync
        def _(sync):
            pairs = [(MS1, vms1), (MS2, vms2), (MSH1, vmsh1), (MSH2, vmsh2),
                     (MS3, vms3), (DVEC, vdvec)]
            for i in range(2):
                pairs += [(LN1W[i], vln1w[i]), (LN1B[i], vln1b[i]),
                          (LN2W[i], vln2w[i]), (LN2B[i], vln2b[i]),
                          (GNW[i], vgnw[i]), (GNB[i], vgnb[i]),
                          (FB1[i], vfb1[i]), (FB2[i], vfb2[i])]
            for src, dst in pairs:
                b.inc(sync.dma_start(out=dst[:], in_=src), "cl", 16)
        cl_consts = b.v("cl")

        @block.vector
        def _(vector):
            vector.memset(ones_col[:], 1.0)
            vector.memset(ones_colb[:], 1.0)
            vector.memset(ones_row[:], 1.0)
            vector.memset(sm["eps"][:], EPS)
            i = vector.memset(sm["nh"][:], -0.5)
            b.inc(i, "ev")
        ev_ones = b.v("ev")

        for e in ("tensor", "scalar"):
            @getattr(block, e)
            def _(eng):
                eng.wait_ge(b.sem("ev"), ev_ones)
                eng.wait_ge(b.sem("cl"), cl_consts)

        b.cv_marks = {}
        cv_jobs = []

        def flat(ap, N, M):
            return ap.rearrange("a p " + " ".join(
                f"d{i}" for i in range(len(ap.shape) - 2)) + " -> a p (" + " ".join(
                f"d{i}" for i in range(len(ap.shape) - 2)) + ")")

        for key, fsrc in (("fw1_0", FW1[0]), ("fw2_0", FW2[0]),
                          ("fw1_1", FW1[1]), ("fw2_1", FW2[1]),
                          ("m1", M1), ("m2", M2), ("mh1", MH1), ("mh2", MH2),
                          ("m3", M3)):
            shp = CVshapes[key]
            N = shp[0]
            M = int(np.prod(shp[2:]))
            cv_jobs.append((flat(fsrc, N, M), flat(CV[key], N, M), key, N, M))

        with nc.named_scope("p0"):
            _p0(b, nc, xin, resid, xnb, vln1w[0], vln1b[0], ones_col, ones_row, sm)
            emit_conversions(b, cv_jobs, cvb)
        if tap == "p0":
            _finish(b)
            stack.close()
            return nc

        consts = dict(CQ=CQ, SQ_=SQ_, CK=CK, SK_=SK_, GDEC=GDEC, MASKS=MASKS,
                      vdvec=vdvec, resid=resid, xnb=xnb, ybuf=ybuf)
        for li in range(2):
            Wl = dict(wq=WQd[li], wk=WKd[li], wv=WVd[li], wg=WGd[li], wo=WOd[li],
                      fw1=CV[f"fw1_{li}"], fw2=CV[f"fw2_{li}"],
                      cv_fw1=f"fw1_{li}", cv_fw2=f"fw2_{li}",
                      vln2w=vln2w[li], vln2b=vln2b[li],
                      vgnw=vgnw[li], vgnb=vgnb[li],
                      vfb1=vfb1[li], vfb2=vfb2[li],
                      vln1w_next=vln1w[1], vln1b_next=vln1b[1])
            with nc.named_scope(f"p1_{li}"):
                _p1_retention(b, li, nc, Wl, consts, ones_col, ones_row, sm)
            if tap == f"r{li}":
                _finish(b)
                stack.close()
                return nc
            with nc.named_scope(f"p3_{li}"):
                _p3_woffn(b, li, nc, Wl, consts, ones_col, ones_row, sm)
            if tap == f"l{li}":
                _finish(b)
                stack.close()
                return nc

        with nc.named_scope("p4"):
            _p4_mid(b, nc, xin, resid, xnb, ones_col, ones_row, sm)
        if tap == "p4":
            _finish(b)
            stack.close()
            return nc

        Wm = dict(M1=CV["m1"], M2=CV["m2"], MH1=CV["mh1"], MH2=CV["mh2"],
                  M3=CV["m3"],
                  vms1=vms1, vms2=vms2, vmsh1=vmsh1, vmsh2=vmsh2, vms3=vms3,
                  resid=resid, xnb=xnb, xout=xout)
        with nc.named_scope("p5"):
            _p5_mlp(b, nc, Wm, ones_col, ones_row, sm)
        _finish(b)

    stack.close()
    return nc


def _finish(b):
    for e in ("scalar", "sync"):
        @getattr(b.block, e)
        def _(eng):
            eng.wait_ge(b.sem("st"), b.v("st"))
            eng.wait_ge(b.sem("so"), b.v("so"))



def _p0(b, nc, xin, resid, xnb, vw, vb, ones_col, ones_row, sm):
    with ExitStack() as ps_:
        e_ = ps_.enter_context
        xg2 = e_(nc.sbuf_tensor("xg2", [128, 2, H, TG], F32))
        h0t = e_(nc.sbuf_tensor("h0t", [128, H, TG], F32))
        sqg = e_(nc.sbuf_tensor("sqg", [128, H, TG], F32))
        xnt = e_(nc.sbuf_tensor("xnt", [128, H, TG], BF16))
        pS1 = e_(nc.psum_tensor("pS1", [128, 512], F32))
        pS2 = e_(nc.psum_tensor("pS2", [128, 512], F32))
        pR = e_(nc.psum_tensor("pR", [128, 512], F32))
        pMR = e_(nc.psum_tensor("pMR", [128, 512], F32))

        ev_xg_free = {}
        ev_pfree_prev = 0
        st_h0_prev = 0
        st_xn_prev = 0
        ac_xn_prev = 0
        for g in range(NG):
            t0 = g * TG

            @b.block.sync
            def _(sync, _g=g, _t0=t0):
                b.w(sync, "ev", ev_xg_free.get(_g - 2))
                i = sync.dma_start(out=xg2[:, _g % 2],
                                   in_=xin[:, :, _t0:_t0 + TG].rearrange("c p t -> p c t"))
                b.inc(i, "xl", 16)
            xl_g = b.v("xl")
            xg = xg2[:, g % 2]

            pq1 = (lambda _pf=ev_pfree_prev, _xl=xl_g: (lambda tensor: (
                tensor.wait_ge(b.sem("xl"), _xl), b.w(tensor, "ev", _pf))))()
            ps1 = (lambda _xl=xl_g, _ac=ac_xn_prev: (lambda vector: (
                vector.wait_ge(b.sem("xl"), _xl), b.w(vector, "ac", _ac))))()
            mm_bc1, ev_pfree = ln_block(b, xg, H, sqg, pS1, pS2, pR, pMR,
                                        ones_col, ones_row, sm, float(D),
                                        pre_stats_wait=pq1, pre_sq_wait=ps1)

            @b.block.vector
            def _(vector, _mb=mm_bc1, _st=st_h0_prev):
                vector.wait_ge(b.sem("mm"), _mb)
                b.w(vector, "st", _st)
                for c in range(H):
                    vector.tensor_mul(h0t[:, c, :], xg[:, c, :], pR[:])
                    i = vector.tensor_sub(h0t[:, c, :], h0t[:, c, :], pMR[:])
                b.inc(i, "ev")
            ev_xg_free[g] = b.v("ev")
            ev_apply1 = b.v("ev")

            pq2 = (lambda _pf=ev_pfree, _e1=ev_apply1: (lambda tensor: (
                tensor.wait_ge(b.sem("ev"), _e1), b.w(tensor, "ev", _pf))))()
            mm_bc2, ev_pfree2 = ln_block(b, h0t, H, sqg, pS1, pS2, pR, pMR,
                                         ones_col, ones_row, sm, float(D),
                                         pre_stats_wait=pq2)
            ev_pfree_prev = ev_pfree2

            @b.block.vector
            def _(vector, _mb=mm_bc2):
                vector.wait_ge(b.sem("mm"), _mb)
                for c in range(H):
                    vector.tensor_mul(sqg[:, c, :], h0t[:, c, :], pR[:])
                    i = vector.tensor_sub(sqg[:, c, :], sqg[:, c, :], pMR[:])
                b.inc(i, "ev")
            ev_u = b.v("ev")

            @b.block.scalar
            def _(scalar, _e=ev_u, _st=st_xn_prev):
                scalar.wait_ge(b.sem("ev"), _e)
                b.w(scalar, "st", _st)
                for c in range(H):
                    i = scalar.activation(xnt[:, c, :], sqg[:, c, :], AF.Identity,
                                          bias=vb[:, c:c + 1], scale=vw[:, c:c + 1])
                b.inc(i, "ac")
            ac_xn_prev = b.v("ac")
            ac_xn = b.v("ac")

            @b.block.scalar
            def _(scalar, _t0=t0, _ac=ac_xn):
                i = scalar.dma_start(out=resid[:, :, _t0:_t0 + TG]
                                     .rearrange("c p t -> p c t"), in_=h0t[:])
                b.inc(i, "st", 16)
                scalar.wait_ge(b.sem("ac"), _ac)
                i = scalar.dma_start(out=xnb[:, :, _t0:_t0 + TG]
                                     .rearrange("c p t -> p c t"), in_=xnt[:])
                b.inc(i, "st", 16)
            st_h0_prev = b.v("st") - 16
            st_xn_prev = b.v("st")

        @b.block.scalar
        def _(scalar):
            scalar.wait_ge(b.sem("st"), b.v("st"))
        b.barrier()



def _p1_retention(b, li, nc, W, C, ones_col, ones_row, sm):
    with ExitStack() as ps_:
        e_ = ps_.enter_context
        tcq = e_(nc.sbuf_tensor(f"tcq{li}", [128, S], F32))
        tsq = e_(nc.sbuf_tensor(f"tsq{li}", [128, S], F32))
        tck = e_(nc.sbuf_tensor(f"tck{li}", [128, S], F32))
        tsk = e_(nc.sbuf_tensor(f"tsk{li}", [128, S], F32))
        tmsk = e_(nc.sbuf_tensor(f"tmsk{li}", [128, 4, TG], BF16))
        tgd = e_(nc.sbuf_tensor(f"tgd{li}", [128, 2, TG], F32))
        xnh = e_(nc.sbuf_tensor(f"xnh{li}", [128, H, S], BF16))
        wvf = e_(nc.sbuf_tensor(f"wvf{li}", [128, 1024], F32))
        wvb = e_(nc.sbuf_tensor(f"wvb{li}", [128, 8, 1024], BF16))
        wqkf = e_(nc.sbuf_tensor(f"wqkf{li}", [128, 2, 8, 128], F32))
        wqkb = e_(nc.sbuf_tensor(f"wqkb{li}", [128, 2, 2, 8, 128], BF16))
        vsb = e_(nc.sbuf_tensor(f"vsb{li}", [128, 16, 256], BF16))
        qsb = e_(nc.sbuf_tensor(f"qsb{li}", [128, S], BF16))
        ksb = e_(nc.sbuf_tensor(f"ksb{li}", [128, S], BF16))
        scsb = e_(nc.sbuf_tensor(f"scsb{li}", [128, 16, TG], BF16))
        xt1 = e_(nc.sbuf_tensor(f"xt1{li}", [128, TG], F32))
        xt2 = e_(nc.sbuf_tensor(f"xt2{li}", [128, TG], F32))
        ytl2 = e_(nc.sbuf_tensor(f"ytl2{li}", [128, 2, TG], F32))
        sq12 = e_(nc.sbuf_tensor(f"sq12{li}", [128, 2, TG], F32))
        gxt = e_(nc.sbuf_tensor(f"gxt{li}", [128, TG], F32))
        ygn2 = e_(nc.sbuf_tensor(f"ygn2{li}", [128, 2, S], BF16))
        pq0 = e_(nc.psum_tensor(f"pq0{li}", [128, 512], F32))
        pq1 = e_(nc.psum_tensor(f"pq1{li}", [128, 512], F32))
        pq2 = e_(nc.psum_tensor(f"pq2{li}", [128, 512], F32))
        psc0 = e_(nc.psum_tensor(f"psc0{li}", [128, 512], F32))
        psc1 = e_(nc.psum_tensor(f"psc1{li}", [128, 512], F32))
        pyh = e_(nc.psum_tensor(f"pyh{li}", [128, 512], F32))
        pgA = e_(nc.psum_tensor(f"pgA{li}", [128, 512], F32))
        pgB = e_(nc.psum_tensor(f"pgB{li}", [128, 512], F32))
        qk_banks = [pq0, pq1, pq2]
        sc_banks = [psc0, psc1, pq0]

        @b.block.sync
        def _(sync):
            for src, dst in ((C["CQ"], tcq), (C["SQ_"], tsq),
                             (C["CK"], tck), (C["SK_"], tsk)):
                b.inc(sync.dma_start(out=dst[:], in_=src), "cl", 16)
            i = sync.dma_start(out=tmsk[:], in_=C["MASKS"].rearrange("d p t -> p d t"))
            b.inc(i, "cl", 16)
        cl_tab = b.v("cl")

        ev_wv_prev = 0
        for dc in range(8):
            @b.block.sync
            def _(sync, _dc=dc, _ev=ev_wv_prev):
                b.w(sync, "ev", _ev)
                i = sync.dma_start(out=wvf[:], in_=W["wv"][_dc])
                b.inc(i, "wl", 16)
            wl_v = b.v("wl")

            @b.block.vector
            def _(vector, _dc=dc, _wl=wl_v):
                vector.wait_ge(b.sem("wl"), _wl)
                i = vector.tensor_copy(wvb[:, _dc, :], wvf[:])
                b.inc(i, "ev")
            ev_wv_prev = b.v("ev")
        ev_wv = b.v("ev")

        st = {"mm_last_v_use": 0,
              "ev_wqkf_prev": 0,
              "ev_ytl_prev": 0,
              "ev_apply_prev": 0,
              "ac_affine_prev": 0,
              "ev_mr": 0,
              "st_ygn": {0: 0, 1: 0},
              "ev_tgd_free": {0: 0, 1: 0},
              "psc_free": {0: None, 1: None, 2: None},
              "qkfree": {0: None, 1: None, 2: None},
              "pend": None}

        def wait_marks(eng, marks):
            mx = {}
            for m in marks:
                if m is not None:
                    mx[m[0]] = max(mx.get(m[0], 0), m[1])
            for sname, v in mx.items():
                eng.wait_ge(b.sem(sname), v)

        def make_pend(_h, _sc, _slot, _evsq, _s0, _tb0):
            def stats():
                @b.block.tensor
                def _(tensor):
                    tensor.wait_ge(b.sem("ev"), max(_evsq, st["ev_apply_prev"]))
                    tensor.matmul(pgA[0:1, :], ones_col[:], ytl2[:, _slot, :],
                                  start=True, stop=True)
                    i = tensor.matmul(pgB[0:1, :], ones_col[:], sq12[:, _slot, :],
                                      start=True, stop=True)
                    b.inc(i, "mm")
                ln_smalls(b, sm, pgA, pgB, float(HD), b.v("mm"))
                st["ev_mr"] = b.v("ev")

            def apply():
                @b.block.tensor
                def _(tensor):
                    tensor.wait_ge(b.sem("ev"), st["ev_mr"])
                    tensor.matmul(pgA[:], ones_row[:], sm["rstd"][0:1, :],
                                  start=True, stop=True)
                    i = tensor.matmul(pgB[:], ones_row[:], sm["mr"][0:1, :],
                                      start=True, stop=True)
                    b.inc(i, "mm")
                mm_bc = b.v("mm")

                @b.block.vector
                def _(vector):
                    vector.wait_ge(b.sem("mm"), mm_bc)
                    b.w(vector, "ac", st["ac_affine_prev"])
                    vector.tensor_mul(gxt[:], ytl2[:, _slot, :], pgA[:])
                    i = vector.tensor_sub(gxt[:], gxt[:], pgB[:])
                    b.inc(i, "ev")
                st["ev_apply_prev"] = b.v("ev")
                ev_ap = b.v("ev")
                ybs = _h % 2

                @b.block.scalar
                def _(scalar):
                    scalar.wait_ge(b.sem("ev"), ev_ap)
                    if _sc == 0:
                        b.w(scalar, "st", st["st_ygn"][ybs])
                    i = scalar.activation(ygn2[:, ybs, _s0:_s0 + TG], gxt[:],
                                          AF.Identity,
                                          bias=W["vgnb"][:, _h:_h + 1],
                                          scale=W["vgnw"][:, _h:_h + 1])
                    b.inc(i, "ac")
                    if _sc == 3:
                        scalar.wait_ge(b.sem("ac"), b.v("ac"))
                        i = scalar.dma_start(out=C["ybuf"][_h][:, _tb0:_tb0 + S],
                                             in_=ygn2[:, ybs, :])
                        b.inc(i, "st", 16)
                st["ac_affine_prev"] = b.v("ac")
                if _sc == 3:
                    st["st_ygn"][ybs] = b.v("st")

            return {"stats": stats, "apply": apply}

        def flush_stats():
            if st["pend"] is not None:
                st["pend"]["stats"]()

        def flush_apply():
            if st["pend"] is not None:
                st["pend"]["apply"]()
                st["pend"] = None

        for bb in range(BLOC):
            tb0 = bb * S

            @b.block.sync
            def _(sync, _tb0=tb0):
                sync.wait_ge(b.sem("st"), b.v("st"))
                b.w(sync, "mm", b.v("mm"))
                i = sync.dma_start(out=xnh[:], in_=C["xnb"][:, :, _tb0:_tb0 + S]
                                   .rearrange("c p t -> p c t"))
                b.inc(i, "xl", 16)
            xl_xn = b.v("xl")

            for hg in range(4):
                bank_marks = [st["qkfree"][k] for k in range(3)] + [("ev", ev_wv)]
                flush_stats()

                def vgrp(tensor, tt, _hg=hg, _xl=xl_xn, _bm=bank_marks):
                    if tt == 0:
                        tensor.wait_ge(b.sem("xl"), _xl)
                        wait_marks(tensor, _bm)
                    for dc in range(8):
                        i = tensor.matmul(qk_banks[tt % 3][:, 0:256],
                                          xnh[:, dc, tt * 128:(tt + 1) * 128],
                                          wvb[:, dc, 256 * _hg:256 * (_hg + 1)],
                                          start=(dc == 0), stop=(dc == 7))
                    return i

                def vevac(eng, tt, _mv=st["mm_last_v_use"]):
                    if tt == 0:
                        b.w(eng, "mm", _mv)
                    return eng.activation(vsb[:, tt, :], qk_banks[tt % 3][:, 0:256],
                                          AF.Copy)

                ac_v_base = b.v("ac")
                _, ac_v = pipe(b, 16, 3, vgrp, vevac, "scalar")
                st["qkfree"] = {0: ("ac", ac_v_base + 16),
                                1: ("ac", ac_v_base + 14),
                                2: ("ac", ac_v_base + 15)}
                if DBG and li == 0 and bb == 0 and hg == 0:
                    @b.block.sync
                    def _(sync, _a=ac_v):
                        sync.wait_ge(b.sem("ac"), _a)
                        b.inc(sync.dma_start(out=DBG["dv"], in_=vsb[:]), "st", 16)
                flush_apply()

                for h in (2 * hg, 2 * hg + 1):
                    hcol = h - 2 * hg
                    par = h % 2

                    @b.block.sync
                    def _(sync, _par=par, _h=h, _ewf=st["ev_wqkf_prev"],
                          _egs=st["ev_tgd_free"][par]):
                        b.w(sync, "ev", _ewf)
                        b.inc(sync.dma_start(out=wqkf[:, 0], in_=W["wq"][_h]), "wl", 16)
                        b.inc(sync.dma_start(out=wqkf[:, 1], in_=W["wk"][_h]), "wl", 16)
                        b.w(sync, "ev", _egs)
                        b.inc(sync.dma_start(out=tgd[:, _par], in_=C["GDEC"][_h]),
                              "cl", 16)
                    wl_qk = b.v("wl")
                    cl_gd = b.v("cl")

                    @b.block.vector
                    def _(vector, _par=par, _wl=wl_qk):
                        vector.wait_ge(b.sem("wl"), _wl)
                        i = vector.tensor_copy(wqkb[:, _par], wqkf[:])
                        b.inc(i, "ev")
                    ev_qk_w = b.v("ev")
                    st["ev_wqkf_prev"] = ev_qk_w

                    bm2 = [st["qkfree"][k] for k in range(3)] + [("ev", ev_qk_w)]

                    def qkgrp(tensor, tt, _par=par, _bm=bm2):
                        qk, gc = tt // 4, tt % 4
                        if tt == 0:
                            wait_marks(tensor, _bm)
                            tensor.wait_ge(b.sem("cl"), cl_tab)
                        for dc in range(8):
                            i = tensor.matmul(qk_banks[tt % 3][:],
                                              wqkb[:, _par, qk, dc, :],
                                              xnh[:, dc, gc * TG:(gc + 1) * TG],
                                              start=(dc == 0), stop=(dc == 7))
                        return i

                    def qkevac(eng, tt):
                        qk, gc = tt // 4, tt % 4
                        dst = qsb if qk == 0 else ksb
                        ct = tcq if qk == 0 else tck
                        stb = tsq if qk == 0 else tsk
                        sl = slice(gc * TG, (gc + 1) * TG)
                        ps = qk_banks[tt % 3]
                        eng.tensor_mul(xt1[:], ps[:], ct[:, sl])
                        eng.tensor_mul(xt2[0:64, :], ps[64:128, :], stb[0:64, sl])
                        eng.tensor_mul(xt2[64:128, :], ps[0:64, :], stb[64:128, sl])
                        return eng.tensor_add(dst[:, sl], xt1[:], xt2[:])

                    ev_qk_base = b.v("ev")
                    pipe(b, 8, 3, qkgrp, qkevac, "vector")
                    ev_qk_done = b.v("ev")
                    if DBG and li == 0 and bb == 0 and h == 0:
                        @b.block.sync
                        def _(sync, _e=ev_qk_done):
                            sync.wait_ge(b.sem("ev"), _e)
                            b.inc(sync.dma_start(out=DBG["dq"], in_=qsb[:]), "st", 16)
                            b.inc(sync.dma_start(out=DBG["dk"], in_=ksb[:]), "st", 16)
                    st["psc_free"][2] = ("ev", ev_qk_base + 7)
                    st["qkfree"] = {0: ("ev", ev_qk_base + 7),
                                    1: ("ev", ev_qk_base + 8),
                                    2: ("ev", ev_qk_base + 6)}
                    flush_stats()
                    flush_apply()

                    for sc in range(4):
                        s0 = sc * TG
                        ntj = 4 * sc + 4
                        slot = sc % 2

                        for tj in range(ntj):
                            bank = tj % 3
                            kneed = ev_qk_base + 5 + tj // 4
                            if tj == 0:
                                kneed = max(kneed, ev_qk_base + 1 + sc)

                            @b.block.tensor
                            def _(tensor, _tj=tj, _bank=bank, _s0=s0,
                                  _kn=kneed, _fr=st["psc_free"][bank]):
                                marks = [_fr, ("ev", _kn)]
                                wait_marks(tensor, marks)
                                i = tensor.matmul(sc_banks[_bank][:],
                                                  ksb[:, _tj * 128:(_tj + 1) * 128],
                                                  qsb[:, _s0:_s0 + TG],
                                                  start=True, stop=True)
                                b.inc(i, "mm")
                            mmv = b.v("mm")

                            if tj >= 4 * sc:
                                d = tj - 4 * sc

                                @b.block.vector
                                def _(vector, _tj=tj, _bank=bank, _d=d, _h=h,
                                      _mm=mmv):
                                    vector.wait_ge(b.sem("mm"), _mm)
                                    i = vector.scalar_tensor_tensor(
                                        scsb[:, _tj, :], sc_banks[_bank][:],
                                        C["vdvec"][:, _h, 3 - _d:4 - _d],
                                        tmsk[:, _d, :],
                                        op0=ALU.mult, op1=ALU.mult)
                                    b.inc(i, "ev")
                                st["psc_free"][bank] = ("ev", b.v("ev"))
                            elif tj % 2 == 1:
                                mi = 4 * sc - tj + 3

                                @b.block.vector
                                def _(vector, _tj=tj, _bank=bank, _h=h,
                                      _mi=mi, _mm=mmv):
                                    vector.wait_ge(b.sem("mm"), _mm)
                                    i = vector.tensor_scalar(
                                        scsb[:, _tj, :], sc_banks[_bank][:],
                                        C["vdvec"][:, _h, _mi:_mi + 1], None,
                                        op0=ALU.mult)
                                    b.inc(i, "ev")
                                st["psc_free"][bank] = ("ev", b.v("ev"))
                            else:
                                mi = 4 * sc - tj + 3

                                @b.block.scalar
                                def _(scalar, _tj=tj, _bank=bank, _h=h,
                                      _mi=mi, _mm=mmv):
                                    scalar.wait_ge(b.sem("mm"), _mm)
                                    i = scalar.activation(
                                        scsb[:, _tj, :], sc_banks[_bank][:],
                                        AF.Identity,
                                        scale=C["vdvec"][:, _h, _mi:_mi + 1])
                                    b.inc(i, "ac")
                                st["psc_free"][bank] = ("ac", b.v("ac"))
                            if bank == 2:
                                st["qkfree"][0] = st["psc_free"][2]

                        sc_ac, sc_ev = b.v("ac"), b.v("ev")
                        if sc >= 1:
                            flush_stats()

                        @b.block.tensor
                        def _(tensor, _hcol=hcol, _ntj=ntj, _a=sc_ac, _e=sc_ev,
                              _ey=st["ev_ytl_prev"]):
                            tensor.wait_ge(b.sem("ac"), _a)
                            tensor.wait_ge(b.sem("ev"), _e)
                            b.w(tensor, "ev", _ey)
                            for tj in range(_ntj):
                                i = tensor.matmul(
                                    pyh[:], vsb[:, tj, _hcol * 128:(_hcol + 1) * 128],
                                    scsb[:, tj, :],
                                    start=(tj == 0), stop=(tj == _ntj - 1))
                            b.inc(i, "mm")
                        mm_yh = b.v("mm")
                        st["mm_last_v_use"] = mm_yh

                        @b.block.vector
                        def _(vector, _par=par, _slot=slot, _my=mm_yh, _cg=cl_gd):
                            vector.wait_ge(b.sem("mm"), _my)
                            vector.wait_ge(b.sem("cl"), _cg)
                            i = vector.tensor_mul(ytl2[:, _slot, :], pyh[:],
                                                  tgd[:, _par])
                            b.inc(i, "ev")
                            i = vector.tensor_mul(sq12[:, _slot, :],
                                                  ytl2[:, _slot, :],
                                                  ytl2[:, _slot, :])
                            b.inc(i, "ev")
                        st["ev_ytl_prev"] = b.v("ev") - 1
                        if sc == 3:
                            st["ev_tgd_free"][par] = b.v("ev") - 1
                        ev_yt = b.v("ev")
                        if sc >= 1:
                            flush_apply()
                        st["pend"] = make_pend(h, sc, slot, ev_yt, s0, tb0)
                        if not LAG_GN:
                            flush_stats()
                            flush_apply()

        flush_stats()
        flush_apply()

        @b.block.scalar
        def _(scalar):
            scalar.wait_ge(b.sem("st"), b.v("st"))
        b.barrier()



def _p3_woffn(b, li, nc, W, C, ones_col, ones_row, sm):
    resid, xnb, ybuf = C["resid"], C["xnb"], C["ybuf"]
    last = (li == 1)
    with nc.sbuf_tensor(f"wgo{li}", [128, 2, 8, 8, 128], BF16) as wgo:
        with nc.sbuf_tensor(f"wgof{li}", [128, 4, 8, 128], F32) as wgof:
            ev_cv_prev = 0
            for k in range(2):
                wsrc = W["wg"] if k == 0 else W["wo"]
                for hf in range(2):
                    @b.block.sync
                    def _(sync, _hf=hf, _w=wsrc, _ev=ev_cv_prev):
                        b.w(sync, "ev", _ev)
                        i = sync.dma_start(out=wgof[:], in_=_w[_hf])
                        b.inc(i, "wl", 16)
                    wl_w = b.v("wl")

                    @b.block.vector
                    def _(vector, _k=k, _hf=hf, _wl=wl_w):
                        vector.wait_ge(b.sem("wl"), _wl)
                        i = vector.tensor_copy(wgo[:, _k, 4 * _hf:4 * _hf + 4],
                                               wgof[:])
                        b.inc(i, "ev")
                    ev_cv_prev = b.v("ev")
            ev_wgo = b.v("ev")

        with ExitStack() as ps_:
            e_ = ps_.enter_context
            xng2 = e_(nc.sbuf_tensor(f"xng2{li}", [128, 2, H, TG], BF16))
            ybg2 = e_(nc.sbuf_tensor(f"ybg2{li}", [128, 2, H, TG], BF16))
            xrg = e_(nc.sbuf_tensor(f"xrg{li}", [128, H, TG], F32))
            gg = e_(nc.sbuf_tensor(f"gg{li}", [128, H, TG], BF16))
            ysb3 = e_(nc.sbuf_tensor(f"ysb3{li}", [128, H, TG], F32))
            ynt = e_(nc.sbuf_tensor(f"ynt{li}", [128, H, TG], BF16))
            hh = e_(nc.sbuf_tensor(f"hh{li}", [128, 32, TG], BF16))
            sq3 = e_(nc.sbuf_tensor(f"sq3{li}", [128, 2, TG], BF16))
            ap3 = e_(nc.sbuf_tensor(f"ap3{li}", [128, 2, TG], F32))
            f1b = e_(nc.sbuf_tensor(f"f1b{li}", [128, 2, 4, 8, 128], BF16))
            f2b = e_(nc.sbuf_tensor(f"f2b{li}", [128, 2, 32, 128], BF16))
            pa = e_(nc.psum_tensor(f"pa{li}", [128, 512], F32))
            pb = e_(nc.psum_tensor(f"pb{li}", [128, 512], F32))
            pc = e_(nc.psum_tensor(f"pc{li}", [128, 512], F32))
            pd = e_(nc.psum_tensor(f"pd{li}", [128, 512], F32))
            pe = e_(nc.psum_tensor(f"pe{li}", [128, 512], F32))
            pf = e_(nc.psum_tensor(f"pf{li}", [128, 512], F32))
            pg = e_(nc.psum_tensor(f"pg{li}", [128, 512], F32))
            ph = e_(nc.psum_tensor(f"ph{li}", [128, 512], F32))

            stv = {"ev_pfree": 0, "ev_apply": 0, "st_resid": 0, "st_xnn": 0,
                   "ev_pg": {0: 0, 1: 0}, "mm_xn": {0: 0, 1: 0}, "ev_y": 0,
                   "ac_g": 0}
            mm_f1_piece = {0: 0, 1: 0}
            mm_f2_piece = {0: 0, 1: 0}

            def emit_loads(g):
                t0 = g * TG

                @b.block.sync
                def _(sync, _g=g, _t0=t0):
                    if _g == 0:
                        sync.wait_ge(b.sem("st"), b.v("st"))
                        sync.wait_ge(b.sem("ev"), ev_wgo)
                    b.w(sync, "mm", stv["mm_xn"][_g % 2])
                    i = sync.dma_start(out=xng2[:, _g % 2],
                                       in_=xnb[:, :, _t0:_t0 + TG]
                                       .rearrange("c p t -> p c t"))
                    b.inc(i, "xl", 16)
                    b.w(sync, "ev", stv["ev_pg"][_g % 2])
                    i = sync.dma_start(out=ybg2[:, _g % 2],
                                       in_=ybuf[:, :, _t0:_t0 + TG]
                                       .rearrange("c p t -> p c t"))
                    b.inc(i, "xl", 16)
                    b.w(sync, "ev", stv["ev_y"])
                    i = sync.dma_start(out=xrg[:],
                                       in_=resid[:, :, _t0:_t0 + TG]
                                       .rearrange("c p t -> p c t"))
                    b.inc(i, "xl", 16)
                return b.v("xl")

            def emit_G(g, xl_g):
                xng = xng2[:, g % 2]

                def ggrp(tensor, oc, _xl=xl_g, _ew=ev_wgo):
                    if oc == 0:
                        tensor.wait_ge(b.sem("xl"), _xl - 32)
                        tensor.wait_ge(b.sem("ev"), _ew)
                    for dc in range(8):
                        i = tensor.matmul((pe if oc % 2 == 0 else pf)[:],
                                          wgo[:, 0, oc, dc, :], xng[:, dc, :],
                                          start=(dc == 0), stop=(dc == 7))
                    return i

                def gevac(eng, oc):
                    return eng.activation(gg[:, oc, :],
                                          (pe if oc % 2 == 0 else pf)[:], AF.Silu)

                mm_g, ac_g = pipe(b, H, 2, ggrp, gevac, "scalar")
                stv["mm_xn"][g % 2] = mm_g
                stv["ac_g"] = ac_g

            xl_cur = emit_loads(0)
            emit_G(0, xl_cur)
            for g in range(NG):
                t0 = g * TG
                xl_g = xl_cur
                ybg = ybg2[:, g % 2]

                @b.block.vector
                def _(vector, _a=stv["ac_g"], _xl=xl_g):
                    vector.wait_ge(b.sem("ac"), _a)
                    vector.wait_ge(b.sem("xl"), _xl - 16)
                    i = vector.tensor_mul(gg[:], gg[:], ybg[:])
                    b.inc(i, "ev")
                stv["ev_pg"][g % 2] = b.v("ev")
                ev_pg = b.v("ev")

                def wgrp(tensor, oc, _ev=ev_pg, _xl=xl_g):
                    if oc == 0:
                        tensor.wait_ge(b.sem("ev"), _ev)
                        tensor.wait_ge(b.sem("xl"), _xl)
                    for dc in range(8):
                        i = tensor.matmul((pc if oc % 2 == 0 else pd)[:],
                                          wgo[:, 1, oc, dc, :], gg[:, dc, :],
                                          start=(dc == 0), stop=(dc == 7))
                    return i

                def wevac(eng, oc, _st=stv["st_resid"]):
                    if oc == 0:
                        b.w(eng, "st", _st)
                    return eng.tensor_add(ysb3[:, oc, :],
                                          (pc if oc % 2 == 0 else pd)[:],
                                          xrg[:, oc, :])

                _, ev_y = pipe(b, H, 2, wgrp, wevac, "vector")
                stv["ev_y"] = ev_y

                if g + 1 < NG:
                    xl_cur = emit_loads(g + 1)
                    emit_G(g + 1, xl_cur)

                pq = (lambda _pf=stv["ev_pfree"], _ey=ev_y: (lambda tensor: (
                    tensor.wait_ge(b.sem("ev"), _ey), b.w(tensor, "ev", _pf))))()
                ps = (lambda _ea=stv["ev_apply"], _ey=ev_y, _sx=stv["st_xnn"]:
                      (lambda scalar: (scalar.wait_ge(b.sem("ev"), _ey),
                                       b.w(scalar, "ev", _ea),
                                       b.w(scalar, "st", _sx))))()
                mm_bc, ev_pfree = ln_block(b, ysb3, H, sq3, pg, ph, pg, ph,
                                           ones_col, ones_row, sm, float(D),
                                           sq_on_vector=False,
                                           pre_stats_wait=pq, pre_sq_wait=ps,
                                           ones_col_b=b.ones_colb)
                stv["ev_pfree"] = ev_pfree

                ev_ap_base = b.v("ev")
                ac_aff_base = b.v("ac")

                @b.block.vector
                def _(vector, _mb=mm_bc, _ab=ac_aff_base):
                    vector.wait_ge(b.sem("mm"), _mb)
                    for c in range(H):
                        if c >= 2:
                            vector.wait_ge(b.sem("ac"), _ab + c - 1)
                        vector.tensor_mul(ap3[:, c % 2, :], ysb3[:, c, :], pg[:])
                        i = vector.tensor_sub(ap3[:, c % 2, :], ap3[:, c % 2, :],
                                              ph[:])
                        b.inc(i, "ev")
                stv["ev_apply"] = b.v("ev")

                @b.block.scalar
                def _(scalar, _ab=ev_ap_base, _sx=stv["st_xnn"]):
                    b.w(scalar, "st", _sx)
                    for c in range(H):
                        scalar.wait_ge(b.sem("ev"), _ab + c + 1)
                        i = scalar.activation(ynt[:, c, :], ap3[:, c % 2, :],
                                              AF.Identity,
                                              bias=W["vln2b"][:, c:c + 1],
                                              scale=W["vln2w"][:, c:c + 1])
                        b.inc(i, "ac")
                ac_yn = b.v("ac")

                mm_ffn1_base = b.v("mm")
                gl_f1 = {}
                for piece in range(8):
                    @b.block.sync
                    def _(sync, _p=piece, _g=g):
                        if _g == 0 and _p == 0:
                            sync.wait_ge(b.sem("cv"), b.cv_marks[W["cv_fw1"]])
                        key = _p % 2
                        b.w(sync, "mm", mm_f1_piece[key])
                        i = sync.dma_start(out=f1b[:, key], in_=W["fw1"][_p])
                        b.inc(i, "gl", 16)
                        mm_f1_piece[key] = mm_ffn1_base + 4 * (_p + 1)
                    gl_f1[piece] = b.v("gl")

                def f1grp(tensor, hc, _ac=ac_yn):
                    piece, wi = hc // 4, hc % 4
                    if hc == 0:
                        tensor.wait_ge(b.sem("ac"), _ac)
                    if wi == 0:
                        tensor.wait_ge(b.sem("gl"), gl_f1[piece])
                    for dc in range(8):
                        i = tensor.matmul((pa if hc % 2 == 0 else pb)[:],
                                          f1b[:, piece % 2, wi, dc, :],
                                          ynt[:, dc, :],
                                          start=(dc == 0), stop=(dc == 7))
                    return i

                def f1evac(eng, hc):
                    return eng.activation(hh[:, hc, :],
                                          (pa if hc % 2 == 0 else pb)[:], AF.Silu,
                                          bias=W["vfb1"][:, hc:hc + 1])

                _, ac_h = pipe(b, 32, 2, f1grp, f1evac, "scalar")

                mm_ffn2_base = b.v("mm")
                gl_f2 = {}
                for oc in range(8):
                    @b.block.sync
                    def _(sync, _oc=oc, _g=g):
                        if _g == 0 and _oc == 0:
                            sync.wait_ge(b.sem("cv"), b.cv_marks[W["cv_fw2"]])
                        key = _oc % 2
                        b.w(sync, "mm", mm_f2_piece[key])
                        i = sync.dma_start(out=f2b[:, key], in_=W["fw2"][_oc])
                        b.inc(i, "gl", 16)
                        mm_f2_piece[key] = mm_ffn2_base + _oc + 1
                    gl_f2[oc] = b.v("gl")

                def f2grp(tensor, oc, _ac=ac_h):
                    if oc == 0:
                        tensor.wait_ge(b.sem("ac"), _ac)
                    tensor.wait_ge(b.sem("gl"), gl_f2[oc])
                    for hc in range(32):
                        i = tensor.matmul((pc if oc % 2 == 0 else pd)[:],
                                          f2b[:, oc % 2, hc, :], hh[:, hc, :],
                                          start=(hc == 0), stop=(hc == 31))
                    return i

                def f2evac(eng, oc):
                    return eng.scalar_tensor_tensor(
                        ysb3[:, oc, :], (pc if oc % 2 == 0 else pd)[:],
                        W["vfb2"][:, oc:oc + 1], ysb3[:, oc, :],
                        op0=ALU.add, op1=ALU.add)

                _, ev_xp = pipe(b, H, 2, f2grp, f2evac, "vector")

                @b.block.scalar
                def _(scalar, _t0=t0, _ev=ev_xp):
                    scalar.wait_ge(b.sem("ev"), _ev)
                    i = scalar.dma_start(out=resid[:, :, _t0:_t0 + TG]
                                         .rearrange("c p t -> p c t"), in_=ysb3[:])
                    b.inc(i, "st", 16)
                stv["st_resid"] = b.v("st")

                if not last:
                    pq2 = (lambda _pf=stv["ev_pfree"], _ex=ev_xp: (lambda tensor: (
                        tensor.wait_ge(b.sem("ev"), _ex),
                        b.w(tensor, "ev", _pf))))()
                    ps2 = (lambda _ea=stv["ev_apply"], _ex=ev_xp:
                           (lambda scalar: (scalar.wait_ge(b.sem("ev"), _ex),
                                            b.w(scalar, "ev", _ea))))()
                    mm_bc2, ev_pfree2 = ln_block(b, ysb3, H, sq3, pg, ph, pg, ph,
                                                 ones_col, ones_row, sm, float(D),
                                                 sq_on_vector=False,
                                                 pre_stats_wait=pq2,
                                                 pre_sq_wait=ps2,
                                                 ones_col_b=b.ones_colb)
                    stv["ev_pfree"] = ev_pfree2

                    ev_ap2 = b.v("ev")
                    ac_af2 = b.v("ac")

                    @b.block.vector
                    def _(vector, _mb=mm_bc2, _ab=ac_af2):
                        vector.wait_ge(b.sem("mm"), _mb)
                        for c in range(H):
                            if c >= 2:
                                vector.wait_ge(b.sem("ac"), _ab + c - 1)
                            vector.tensor_mul(ap3[:, c % 2, :], ysb3[:, c, :],
                                              pg[:])
                            i = vector.tensor_sub(ap3[:, c % 2, :],
                                                  ap3[:, c % 2, :], ph[:])
                            b.inc(i, "ev")
                    stv["ev_apply"] = b.v("ev")

                    @b.block.scalar
                    def _(scalar, _ab=ev_ap2):
                        for c in range(H):
                            scalar.wait_ge(b.sem("ev"), _ab + c + 1)
                            i = scalar.activation(ynt[:, c, :], ap3[:, c % 2, :],
                                                  AF.Identity,
                                                  bias=W["vln1b_next"][:, c:c + 1],
                                                  scale=W["vln1w_next"][:, c:c + 1])
                            b.inc(i, "ac")
                    ac_xnn = b.v("ac")

                    @b.block.scalar
                    def _(scalar, _t0=t0, _a=ac_xnn):
                        scalar.wait_ge(b.sem("ac"), _a)
                        i = scalar.dma_start(out=xnb[:, :, _t0:_t0 + TG]
                                             .rearrange("c p t -> p c t"),
                                             in_=ynt[:])
                        b.inc(i, "st", 16)
                    stv["st_xnn"] = b.v("st")

            @b.block.scalar
            def _(scalar):
                scalar.wait_ge(b.sem("st"), b.v("st"))
            b.barrier()



def _p4_mid(b, nc, xin, resid, xnb, ones_col, ones_row, sm):
    with ExitStack() as ps_:
        e_ = ps_.enter_context
        ag2 = e_(nc.sbuf_tensor("ag2", [128, 2, H, TG], F32))
        x0g2 = e_(nc.sbuf_tensor("x0g2", [128, 2, H, TG], F32))
        sqg4 = e_(nc.sbuf_tensor("sqg4", [128, H, TG], F32))
        x1t = e_(nc.sbuf_tensor("x1t", [128, H, TG], F32))
        xmt = e_(nc.sbuf_tensor("xmt", [128, H, TG], BF16))
        qS1 = e_(nc.psum_tensor("qS1", [128, 512], F32))
        qS2 = e_(nc.psum_tensor("qS2", [128, 512], F32))
        qR = e_(nc.psum_tensor("qR", [128, 512], F32))
        qMR = e_(nc.psum_tensor("qMR", [128, 512], F32))

        ev_slot_free = {}
        ev_pfree_prev = 0
        st_x1_prev = 0
        st_xm_prev = 0
        for g in range(NG):
            t0 = g * TG

            @b.block.sync
            def _(sync, _g=g, _t0=t0):
                if _g == 0:
                    sync.wait_ge(b.sem("st"), b.v("st"))
                b.w(sync, "ev", ev_slot_free.get(_g - 2))
                i = sync.dma_start(out=ag2[:, _g % 2],
                                   in_=resid[:, :, _t0:_t0 + TG]
                                   .rearrange("c p t -> p c t"))
                b.inc(i, "xl", 16)
                i = sync.dma_start(out=x0g2[:, _g % 2],
                                   in_=xin[:, :, _t0:_t0 + TG]
                                   .rearrange("c p t -> p c t"))
                b.inc(i, "xl", 16)
            xl_g = b.v("xl")
            ag = ag2[:, g % 2]
            x0g = x0g2[:, g % 2]

            pq = (lambda _pf=ev_pfree_prev, _xl=xl_g: (lambda tensor: (
                tensor.wait_ge(b.sem("xl"), _xl - 16), b.w(tensor, "ev", _pf))))()
            ps = (lambda _xl=xl_g: (lambda vector:
                  vector.wait_ge(b.sem("xl"), _xl - 16)))()
            mm_bc, ev_pfree = ln_block(b, ag, H, sqg4, qS1, qS2, qR, qMR,
                                       ones_col, ones_row, sm, float(D),
                                       pre_stats_wait=pq, pre_sq_wait=ps)

            @b.block.vector
            def _(vector, _mb=mm_bc, _st=st_x1_prev, _xl=xl_g):
                vector.wait_ge(b.sem("mm"), _mb)
                vector.wait_ge(b.sem("xl"), _xl)
                b.w(vector, "st", _st)
                for c in range(H):
                    vector.tensor_mul(x1t[:, c, :], ag[:, c, :], qR[:])
                    vector.tensor_sub(x1t[:, c, :], x1t[:, c, :], qMR[:])
                    i = vector.tensor_add(x1t[:, c, :], x1t[:, c, :], x0g[:, c, :])
                b.inc(i, "ev")
            ev_slot_free[g] = b.v("ev")
            ev_x1 = b.v("ev")

            pq2 = (lambda _pf=ev_pfree, _e1=ev_x1: (lambda tensor: (
                tensor.wait_ge(b.sem("ev"), _e1), b.w(tensor, "ev", _pf))))()
            mm_bc2, ev_pfree2 = ln_block(b, x1t, H, sqg4, qS1, qS2, qR, qMR,
                                         ones_col, ones_row, sm, float(D),
                                         pre_stats_wait=pq2)
            ev_pfree_prev = ev_pfree2

            @b.block.vector
            def _(vector, _mb=mm_bc2, _st=st_xm_prev):
                vector.wait_ge(b.sem("mm"), _mb)
                b.w(vector, "st", _st)
                for c in range(H):
                    vector.tensor_mul(sqg4[:, c, :], x1t[:, c, :], qR[:])
                    i = vector.tensor_sub(xmt[:, c, :], sqg4[:, c, :], qMR[:])
                b.inc(i, "ev")
            ev_xm = b.v("ev")

            @b.block.scalar
            def _(scalar, _t0=t0, _e=ev_xm):
                scalar.wait_ge(b.sem("ev"), _e)
                i = scalar.dma_start(out=resid[:, :, _t0:_t0 + TG]
                                     .rearrange("c p t -> p c t"), in_=x1t[:])
                b.inc(i, "st", 16)
                i = scalar.dma_start(out=xnb[:, :, _t0:_t0 + TG]
                                     .rearrange("c p t -> p c t"), in_=xmt[:])
                b.inc(i, "st", 16)
            st_x1_prev = b.v("st") - 16
            st_xm_prev = b.v("st")

        @b.block.scalar
        def _(scalar):
            scalar.wait_ge(b.sem("st"), b.v("st"))
        b.barrier()



def _p5_mlp(b, nc, W, ones_col, ones_row, sm):
    resid, xnb, xout = W["resid"], W["xnb"], W["xout"]
    with ExitStack() as ps_:
        e_ = ps_.enter_context
        xmg2 = e_(nc.sbuf_tensor("xmg2", [128, 2, H, TG], BF16))
        x1g = e_(nc.sbuf_tensor("x1g", [128, H, TG], F32))
        ham = e_(nc.sbuf_tensor("ham", [128, 32, TG], BF16))
        hbm = e_(nc.sbuf_tensor("hbm", [128, 32, TG], BF16))
        tmp2k = e_(nc.sbuf_tensor("tmp2k", [128, 2, TG], BF16))
        mot = e_(nc.sbuf_tensor("mot", [128, H, TG], F32))
        sqf5 = e_(nc.sbuf_tensor("sqf5", [128, H, TG], BF16))
        ap5 = e_(nc.sbuf_tensor("ap5", [128, 2, TG], F32))
        w12b = e_(nc.sbuf_tensor("w12b", [128, 2, 2, 2, 8, 128], BF16))
        whb = e_(nc.sbuf_tensor("whb", [128, 3, 32, 128], BF16))
        w3b = e_(nc.sbuf_tensor("w3b", [128, 2, 32, 128], BF16))
        ma = e_(nc.psum_tensor("ma", [128, 512], F32))
        mb = e_(nc.psum_tensor("mb", [128, 512], F32))
        mc = e_(nc.psum_tensor("mc", [128, 512], F32))
        md = e_(nc.psum_tensor("md", [128, 512], F32))
        me = e_(nc.psum_tensor("me", [128, 512], F32))
        mf = e_(nc.psum_tensor("mf", [128, 512], F32))

        cv_all = b.v("cv")
        mm_w12_piece = {0: 0, 1: 0}
        mm_wh_slab = {0: 0, 1: 0, 2: 0}
        st_ = {"mm_w3_used": 0, "ev_pfree_prev": 0, "ev_apply_prev": 0,
               "ev_fin_prev": 0, "so_prev": 0, "ac_affine_prev": 0, "xl_x1": 0}
        mm_xm_used = 0
        fin_pend = None
        for g in range(NG):
            t0 = g * TG

            @b.block.sync
            def _(sync, _g=g, _t0=t0, _mx=mm_xm_used):
                if _g == 0:
                    sync.wait_ge(b.sem("st"), b.v("st"))
                b.w(sync, "mm", _mx)
                i = sync.dma_start(out=xmg2[:, _g % 2],
                                   in_=xnb[:, :, _t0:_t0 + TG]
                                   .rearrange("c p t -> p c t"))
                b.inc(i, "xl", 16)
            xl_g = b.v("xl")
            if g == 0:
                @b.block.sync
                def _(sync, _t0=t0):
                    i = sync.dma_start(out=x1g[:],
                                       in_=resid[:, :, _t0:_t0 + TG]
                                       .rearrange("c p t -> p c t"))
                    b.inc(i, "x2", 16)
                st_["xl_x1"] = b.v("x2")
            xmg = xmg2[:, g % 2]

            mm_tl_base = b.v("mm")
            gl_12 = {}
            for piece in range(16):
                @b.block.sync
                def _(sync, _p=piece, _g=g):
                    if _g == 0 and _p == 0:
                        sync.wait_ge(b.sem("cv"), cv_all)
                    key = _p % 2
                    b.w(sync, "mm", mm_w12_piece[key])
                    i = sync.dma_start(out=w12b[:, key, 0], in_=W["M1"][_p])
                    b.inc(i, "gl", 16)
                    i = sync.dma_start(out=w12b[:, key, 1], in_=W["M2"][_p])
                    b.inc(i, "gl", 16)
                    mm_w12_piece[key] = mm_tl_base + 4 * (_p + 1)
                gl_12[piece] = b.v("gl")

            ac_tl_base = b.v("ac")
            ev_tl_base = b.v("ev")

            @b.block.tensor
            def _(tensor, _xl=xl_g, _g=g):
                tensor.wait_ge(b.sem("xl"), _xl - (16 if _g == 0 else 0))
                for hc in range(32):
                    piece, wi = hc // 2, hc % 2
                    if wi == 0:
                        tensor.wait_ge(b.sem("gl"), gl_12[piece])
                    if hc >= 2:
                        tensor.wait_ge(b.sem("ev"), ev_tl_base + hc - 1)
                    pA = ma if hc % 2 == 0 else mc
                    pB = mb if hc % 2 == 0 else md
                    for dc in range(8):
                        i = tensor.matmul(pA[:], w12b[:, piece % 2, 0, wi, dc, :],
                                          xmg[:, dc, :], start=(dc == 0),
                                          stop=(dc == 7))
                    b.inc(i, "mm")
                    for dc in range(8):
                        i = tensor.matmul(pB[:], w12b[:, piece % 2, 1, wi, dc, :],
                                          xmg[:, dc, :], start=(dc == 0),
                                          stop=(dc == 7))
                    b.inc(i, "mm")
            mm_xm_used = b.v("mm")

            if fin_pend is not None:
                _fa_ev = fin_pend[0]()
                _fb_mm = fin_pend[1](_fa_ev)

            @b.block.scalar
            def _(scalar):
                for hc in range(32):
                    scalar.wait_ge(b.sem("mm"), mm_tl_base + 2 * hc + 1)
                    i = scalar.activation(tmp2k[:, hc % 2, :],
                                          (ma if hc % 2 == 0 else mc)[:], AF.Silu,
                                          scale=W["vms1"][:, hc:hc + 1])
                    b.inc(i, "ac")

            @b.block.vector
            def _(vector):
                for hc in range(32):
                    vector.wait_ge(b.sem("ac"), ac_tl_base + hc + 1)
                    vector.wait_ge(b.sem("mm"), mm_tl_base + 2 * hc + 2)
                    i = vector.scalar_tensor_tensor(
                        ham[:, hc, :], (mb if hc % 2 == 0 else md)[:],
                        W["vms2"][:, hc:hc + 1], tmp2k[:, hc % 2, :],
                        op0=ALU.mult, op1=ALU.mult)
                    b.inc(i, "ev")
            ev_ha = b.v("ev")

            if fin_pend is not None:
                _fb_ev = fin_pend[2](_fb_mm)

            def hid(wdram, scales, src, dst, ev_src_ready, q, lsem):
                mm_hid_base = b.v("mm")
                gl_h = {}
                for oc in range(32):
                    @getattr(b.block, q)
                    def _(eng, _oc=oc):
                        key = _oc % 3
                        b.w(eng, "mm", mm_wh_slab[key])
                        i = eng.dma_start(out=whb[:, key], in_=wdram[_oc])
                        b.inc(i, lsem, 16)
                        mm_wh_slab[key] = mm_hid_base + _oc + 1
                    gl_h[oc] = b.v(lsem)

                def hgrp(tensor, oc, _esr=ev_src_ready):
                    if oc == 0:
                        tensor.wait_ge(b.sem("ev"), _esr)
                    tensor.wait_ge(b.sem(lsem), gl_h[oc])
                    for hc in range(32):
                        i = tensor.matmul((mc if oc % 2 == 0 else md)[:],
                                          whb[:, oc % 3, hc, :], src[:, hc, :],
                                          start=(hc == 0), stop=(hc == 31))
                    return i

                def hevac(eng, oc):
                    return eng.activation(dst[:, oc, :],
                                          (mc if oc % 2 == 0 else md)[:], AF.Gelu,
                                          scale=scales[:, oc:oc + 1])

                return pipe(b, 32, 2, hgrp, hevac, "scalar")

            _, ac_h1 = hid(W["MH1"], W["vmsh1"], ham, hbm, ev_ha, "sync", "gl")

            if fin_pend is not None:
                fin_pend[3](_fb_ev)
                fin_pend = None

            @b.block.vector
            def _(vector, _a=ac_h1):
                vector.wait_ge(b.sem("ac"), _a)
                i = vector.engine_nop()
                b.inc(i, "ev")
            ev_h1 = b.v("ev")
            _, ac_h2 = hid(W["MH2"], W["vmsh2"], hbm, ham, ev_h1, "gpsimd", "g2")

            @b.block.vector
            def _(vector, _a=ac_h2):
                vector.wait_ge(b.sem("ac"), _a)
                i = vector.engine_nop()
                b.inc(i, "ev")
            ev_h2 = b.v("ev")

            mm_m3_base = b.v("mm")
            gl_3 = {}
            for oc in range(8):
                @b.block.sync
                def _(sync, _oc=oc):
                    b.w(sync, "mm",
                        st_["mm_w3_used"] if _oc < 2 else mm_m3_base + _oc - 1)
                    i = sync.dma_start(out=w3b[:, _oc % 2], in_=W["M3"][_oc])
                    b.inc(i, "gl", 16)
                gl_3[oc] = b.v("gl")

            def m3grp(tensor, oc, _e=ev_h2, _ef=st_["ev_fin_prev"]):
                if oc == 0:
                    tensor.wait_ge(b.sem("ev"), _e)
                    b.w(tensor, "ev", _ef)
                tensor.wait_ge(b.sem("gl"), gl_3[oc])
                for hc in range(32):
                    i = tensor.matmul((me if oc % 2 == 0 else mf)[:],
                                      w3b[:, oc % 2, hc, :], ham[:, hc, :],
                                      start=(hc == 0), stop=(hc == 31))
                return i

            def m3evac(eng, oc):
                return eng.activation(mot[:, oc, :],
                                      (me if oc % 2 == 0 else mf)[:], AF.Identity,
                                      scale=W["vms3"][:, oc:oc + 1])

            mm_m3, ac_mo = pipe(b, H, 2, m3grp, m3evac, "scalar")
            st_["mm_w3_used"] = mm_m3

            def fin_a(_ac_mo=ac_mo, _ea=st_["ev_apply_prev"]):
                @b.block.vector
                def _(vector):
                    vector.wait_ge(b.sem("ac"), _ac_mo)
                    b.w(vector, "ev", _ea)
                    i = vector.tensor_mul(sqf5[:], mot[:], mot[:])
                    b.inc(i, "ev")
                return b.v("ev")

            def fin_b1(ev_sq, _ac_mo=ac_mo, _pf=st_["ev_pfree_prev"]):
                @b.block.tensor
                def _(tensor):
                    tensor.wait_ge(b.sem("ac"), _ac_mo)
                    b.w(tensor, "ev", _pf)
                    for c in range(H):
                        tensor.matmul(me[0:1, :], ones_col[:], mot[:, c, :],
                                      start=(c == 0), stop=(c == H - 1))
                    tensor.wait_ge(b.sem("ev"), ev_sq)
                    for c in range(H):
                        i = tensor.matmul(mf[0:1, :], b.ones_colb[:],
                                          sqf5[:, c, :],
                                          start=(c == 0), stop=(c == H - 1))
                    b.inc(i, "mm")
                return b.v("mm")

            def fin_b2(mm_stats):
                ev_pfree = ln_smalls(b, sm, me, mf, float(D), mm_stats)
                st_["ev_pfree_prev"] = ev_pfree
                return b.v("ev")

            def fin_c(ev_mr, _g=g, _t0=t0, _xl1=st_["xl_x1"]):
                @b.block.tensor
                def _(tensor):
                    tensor.wait_ge(b.sem("ev"), ev_mr)
                    tensor.matmul(me[:], ones_row[:], sm["rstd"][0:1, :],
                                  start=True, stop=True)
                    i = tensor.matmul(mf[:], ones_row[:], sm["mr"][0:1, :],
                                      start=True, stop=True)
                    b.inc(i, "mm")
                mm_bco = b.v("mm")

                @b.block.vector
                def _(vector, _mb=mm_bco, _st=st_["so_prev"]):
                    vector.wait_ge(b.sem("mm"), _mb)
                    vector.wait_ge(b.sem("x2"), _xl1)
                    b.w(vector, "so", _st)
                    for c in range(H):
                        vector.tensor_mul(ap5[:, 0, :], mot[:, c, :], me[:])
                        vector.tensor_sub(ap5[:, 1, :], ap5[:, 0, :], mf[:])
                        i = vector.tensor_add(mot[:, c, :], ap5[:, 1, :],
                                              x1g[:, c, :])
                        b.inc(i, "ev")
                st_["ev_apply_prev"] = b.v("ev")
                st_["ev_fin_prev"] = b.v("ev")
                ev_o = b.v("ev")
                if _g + 1 < NG:
                    @b.block.sync
                    def _(sync, _e=ev_o, _t02=_t0):
                        sync.wait_ge(b.sem("ev"), _e)
                        i = sync.dma_start(
                            out=x1g[:],
                            in_=resid[:, :, _t02 + TG:_t02 + 2 * TG]
                            .rearrange("c p t -> p c t"))
                        b.inc(i, "x2", 16)
                    st_["xl_x1"] = b.v("x2")

                @b.block.scalar
                def _(scalar, _e=ev_o):
                    scalar.wait_ge(b.sem("ev"), _e)
                    i = scalar.dma_start(out=xout[:, :, _t0:_t0 + TG]
                                         .rearrange("c p t -> p c t"), in_=mot[:])
                    b.inc(i, "so", 16)
                st_["so_prev"] = b.v("so")

            fin_pend = (fin_a, fin_b1, fin_b2, fin_c)

        _fa_ev = fin_pend[0]()
        _fb_mm = fin_pend[1](_fa_ev)
        _fb_ev = fin_pend[2](_fb_mm)
        fin_pend[3](_fb_ev)

        @b.block.scalar
        def _(scalar):
            scalar.wait_ge(b.sem("so"), b.v("so"))
        b.barrier()



_CACHED = {}


def get_nc(tap=None):
    if tap not in _CACHED:
        _CACHED[tap] = build_program(tap)
    return _CACHED[tap]


def kernel(**inputs):
    nc = get_nc()
    in_maps = [prep_core_inputs(inputs, k) for k in range(NCORES)]
    res = run_bass_kernel_spmd(nc, in_maps, core_ids=list(range(NCORES)))
    outs = []
    for k in range(NCORES):
        o = np.asarray(res.results[k]["xout"])
        o = o.reshape(H, 128, BLOC, S).transpose(2, 3, 0, 1).reshape(BLOC, S, D)
        outs.append(o)
    return np.concatenate(outs, axis=0).astype(np.float32)
